# version 01
# baseline (speedup 1.0000x reference)
import numpy as np
import ml_dtypes
import concourse.bass as bass
import concourse.mybir as mybir
from concourse.bass_utils import run_bass_kernel_spmd

F32 = mybir.dt.float32
BF16 = mybir.dt.bfloat16
U32 = mybir.dt.uint32
AF = mybir.ActivationFunctionType
ALU = mybir.AluOpType
AX = mybir.AxisListType

D = 1024
S = 4096
B = 4
TOWN = 2048
NEG = -1.0e30
bf16 = ml_dtypes.bfloat16


def f32c(x):
    return np.ascontiguousarray(x, dtype=np.float32)


def bfc(x):
    return np.ascontiguousarray(np.asarray(x, dtype=np.float32).astype(bf16))


class Ser:
    PASS = {"wait_ge", "nop", "drain", "sem_inc", "dma_start",
            "dma_start_transpose", "engine_nop", "indirect_dma_start"}

    def __init__(self, cx, eng, sem):
        self._cx, self._eng, self._sem = cx, eng, sem

    def __getattr__(self, name):
        f = getattr(self._eng, name)
        if name in Ser.PASS or not callable(f):
            return f
        cx, eng, sem = self._cx, self._eng, self._sem

        def wrapped(*a, **k):
            if cx.val(sem) > 0:
                eng.wait_ge(sem, cx.val(sem))
            ins = f(*a, **k)
            cx.inc(ins, sem, 1)
            return ins
        return wrapped


class Ctx:
    def __init__(self, nc, n_psum=6, psum_shape=(128, 512)):
        self.nc = nc
        self.cnt = {}
        self._stack = []
        self._open = {}
        self._order = []
        self.ld = self.sem("c_ld")
        self.lg = self.sem("c_lg")
        self.st = self.sem("c_st")
        self.mm = self.sem("c_mm")
        self.ev = self.sem("c_ev")
        self.vv = self.sem("c_vv")
        self.gg = self.sem("c_gg")
        self.all_sems = [self.ld, self.lg, self.st, self.mm, self.ev, self.vv, self.gg]
        self.psum = [self.psum_t(f"cps{i}", list(psum_shape), F32)
                     for i in range(n_psum)]
        self.free_at = [(self.ev, 0) for _ in range(n_psum)]
        self.rr = 0
        self.psb = [self.psum_t(f"cpb{i}", [128, 1024], BF16)
                    for i in range(2)]
        self.free_at_b = [(self.ev, 0) for _ in range(2)]
        self.rrb = 0

    def sem(self, name):
        h = self.nc.semaphore(name)
        s = h.__enter__()
        self._stack.append(h)
        return s

    def sbuf(self, name, shape, dtype):
        self._uid = getattr(self, "_uid", 0) + 1
        h = self.nc.sbuf_tensor(f"sb{self._uid}_{name}", list(shape), dtype)
        t = h.__enter__()
        self._open[name] = h
        self._order.append(name)
        return t

    def free_sbuf(self, *names):
        if getattr(self, "block", None) is not None:
            self.fence(self.block)
        for name in sorted(names, key=lambda n: -self._order.index(n)):
            self._open.pop(name).__exit__(None, None, None)
            self._order.remove(name)

    def psum_t(self, name, shape, dtype):
        h = self.nc.psum_tensor(name, shape, dtype)
        t = h.__enter__()
        self._stack.append(h)
        return t

    def close(self):
        for name in reversed(self._order):
            self._open.pop(name).__exit__(None, None, None)
        self._order.clear()
        self._open.clear()
        for h in reversed(self._stack):
            h.__exit__(None, None, None)
        self._stack.clear()

    def inc(self, instr, sem, k=1):
        instr.then_inc(sem, k)
        self.cnt[sem] = self.cnt.get(sem, 0) + k
        return instr

    def val(self, sem):
        return self.cnt.get(sem, 0)

    def wait(self, eng, *sems):
        for sem in sems:
            if self.val(sem) > 0:
                eng.wait_ge(sem, self.val(sem))

    def wait_all(self, eng):
        self.wait(eng, *self.all_sems)

    def fence(self, block):
        for en in ("tensor", "vector", "scalar", "gpsimd", "sync"):
            @getattr(block, en)
            def _(e):
                self.wait_all(e)

    def ser_v(self, v):
        return Ser(self, v, self.vv)

    def ser_s(self, s):
        return Ser(self, s, self.ev)

    def ser_g(self, g):
        return Ser(self, g, self.gg)

    def bank(self, te):
        i = self.rr % len(self.psum)
        self.rr += 1
        sem, cnt = self.free_at[i]
        if cnt > 0:
            te.wait_ge(sem, cnt)
        return i, self.psum[i]

    def bank_freed(self, i, sem):
        self.free_at[i] = (sem, self.val(sem))

    def bankb(self, te):
        i = self.rrb % 2
        self.rrb += 1
        sem, cnt = self.free_at_b[i]
        if cnt > 0:
            te.wait_ge(sem, cnt)
        return i, self.psb[i]

    def bankb_freed(self, i, sem):
        self.free_at_b[i] = (sem, self.val(sem))



def wload(g, cx, dst, src, nci):
    O = src.shape[1]
    ins = g.dma_start(dst.rearrange("p (c o) -> p c o", c=nci),
                      src.rearrange("(c p) o -> p c o", p=128))
    cx.inc(ins, cx.lg, 16)
    g.wait_ge(cx.lg, cx.val(cx.lg))
    return ins


def emit_ln(cx, block, x_dram, xn, T, ones, tap=None, extra=()):
    xs = cx.sbuf("ln_xs", (128, 8 * 512), F32)
    xe = cx.sbuf("ln_xe", (128, 8 * 512), F32) if extra else None
    xnf = cx.sbuf("ln_xnf", (128, 8 * 512), F32) if tap is not None else None
    xb = cx.sbuf("ln_xb", (128, 8 * 512), BF16)
    sq = cx.sbuf("ln_sq", (128, 8 * 512), BF16)
    mb = cx.sbuf("ln_mb", (128, 512), F32)
    vb = cx.sbuf("ln_vb", (128, 512), F32)
    t2 = cx.sbuf("ln_t2", (128, 512), F32)

    for c in range(T // 512):
        c0 = c * 512

        @block.sync
        def _(sy):
            cx.wait(sy, cx.vv, cx.ev)
            cx.inc(sy.dma_start(xs[:].rearrange("p (i t) -> p i t", i=8),
                                x_dram[:, c0:c0 + 512]
                                .rearrange("(i p) t -> p i t", p=128)),
                   cx.ld, 16)
            sy.wait_ge(cx.ld, cx.val(cx.ld))

        if extra:
            for a in extra:
                @block.sync
                def _(sy):
                    cx.wait(sy, cx.vv)
                    cx.inc(sy.dma_start(xe[:].rearrange("p (i t) -> p i t", i=8),
                                        a[:, c0:c0 + 512]
                                        .rearrange("(i p) t -> p i t", p=128)),
                           cx.ld, 16)
                    sy.wait_ge(cx.ld, cx.val(cx.ld))

                @block.vector
                def _(v):
                    v = cx.ser_v(v)
                    cx.wait(v, cx.ld)
                    for i in range(8):
                        v.tensor_tensor(xs[:, i * 512:(i + 1) * 512],
                                        xs[:, i * 512:(i + 1) * 512],
                                        xe[:, i * 512:(i + 1) * 512],
                                        op=ALU.add)

        @block.scalar
        def _(s):
            s = cx.ser_s(s)
            cx.wait_all(s)
            for i in range(8):
                s.activation(xb[:, i * 512:(i + 1) * 512],
                             xs[:, i * 512:(i + 1) * 512], AF.Copy)
                ins = s.activation(sq[:, i * 512:(i + 1) * 512],
                                   xs[:, i * 512:(i + 1) * 512], AF.Square)
            del ins

        @block.tensor
        def _(t):
            cx.wait(t, cx.ev)
            bi, ps = cx.bank(t)
            for i in range(8):
                t.matmul(ps[:], ones[:], xb[:, i * 512:(i + 1) * 512],
                         start=(i == 0), stop=(i == 7))
            bj, ps2 = cx.bank(t)
            for i in range(8):
                ins = t.matmul(ps2[:], ones[:], sq[:, i * 512:(i + 1) * 512],
                               start=(i == 0), stop=(i == 7))
            cx.inc(ins, cx.mm)

            @block.vector
            def _(v):
                v = cx.ser_v(v)
                cx.wait(v, cx.mm)
                v.tensor_copy(mb[:], ps[:])
                v.tensor_copy(vb[:], ps2[:])
                v.tensor_tensor(t2[:], mb[:], mb[:], op=ALU.mult)
                v.tensor_tensor(vb[:], vb[:], t2[:], op=ALU.subtract)
                ins = v.tensor_scalar(vb[:], vb[:], 1e-5, None, op0=ALU.add)
                del ins
            cx.bank_freed(bi, cx.vv)
            cx.bank_freed(bj, cx.vv)

        @block.scalar
        def _(s):
            s = cx.ser_s(s)
            cx.wait(s, cx.vv)
            ins = s.activation(t2[:], vb[:], AF.Sqrt)
            del ins

        @block.vector
        def _(v):
            v = cx.ser_v(v)
            cx.wait(v, cx.ev)
            v.reciprocal(vb[:], t2[:])
            for i in range(8):
                v.tensor_tensor(t2[:], xs[:, i * 512:(i + 1) * 512], mb[:],
                                op=ALU.subtract)
                if tap is None:
                    ins = v.tensor_tensor(xn[:, i * T + c0:i * T + c0 + 512],
                                          t2[:], vb[:], op=ALU.mult)
                else:
                    ins = v.tensor_tensor(xnf[:, i * 512:(i + 1) * 512],
                                          t2[:], vb[:], op=ALU.mult)
            del ins

        if tap is not None:
            @block.scalar
            def _(s):
                s = cx.ser_s(s)
                cx.wait(s, cx.vv)
                for i in range(8):
                    ins = s.activation(xn[:, i * T + c0:i * T + c0 + 512],
                                       xnf[:, i * 512:(i + 1) * 512], AF.Copy)
                del ins
            tap(c0, xnf)

    cx.free_sbuf(*(["ln_xnf"] if tap is not None else []),
                 *(["ln_xe"] if extra else []),
                 "ln_xs", "ln_xb", "ln_sq", "ln_mb", "ln_vb", "ln_t2")


def emit_causal_mha(cx, block, inp, xn_t, oTh, nheads, consts):
    idb, cosb, sinb, cmask = (consts["idb"], consts["cos"], consts["sin"],
                              consts["cmask"])
    H = nheads
    wqkv = cx.sbuf("wqkv", (128, 3 * 512), BF16)
    bqkv = cx.sbuf("bqkv", (64, 3 * H), F32)
    qTa = cx.sbuf("qTa", (32, S), BF16)
    qTb = cx.sbuf("qTb", (32, S), BF16)
    kTa = cx.sbuf("kTa", (32, S), BF16)
    kTb = cx.sbuf("kTb", (32, S), BF16)
    pra = cx.sbuf("pra", (32, 2 * 512), F32)
    prb = cx.sbuf("prb", (32, 2 * 512), F32)
    rta = cx.sbuf("rta", (32, 512), F32)
    rtb = cx.sbuf("rtb", (32, 512), F32)
    vTM = cx.sbuf("vTM", (128, 32 * 64), BF16)
    vTs = cx.sbuf("vTs", (64, 2 * 512), BF16)
    strip = cx.sbuf("strip", (128, S), F32)
    pstrip = cx.sbuf("pstrip", (128, S), BF16)
    ptbig = cx.sbuf("ptbig", (128, 16 * 128), BF16)
    rsum = cx.sbuf("rsum", (128, 1), F32)
    nmax = cx.sbuf("nmax", (128, 1), F32)

    @block.sync
    def _(sy):
        for j, nm in enumerate(("bq", "bk", "bv")):
            for h in range(H):
                cx.inc(sy.dma_start(bqkv[:, j * H + h:j * H + h + 1],
                                    inp[nm][h * 64:(h + 1) * 64, :]),
                       cx.ld, 16)
                sy.wait_ge(cx.ld, cx.val(cx.ld))

    for h in range(H):
        @block.gpsimd
        def _(g):
            g = cx.ser_g(g)
            cx.wait(g, cx.mm)
            for j, wname in enumerate(("wq", "wk", "wv")):
                wsl = inp[wname][:, h * 64:(h + 1) * 64]
                wload(g, cx, wqkv[:, j * 512:(j + 1) * 512], wsl, 8)

        for j, name in enumerate(("q", "k", "v")):
            for c in range(8):
                c0 = c * 512
                slot = c % 2

                @block.tensor
                def _(t):
                    cx.wait(t, cx.ld, cx.lg)
                    if c == 0:
                        cx.wait(t, cx.vv, cx.ev)
                    bi, ps = cx.bank(t)
                    for ci in range(8):
                        ins = t.matmul(
                            ps[0:64, :],
                            wqkv[:, j * 512 + ci * 64:j * 512 + (ci + 1) * 64],
                            xn_t(ci, c0, 512),
                            start=(ci == 0), stop=(ci == 7))
                    cx.inc(ins, cx.mm)

                    if name == "v":
                        @block.scalar
                        def _(s):
                            s = cx.ser_s(s)
                            cx.wait(s, cx.mm, cx.vv)
                            ins = s.activation(
                                vTs[:, slot * 512:(slot + 1) * 512],
                                ps[0:64, :], AF.Identity,
                                bias=bqkv[:, j * H + h:j * H + h + 1])
                            del ins
                        cx.bank_freed(bi, cx.ev)

                        @block.tensor
                        def _(t2):
                            cx.wait(t2, cx.ev)
                            for u in range(4):
                                kt = c * 4 + u
                                bb, psb = cx.bankb(t2)
                                ins = t2.transpose(
                                    psb[:, 0:64],
                                    vTs[:, slot * 512 + u * 128:
                                        slot * 512 + (u + 1) * 128],
                                    idb[0:64, 0:64])
                                cx.inc(ins, cx.mm)

                                @block.vector
                                def _(v):
                                    v = cx.ser_v(v)
                                    cx.wait(v, cx.mm)
                                    ins = v.tensor_copy(
                                        vTM[:, kt * 64:(kt + 1) * 64],
                                        psb[:, 0:64])
                                    del ins
                                cx.bankb_freed(bb, cx.vv)
                    else:
                        @block.scalar
                        def _(s):
                            s = cx.ser_s(s)
                            cx.wait(s, cx.mm, cx.vv)
                            s.activation(
                                pra[:, slot * 512:(slot + 1) * 512],
                                ps[0:32, :], AF.Identity,
                                bias=bqkv[0:32, j * H + h:j * H + h + 1])
                            ins = s.activation(
                                prb[:, slot * 512:(slot + 1) * 512],
                                ps[32:64, :], AF.Identity,
                                bias=bqkv[32:64, j * H + h:j * H + h + 1])
                            del ins
                        cx.bank_freed(bi, cx.ev)

                        @block.vector
                        def _(v):
                            v = cx.ser_v(v)
                            cx.wait(v, cx.ev)
                            da = qTa if name == "q" else kTa
                            db = qTb if name == "q" else kTb
                            top = pra[:, slot * 512:(slot + 1) * 512]
                            bot = prb[:, slot * 512:(slot + 1) * 512]
                            co = cosb[:, c0:c0 + 512]
                            si = sinb[:, c0:c0 + 512]
                            v.tensor_tensor(rta[:], top, co, op=ALU.mult)
                            v.tensor_tensor(rtb[:], bot, si, op=ALU.mult)
                            v.tensor_tensor(da[:, c0:c0 + 512], rta[:], rtb[:],
                                            op=ALU.subtract)
                            v.tensor_tensor(rta[:], top, si, op=ALU.mult)
                            v.tensor_tensor(rtb[:], bot, co, op=ALU.mult)
                            ins = v.tensor_tensor(db[:, c0:c0 + 512], rta[:],
                                                  rtb[:], op=ALU.add)
                            del ins

        for qi in range(32):
            kw = (qi + 1) * 128
            nch = (kw + 511) // 512

            @block.tensor
            def _(t):
                cx.wait(t, cx.vv, cx.ev)
                for c in range(nch):
                    c0 = c * 512
                    w = min(512, kw - c0)
                    bi, ps = cx.bank(t)
                    t.matmul(ps[:, 0:w], qTa[:, qi * 128:(qi + 1) * 128],
                             kTa[:, c0:c0 + w], start=True, stop=False)
                    ins = t.matmul(ps[:, 0:w], qTb[:, qi * 128:(qi + 1) * 128],
                                   kTb[:, c0:c0 + w], start=False, stop=True)
                    cx.inc(ins, cx.mm)
                    last = (c == nch - 1)

                    @block.vector
                    def _(v):
                        v = cx.ser_v(v)
                        cx.wait(v, cx.mm, cx.ev)
                        if last:
                            r = qi % 4
                            ins = v.tensor_tensor(
                                strip[:, c0:c0 + w], ps[:, 0:w],
                                cmask[:, r * 512:r * 512 + w], op=ALU.add)
                        else:
                            ins = v.tensor_copy(strip[:, c0:c0 + w], ps[:, 0:w])
                        del ins
                    cx.bank_freed(bi, cx.vv)

            @block.vector
            def _(v):
                v = cx.ser_v(v)
                cx.wait(v, cx.vv, cx.ev)
                v.reduce_max(nmax[:], strip[:, 0:kw], axis=AX.X, negate=True)
                ins = v.tensor_scalar(nmax[:], nmax[:], 0.125, None,
                                      op0=ALU.mult)
                del ins

            @block.scalar
            def _(s):
                s = cx.ser_s(s)
                cx.wait(s, cx.vv, cx.mm)
                ins = s.activation(pstrip[:, 0:kw], strip[:, 0:kw], AF.Exp,
                                   bias=nmax[:, 0:1], scale=0.125,
                                   accum_out=rsum[:])
                del ins

            @block.vector
            def _(v):
                v = cx.ser_v(v)
                cx.wait(v, cx.ev)
                v.reciprocal(nmax[:], rsum[:])
                ins = v.tensor_scalar(pstrip[:, 0:kw], pstrip[:, 0:kw],
                                      nmax[:, 0:1], None, op0=ALU.mult)
                del ins

            nkt = qi + 1

            @block.tensor
            def _(t):
                cx.wait(t, cx.vv)
                bo, pso = cx.bank(t)
                for g0 in range(0, nkt, 16):
                    gn = min(16, nkt - g0)
                    for u0 in range(0, gn, 8):
                        un = min(8, gn - u0)
                        bb, psb = cx.bankb(t)
                        for u in range(un):
                            kt = g0 + u0 + u
                            ins = t.transpose(
                                psb[:, u * 128:(u + 1) * 128],
                                pstrip[:, kt * 128:(kt + 1) * 128], idb[:])
                        cx.inc(ins, cx.mm)

                        @block.vector
                        def _(v):
                            v = cx.ser_v(v)
                            cx.wait(v, cx.mm)
                            ins = v.tensor_copy(
                                ptbig[:, (u0) * 128:(u0 + un) * 128],
                                psb[:, 0:un * 128])
                            del ins
                        cx.bankb_freed(bb, cx.vv)
                    cx.wait(t, cx.vv)
                    for u in range(gn):
                        kt = g0 + u
                        ins = t.matmul(pso[0:64, 0:128],
                                       vTM[:, kt * 64:(kt + 1) * 64],
                                       ptbig[:, u * 128:(u + 1) * 128],
                                       start=(kt == 0), stop=(kt == nkt - 1))
                    cx.inc(ins, cx.mm)

                @block.scalar
                def _(s):
                    s = cx.ser_s(s)
                    cx.wait(s, cx.mm)
                    hp = h % 2
                    ins = s.activation(
                        oTh[hp * 64:hp * 64 + 64,
                            (h // 2) * S + qi * 128:(h // 2) * S + (qi + 1) * 128],
                        pso[0:64, 0:128], AF.Copy)
                    del ins
                cx.bank_freed(bo, cx.ev)

    cx.free_sbuf("wqkv", "bqkv", "qTa", "qTb", "kTa", "kTb", "pra", "prb",
                 "rta", "rtb", "vTM", "vTs", "strip", "pstrip", "ptbig",
                 "rsum", "nmax")


def emit_headsum_proj(cx, block, wo_dram, oTh, out_dram, nheads, bias_sb=None):
    H = nheads
    nci = H * 64 // 128
    wo_sb = cx.sbuf("wo_sb", (128, nci * D), BF16)
    stg = cx.sbuf("stg", (128, 2 * 512), F32)

    @block.gpsimd
    def _(g):
        g = cx.ser_g(g)
        wload(g, cx, wo_sb[:], wo_dram, wo_dram.shape[0] // 128)

    for po in range(8):
        for c in range(8):
            c0 = c * 512

            @block.tensor
            def _(t):
                cx.wait(t, cx.ld, cx.lg, cx.ev)
                bi, ps = cx.bank(t)
                for ci in range(nci):
                    lhs = wo_sb[:, ci * D + po * 128:ci * D + (po + 1) * 128]
                    ins = t.matmul(ps[:], lhs,
                                   oTh[:, ci * S + c0:ci * S + c0 + 512],
                                   start=(ci == 0), stop=(ci == nci - 1))
                cx.inc(ins, cx.mm)
                sslot = (po * 8 + c) % 2

                @block.scalar
                def _(s):
                    s = cx.ser_s(s)
                    cx.wait(s, cx.mm, cx.st)
                    kw = dict(bias=bias_sb[:, po:po + 1]) \
                        if bias_sb is not None else {}
                    ins = s.activation(stg[:, sslot * 512:(sslot + 1) * 512],
                                       ps[:], AF.Identity if kw else AF.Copy,
                                       **kw)
                    del ins
                cx.bank_freed(bi, cx.ev)

                @block.sync
                def _(sy):
                    cx.wait(sy, cx.ev)
                    ins = sy.dma_start(
                        out_dram[po * 128:(po + 1) * 128, c0:c0 + 512],
                        stg[:, sslot * 512:(sslot + 1) * 512])
                    cx.inc(ins, cx.st, 16)
                    sy.wait_ge(cx.st, cx.val(cx.st))

    cx.free_sbuf("wo_sb", "stg")



def build_L0():
    nc = bass.Bass("TRN2", target_bir_lowering=False)
    H = 8
    NU = 80
    HID = NU * 256

    f32in = [
        ("xful", [D, S]), ("xown", [D, TOWN]),
        ("wq", [D, H * 64]), ("wk", [D, H * 64]), ("wv", [D, H * 64]),
        ("bq", [H * 64, 1]), ("bk", [H * 64, 1]), ("bv", [H * 64, 1]),
        ("wo", [H * 64, D]),
        ("cos", [32, S]), ("sin", [32, S]),
        ("cmask", [128, 4 * 512]),
        ("rw", [D, 64]), ("rb", [64, 1]),
        ("w1big", [D, HID]), ("b1big", [HID, 1]),
        ("w2big", [HID, D]),
        ("biasmat", [128, D]),
        ("idf", [128, 128]),
    ]
    bf16in = [("idb", [128, 128]), ("onesln", [128, 128])]
    inp = {}
    for name, shape in f32in:
        inp[name] = nc.dram_tensor(name, shape, F32, kind="ExternalInput").ap()
    for name, shape in bf16in:
        inp[name] = nc.dram_tensor(name, shape, BF16, kind="ExternalInput").ap()
    attn_out = nc.dram_tensor("attn_out", [D, S], F32,
                              kind="ExternalOutput").ap()
    xffn_out = nc.dram_tensor("xffn_out", [D, TOWN], F32,
                              kind="ExternalOutput").ap()

    cx = Ctx(nc)
    with nc.Block() as block:
        cx.block = block
        onesln = cx.sbuf("onesln", (128, 128), BF16)
        idf = cx.sbuf("idf", (128, 128), F32)
        idb = cx.sbuf("idb", (128, 128), BF16)
        cosb = cx.sbuf("cosb", (32, S), BF16)
        sinb = cx.sbuf("sinb", (32, S), BF16)
        cmask = cx.sbuf("cmask", (128, 4 * 512), F32)

        @block.sync
        def _(sy):
            for dst, src in [(onesln, inp["onesln"]), (idf, inp["idf"]),
                             (idb, inp["idb"]), (cmask, inp["cmask"])]:
                cx.inc(sy.dma_start(dst[:], src), cx.ld, 16)
                sy.wait_ge(cx.ld, cx.val(cx.ld))

        @block.gpsimd
        def _(g):
            g = cx.ser_g(g)
            for dst, src in [(cosb, inp["cos"]), (sinb, inp["sin"])]:
                cx.inc(g.dma_start(dst[:], src), cx.lg, 16)
                g.wait_ge(cx.lg, cx.val(cx.lg))

        consts = {"ones": onesln, "idb": idb, "cos": cosb, "sin": sinb,
                  "cmask": cmask}

        oTh = cx.sbuf("oTh", (128, 4 * S), BF16)
        xn = cx.sbuf("xn", (128, 8 * S), BF16)
        emit_ln(cx, block, inp["xful"], xn, S, onesln)

        def xn_t(i, c0, w):
            return xn[:, i * S + c0:i * S + c0 + w]

        cx.fence(block)
        emit_causal_mha(cx, block, inp, xn_t, oTh, H, consts)
        cx.fence(block)
        cx.free_sbuf("xn")
        emit_headsum_proj(cx, block, inp["wo"], oTh, attn_out, H)
        cx.free_sbuf("oTh")
        cx.fence(block)

        xno = cx.sbuf("xno", (128, 8 * TOWN), BF16)
        gte = cx.sbuf("gte", (128, TOWN), BF16)
        rw_sb = cx.sbuf("rw_sb", (128, 8 * 64), F32)
        rb_sb = cx.sbuf("rb_sb", (64, 1), F32)
        ltT = cx.sbuf("ltT", (64, TOWN), F32)

        @block.sync
        def _(sy):
            cx.inc(sy.dma_start(rw_sb[:].rearrange("p (c o) -> p c o", c=8),
                                inp["rw"].rearrange("(c p) o -> p c o", p=128)),
                   cx.ld, 16)
            sy.wait_ge(cx.ld, cx.val(cx.ld))
            cx.inc(sy.dma_start(rb_sb[:], inp["rb"]), cx.ld, 16)
            sy.wait_ge(cx.ld, cx.val(cx.ld))

        def router_tap(c0, xnf):
            @block.tensor
            def _(t):
                cx.wait(t, cx.ld, cx.vv)
                bi, ps = cx.bank(t)
                for ci in range(8):
                    ins = t.matmul(ps[0:64, :], rw_sb[:, ci * 64:(ci + 1) * 64],
                                   xnf[:, ci * 512:(ci + 1) * 512],
                                   start=(ci == 0), stop=(ci == 7))
                cx.inc(ins, cx.mm)

                @block.scalar
                def _(s):
                    s = cx.ser_s(s)
                    cx.wait(s, cx.mm)
                    ins = s.activation(ltT[:, c0:c0 + 512], ps[0:64, :],
                                       AF.Identity, bias=rb_sb[:, 0:1])
                    del ins
                cx.bank_freed(bi, cx.ev)

        emit_ln(cx, block, inp["xown"], xno, TOWN, onesln, tap=router_tap)

        def xno_t(i, c0, w):
            return xno[:, i * TOWN + c0:i * TOWN + c0 + w]

        cx.fence(block)


        ltok = cx.sbuf("ltok", (128, 16 * 64), F32)
        gtok = cx.sbuf("gtok", (128, 16 * 64), BF16)
        cur = cx.sbuf("cur", (128, 64), F32)
        sel = cx.sbuf("sel", (128, 64), F32)
        msk = cx.sbuf("msk", (128, 64), F32)
        ee = cx.sbuf("ee", (128, 64), F32)
        mx = cx.sbuf("mx", (128, 1), F32)

        @block.tensor
        def _(t):
            cx.wait(t, cx.ev)
            for kt in range(16):
                bi, ps = cx.bank(t)
                ins = t.transpose(ps[:, 0:64], ltT[:, kt * 128:(kt + 1) * 128],
                                  idf[0:64, 0:64])
                cx.inc(ins, cx.mm)

                @block.vector
                def _(v):
                    v = cx.ser_v(v)
                    cx.wait(v, cx.mm)
                    ins = v.tensor_copy(ltok[:, kt * 64:(kt + 1) * 64],
                                        ps[:, 0:64])
                    del ins
                cx.bank_freed(bi, cx.vv)

        for kt in range(16):
            lt = ltok[:, kt * 64:(kt + 1) * 64]

            @block.vector
            def _(v):
                v = cx.ser_v(v)
                cx.wait(v, cx.vv, cx.ev)
                v.tensor_copy(cur[:], lt)
                ins = v.reduce_max(mx[:], lt, axis=AX.X, negate=True)
                del ins

            @block.scalar
            def _(s):
                s = cx.ser_s(s)
                cx.wait(s, cx.vv)
                ins = s.activation(ee[:], lt, AF.Exp, bias=mx[:, 0:1])
                del ins

            @block.vector
            def _(v):
                v = cx.ser_v(v)
                cx.wait(v, cx.ev)
                v.memset(sel[:], 0.0)
                for k in range(8):
                    v.reduce_max(mx[:], cur[:], axis=AX.X)
                    v.tensor_scalar(msk[:], cur[:], mx[:, 0:1], None,
                                    op0=ALU.is_equal)
                    v.tensor_tensor(sel[:], sel[:], msk[:], op=ALU.add)
                    v.scalar_tensor_tensor(cur[:], msk[:], NEG, cur[:],
                                           op0=ALU.mult, op1=ALU.add)
                v.tensor_tensor(msk[:], sel[:], ee[:], op=ALU.mult)
                v.reduce_sum(mx[:], msk[:], axis=AX.X)
                v.reciprocal(mx[:], mx[:])
                ins = v.tensor_scalar(gtok[:, kt * 64:(kt + 1) * 64], msk[:],
                                      mx[:, 0:1], None, op0=ALU.mult)
                del ins

        @block.vector
        def _(v):
            v = cx.ser_v(v)
            cx.wait(v, cx.vv)
            v.memset(gte[64:128, :], 0.0)
            ins = v.memset(gte[64:67, :], 1.0)
            del ins

        @block.tensor
        def _(t):
            cx.wait(t, cx.vv)
            for kt in range(16):
                bi, psb = cx.bankb(t)
                ins = t.transpose(psb[0:64, 0:128],
                                  gtok[:, kt * 64:(kt + 1) * 64], idb[:])
                cx.inc(ins, cx.mm)

                @block.scalar
                def _(s):
                    s = cx.ser_s(s)
                    cx.wait(s, cx.mm)
                    ins = s.activation(gte[0:64, kt * 128:(kt + 1) * 128],
                                       psb[0:64, 0:128], AF.Copy)
                    del ins
                cx.bankb_freed(bi, cx.ev)

        cx.free_sbuf("ltT", "ltok", "gtok", "cur", "sel", "msk", "ee", "mx")
        cx.fence(block)

        bm_sb = cx.sbuf("bm_sb", (128, D), BF16)
        w1sb = cx.sbuf("w1sb", (128, 2 * 2048), BF16)
        b1sb = cx.sbuf("b1sb", (128, 4), F32)
        grow = cx.sbuf("grow", (1, TOWN), BF16)
        gbc = cx.sbuf("gbc", (128, 1024), BF16)
        mid = cx.sbuf("mid", (128, 2 * 1024), BF16)
        mblk = cx.sbuf("mblk", (128, 20 * 1024), BF16)
        w2sb = cx.sbuf("w2sb", (128, 2 * 20 * 128), BF16)
        outacc = cx.sbuf("outacc", (128, 8 * 1024), F32)
        xot = cx.sbuf("xot", (128, 1024), F32)
        w1_used = {}
        w2_used = {}

        @block.gpsimd
        def _(g):
            g = cx.ser_g(g)
            ins = g.dma_start(bm_sb[:], inp["biasmat"])
            cx.inc(ins, cx.lg, 16)

        for hf in range(2):
            th0 = hf * 1024
            for blk in range(8):
                units = range(10 * blk, 10 * blk + 10)
                for iu, u in enumerate(units):
                    slot = u % 2
                    gated = u < 64

                    @block.gpsimd
                    def _(g):
                        g = cx.ser_g(g)
                        if slot in w1_used:
                            g.wait_ge(cx.mm, w1_used[slot])
                        wload(g, cx, w1sb[:, slot * 2048:(slot + 1) * 2048],
                              inp["w1big"][:, u * 256:(u + 1) * 256], 8)

                    @block.sync
                    def _(sy):
                        cx.wait(sy, cx.ev)
                        for j in range(2):
                            cx.inc(sy.dma_start(
                                b1sb[:, slot * 2 + j:slot * 2 + j + 1],
                                inp["b1big"][u * 256 + j * 128:
                                             u * 256 + (j + 1) * 128, :]),
                                cx.ld, 16)
                            sy.wait_ge(cx.ld, cx.val(cx.ld))
                        if gated:
                            cx.wait(sy, cx.vv, cx.ev)
                            ins = sy.dma_start(grow[:, :], gte[u:u + 1, :])
                            cx.inc(ins, cx.ld, 16)
                            sy.wait_ge(cx.ld, cx.val(cx.ld))

                    if gated:
                        @block.tensor
                        def _(t):
                            cx.wait(t, cx.ld, cx.lg)
                            for c2 in range(2):
                                bi, ps = cx.bank(t)
                                ins = t.matmul(
                                    ps[:],
                                    onesln[0:1, :],
                                    grow[:, th0 + c2 * 512:th0 + (c2 + 1) * 512],
                                    start=True, stop=True)
                                cx.inc(ins, cx.mm)

                                @block.scalar
                                def _(s):
                                    s = cx.ser_s(s)
                                    cx.wait(s, cx.mm)
                                    ins = s.activation(
                                        gbc[:, c2 * 512:(c2 + 1) * 512],
                                        ps[:], AF.Copy, scale=1024.0)
                                    del ins
                                cx.bank_freed(bi, cx.ev)

                    ti0 = 2 * (u - 10 * blk)
                    for po2 in range(2):
                        @block.tensor
                        def _(t):
                            cx.wait(t, cx.ld, cx.lg)
                            for c2 in range(2):
                                bi, ps = cx.bank(t)
                                for ci in range(8):
                                    ins = t.matmul(
                                        ps[:],
                                        w1sb[:, slot * 2048 + ci * 256 +
                                             po2 * 128:slot * 2048 + ci * 256 +
                                             po2 * 128 + 128],
                                        xno_t(ci, th0 + c2 * 512, 512),
                                        start=(ci == 0), stop=(ci == 7))
                                cx.inc(ins, cx.mm)
                                if po2 == 1 and c2 == 1:
                                    w1_used[slot] = cx.val(cx.mm)

                                @block.scalar
                                def _(s):
                                    s = cx.ser_s(s)
                                    cx.wait(s, cx.mm)
                                    dst = (mblk if not gated else mid)
                                    col = ((ti0 + po2) * 1024 + c2 * 512) if \
                                        not gated else (po2 * 1024 + c2 * 512)
                                    ins = s.activation(
                                        dst[:, col:col + 512], ps[:], AF.Silu,
                                        bias=b1sb[:, slot * 2 + po2:
                                                  slot * 2 + po2 + 1])
                                    del ins
                                cx.bank_freed(bi, cx.ev)

                    if gated:
                        @block.vector
                        def _(v):
                            v = cx.ser_v(v)
                            cx.wait(v, cx.ev)
                            for po2 in range(2):
                                ins = v.tensor_tensor(
                                    mblk[:, (ti0 + po2) * 1024:
                                         (ti0 + po2 + 1) * 1024],
                                    mid[:, po2 * 1024:(po2 + 1) * 1024],
                                    gbc[:, 0:1024], op=ALU.mult)
                            del ins

                for po in range(8):
                    pslot = po % 2

                    @block.gpsimd
                    def _(g):
                        g = cx.ser_g(g)
                        if pslot in w2_used:
                            g.wait_ge(cx.mm, w2_used[pslot])
                        src = inp["w2big"][blk * 2560:(blk + 1) * 2560,
                                           po * 128:(po + 1) * 128]
                        wload(g, cx, w2sb[:, pslot * 2560:(pslot + 1) * 2560],
                              src, 20)

                    for c2 in range(2):
                        @block.tensor
                        def _(t):
                            cx.wait(t, cx.ld, cx.lg, cx.vv, cx.ev)
                            bi, ps = cx.bank(t)
                            first = True
                            if blk == 0:
                                t.matmul(ps[:], bm_sb[:, po * 128:(po + 1) * 128],
                                         gte[:, th0 + c2 * 512:
                                             th0 + (c2 + 1) * 512],
                                         start=True, stop=False)
                                first = False
                            for ti in range(20):
                                ins = t.matmul(
                                    ps[:],
                                    w2sb[:, pslot * 2560 + ti * 128:
                                         pslot * 2560 + (ti + 1) * 128],
                                    mblk[:, ti * 1024 + c2 * 512:
                                         ti * 1024 + c2 * 512 + 512],
                                    start=first and ti == 0, stop=(ti == 19))
                                first = False
                            cx.inc(ins, cx.mm)
                            if c2 == 1:
                                w2_used[pslot] = cx.val(cx.mm)

                            @block.vector
                            def _(v):
                                v = cx.ser_v(v)
                                cx.wait(v, cx.mm, cx.st)
                                dst = outacc[:, po * 1024 + c2 * 512:
                                             po * 1024 + c2 * 512 + 512]
                                if blk == 0:
                                    ins = v.tensor_copy(dst, ps[:])
                                else:
                                    ins = v.tensor_tensor(dst, dst, ps[:],
                                                          op=ALU.add)
                                del ins
                            cx.bank_freed(bi, cx.vv)

            for po in range(8):
                @block.sync
                def _(sy):
                    cx.wait(sy, cx.vv)
                    ins = sy.dma_start(xot[:],
                                       inp["xown"][po * 128:(po + 1) * 128,
                                                   th0:th0 + 1024])
                    cx.inc(ins, cx.ld, 16)
                    sy.wait_ge(cx.ld, cx.val(cx.ld))

                @block.vector
                def _(v):
                    v = cx.ser_v(v)
                    cx.wait(v, cx.ld, cx.lg, cx.vv)
                    ins = v.tensor_tensor(outacc[:, po * 1024:(po + 1) * 1024],
                                          outacc[:, po * 1024:(po + 1) * 1024],
                                          xot[:], op=ALU.add)
                    del ins

                @block.sync
                def _(sy):
                    cx.wait(sy, cx.vv)
                    ins = sy.dma_start(
                        xffn_out[po * 128:(po + 1) * 128, th0:th0 + 1024],
                        outacc[:, po * 1024:(po + 1) * 1024])
                    cx.inc(ins, cx.st, 16)
                    sy.wait_ge(cx.st, cx.val(cx.st))

        @block.sync
        def _(sy):
            cx.wait(sy, cx.st)

    return nc, cx, inp




def build_L1():
    nc = bass.Bass("TRN2", target_bir_lowering=False)
    f32in = [
        ("pA", [D, S]), ("pB", [D, S]), ("pC", [D, S]),
        ("pAo", [D, TOWN]), ("pBo", [D, TOWN]), ("pCo", [D, TOWN]),
        ("wrkv", [D, 1536]), ("brkv", [1536, 1]),
        ("nul", [512, 1]), ("gal", [512, 1]),
        ("wo", [512, D]), ("bo", [D, 1]),
        ("w1T", [D, 4096]), ("w2T", [D, 4096]),
        ("h1T", [4096, 4096]), ("h2T", [4096, 4096]),
        ("w3T", [4096, D]),
        ("s1", [128, 32]), ("s2", [128, 32]), ("sh1", [128, 32]),
        ("sh2", [128, 32]), ("s3", [128, 8]),
    ]
    inp = {}
    for name, shape in f32in:
        inp[name] = nc.dram_tensor(name, shape, F32, kind="ExternalInput").ap()
    inp["onesln"] = nc.dram_tensor("onesln", [128, 128], BF16,
                                   kind="ExternalInput").ap()
    attn_out = nc.dram_tensor("attn_out", [D, S], F32,
                              kind="ExternalOutput").ap()
    xffn_out = nc.dram_tensor("xffn_out", [D, TOWN], F32,
                              kind="ExternalOutput").ap()

    cx = Ctx(nc)
    with nc.Block() as block:
        cx.block = block
        onesln = cx.sbuf("onesln", (128, 128), BF16)

        @block.sync
        def _(sy):
            cx.inc(sy.dma_start(onesln[:], inp["onesln"]), cx.ld, 16)
            sy.wait_ge(cx.ld, cx.val(cx.ld))

        xn = cx.sbuf("xn", (128, 8 * S), BF16)
        emit_ln(cx, block, inp["pA"], xn, S, onesln,
                extra=(inp["pB"], inp["pC"]))

        def xn_t(i, c0, w):
            return xn[:, i * S + c0:i * S + c0 + w]

        cx.fence(block)

        wrkv = cx.sbuf("wrkv", (128, 8 * 1536), BF16)
        brkv = cx.sbuf("brkv", (128, 12), F32)
        nlg = cx.sbuf("nlg", (128, 16), F32)
        xrT = cx.sbuf("xrT", (128, 4 * S), BF16)
        rbf = cx.sbuf("rbf", (128, S), BF16)
        kbf = cx.sbuf("kbf", (128, S), BF16)
        vbf = cx.sbuf("vbf", (128, S), BF16)
        u1 = cx.sbuf("u1", (128, S), F32)
        u2 = cx.sbuf("u2", (128, S), F32)

        @block.gpsimd
        def _(g):
            g = cx.ser_g(g)
            wload(g, cx, wrkv[:], inp["wrkv"], 8)

        @block.sync
        def _(sy):
            for j in range(12):
                cx.inc(sy.dma_start(brkv[:, j:j + 1],
                                    inp["brkv"][j * 128:(j + 1) * 128, :]),
                       cx.ld, 16)
                sy.wait_ge(cx.ld, cx.val(cx.ld))
            for j in range(4):
                cx.inc(sy.dma_start(nlg[:, j:j + 1],
                                    inp["nul"][j * 128:(j + 1) * 128, :]),
                       cx.ld, 16)
                sy.wait_ge(cx.ld, cx.val(cx.ld))
                cx.inc(sy.dma_start(nlg[:, 4 + j:5 + j],
                                    inp["gal"][j * 128:(j + 1) * 128, :]),
                       cx.ld, 16)
                sy.wait_ge(cx.ld, cx.val(cx.ld))

        @block.scalar
        def _(s):
            s = cx.ser_s(s)
            cx.wait(s, cx.ld)
            s.activation(nlg[:, 8:12], nlg[:, 0:4], AF.Exp)

        @block.vector
        def _(v):
            v = cx.ser_v(v)
            cx.wait(v, cx.ev, cx.ld)
            v.tensor_scalar(nlg[:, 8:12], nlg[:, 8:12], -1.0, None,
                            op0=ALU.mult)
            v.tensor_tensor(nlg[:, 12:16], nlg[:, 0:4], nlg[:, 4:8],
                            op=ALU.add)

        @block.scalar
        def _(s):
            s = cx.ser_s(s)
            cx.wait(s, cx.vv)
            s.activation(nlg[:, 8:12], nlg[:, 8:12], AF.Exp)
            s.activation(nlg[:, 12:16], nlg[:, 12:16], AF.Exp)

        @block.vector
        def _(v):
            v = cx.ser_v(v)
            cx.wait(v, cx.ev)
            v.tensor_scalar(nlg[:, 12:16], nlg[:, 12:16], -1.0, None,
                            op0=ALU.add)

        for j in range(4):
            for blk_i, (dst, func, bcol) in enumerate(
                    ((rbf, AF.Sigmoid, j), (kbf, AF.Exp, 4 + j),
                     (vbf, AF.Identity, 8 + j))):
                for c in range(8):
                    c0 = c * 512

                    @block.tensor
                    def _(t):
                        cx.wait(t, cx.lg, cx.ld)
                        if c == 0:
                            cx.wait(t, cx.vv, cx.ev)
                        bi, ps = cx.bank(t)
                        col0 = (blk_i * 4 + j) * 128
                        for ci in range(8):
                            ins = t.matmul(ps[:],
                                           wrkv[:, ci * 1536 + col0:
                                                ci * 1536 + col0 + 128],
                                           xn_t(ci, c0, 512),
                                           start=(ci == 0), stop=(ci == 7))
                        cx.inc(ins, cx.mm)

                        @block.scalar
                        def _(s):
                            s = cx.ser_s(s)
                            cx.wait(s, cx.mm)
                            s.activation(dst[:, c0:c0 + 512], ps[:], func,
                                         bias=brkv[:, bcol:bcol + 1])
                        cx.bank_freed(bi, cx.ev)

            @block.vector
            def _(v):
                v = cx.ser_v(v)
                cx.wait(v, cx.ev)
                lamap = nlg[:, 8 + j:9 + j].broadcast_to((128, S))
                gamc = nlg[:, 12 + j:13 + j]
                v.tensor_tensor(vbf[:], kbf[:], vbf[:], op=ALU.mult)
                v.tensor_tensor_scan(u1[:], lamap, vbf[:], 0.0,
                                     op0=ALU.mult, op1=ALU.add)
                v.tensor_tensor_scan(u2[:], lamap, kbf[:], 0.0,
                                     op0=ALU.mult, op1=ALU.add)
                v.scalar_tensor_tensor(u1[:], vbf[:], gamc, u1[:],
                                       op0=ALU.mult, op1=ALU.add)
                v.scalar_tensor_tensor(u2[:], kbf[:], gamc, u2[:],
                                       op0=ALU.mult, op1=ALU.add)
                v.tensor_scalar(u2[:], u2[:], 1e-6, None, op0=ALU.add)
                v.reciprocal(u2[:], u2[:])
                v.tensor_tensor(u1[:], u1[:], u2[:], op=ALU.mult)
                v.tensor_tensor(xrT[:, j * S:(j + 1) * S], u1[:], rbf[:],
                                op=ALU.mult)

        cx.fence(block)
        cx.free_sbuf("rbf", "kbf", "vbf", "u1", "u2")

        wo_sb = cx.sbuf("wo_sb", (128, 4 * D), BF16)
        bo_sb = cx.sbuf("bo_sb", (128, 8), F32)
        stg = cx.sbuf("stg", (128, 2 * 512), F32)

        @block.gpsimd
        def _(g):
            g = cx.ser_g(g)
            wload(g, cx, wo_sb[:], inp["wo"], 4)

        @block.sync
        def _(sy):
            for po in range(8):
                cx.inc(sy.dma_start(bo_sb[:, po:po + 1],
                                    inp["bo"][po * 128:(po + 1) * 128, :]),
                       cx.ld, 16)
                sy.wait_ge(cx.ld, cx.val(cx.ld))

        for po in range(8):
            for c in range(8):
                c0 = c * 512

                @block.tensor
                def _(t):
                    cx.wait(t, cx.lg, cx.ld, cx.vv)
                    bi, ps = cx.bank(t)
                    for j in range(4):
                        ins = t.matmul(ps[:],
                                       wo_sb[:, j * D + po * 128:
                                             j * D + (po + 1) * 128],
                                       xrT[:, j * S + c0:j * S + c0 + 512],
                                       start=(j == 0), stop=(j == 3))
                    cx.inc(ins, cx.mm)
                    sslot = (po * 8 + c) % 2

                    @block.scalar
                    def _(s):
                        s = cx.ser_s(s)
                        cx.wait(s, cx.mm, cx.st)
                        s.activation(stg[:, sslot * 512:(sslot + 1) * 512],
                                     ps[:], AF.Identity,
                                     bias=bo_sb[:, po:po + 1])
                    cx.bank_freed(bi, cx.ev)

                    @block.sync
                    def _(sy):
                        cx.wait(sy, cx.ev)
                        cx.inc(sy.dma_start(
                            attn_out[po * 128:(po + 1) * 128, c0:c0 + 512],
                            stg[:, sslot * 512:(sslot + 1) * 512]), cx.st, 16)
                        sy.wait_ge(cx.st, cx.val(cx.st))

        cx.fence(block)
        cx.free_sbuf("xn", "wrkv", "brkv", "nlg", "xrT", "wo_sb", "bo_sb",
                     "stg")

        xno = cx.sbuf("xno", (128, 8 * TOWN), BF16)
        emit_ln(cx, block, inp["pAo"], xno, TOWN, onesln,
                extra=(inp["pBo"], inp["pCo"]))

        def xno_t(i, c0, w):
            return xno[:, i * TOWN + c0:i * TOWN + c0 + w]

        cx.fence(block)

        h1b = cx.sbuf("h1b", (128, 32 * 1024), BF16)
        h2b = cx.sbuf("h2b", (128, 32 * 1024), BF16)
        scl = cx.sbuf("scl", (128, 4 * 32 + 8), F32)
        wst = cx.sbuf("wst", (128, 2 * 32 * 128), BF16)
        stg2 = cx.sbuf("stg2", (128, 2 * 512), F32)
        xpt = cx.sbuf("xpt", (128, 512), F32)
        w_used = {}

        @block.sync
        def _(sy):
            for si, nm in enumerate(("s1", "s2", "sh1", "sh2")):
                cx.inc(sy.dma_start(scl[:, si * 32:(si + 1) * 32], inp[nm]),
                       cx.ld, 16)
                sy.wait_ge(cx.ld, cx.val(cx.ld))
            cx.inc(sy.dma_start(scl[:, 128:136], inp["s3"]), cx.ld, 16)
            sy.wait_ge(cx.ld, cx.val(cx.ld))

        for hf in range(2):
            th0 = hf * 1024

            def stage(wname, nci, npo, sccol, func, dstbuf, srcfn, scband):
                for po in range(npo):
                    slot = po % 2

                    @block.gpsimd
                    def _(g):
                        g = cx.ser_g(g)
                        if slot in w_used:
                            g.wait_ge(cx.mm, w_used[slot])
                        wload(g, cx, wst[:, slot * 4096:slot * 4096 + nci * 128],
                              inp[wname][:, po * 128:(po + 1) * 128], nci)

                    for ch in range(2):
                        @block.tensor
                        def _(t):
                            cx.wait(t, cx.lg, cx.ld, cx.vv, cx.ev)
                            bi, ps = cx.bank(t)
                            for ci in range(nci):
                                ins = t.matmul(
                                    ps[:],
                                    wst[:, slot * 4096 + ci * 128:
                                        slot * 4096 + (ci + 1) * 128],
                                    srcfn(ci, th0 + ch * 512, 512),
                                    start=(ci == 0), stop=(ci == nci - 1))
                            cx.inc(ins, cx.mm)
                            if ch == 1:
                                w_used[slot] = cx.val(cx.mm)

                            @block.scalar
                            def _(s):
                                s = cx.ser_s(s)
                                cx.wait(s, cx.mm)
                                s.activation(
                                    dstbuf[:, po * 1024 + ch * 512:
                                           po * 1024 + ch * 512 + 512],
                                    ps[:], func,
                                    scale=scl[:, scband + po:scband + po + 1])
                            cx.bank_freed(bi, cx.ev)

            def h1b_t(ci, c0, w):
                return h1b[:, ci * 1024 + (c0 - th0):ci * 1024 + (c0 - th0) + w]

            stage("w1T", 8, 32, 0, AF.Silu, h1b, xno_t, 0)
            stage("w2T", 8, 32, 32, AF.Identity, h2b, xno_t, 32)

            @block.vector
            def _(v):
                v = cx.ser_v(v)
                cx.wait(v, cx.ev)
                for po in range(32):
                    v.tensor_tensor(h1b[:, po * 1024:(po + 1) * 1024],
                                    h1b[:, po * 1024:(po + 1) * 1024],
                                    h2b[:, po * 1024:(po + 1) * 1024],
                                    op=ALU.mult)

            stage("h1T", 32, 32, 64, AF.Gelu, h2b, h1b_t, 64)

            def h2b_t(ci, c0, w):
                return h2b[:, ci * 1024 + (c0 - th0):ci * 1024 + (c0 - th0) + w]

            stage("h2T", 32, 32, 96, AF.Gelu, h1b, h2b_t, 96)

            for po in range(8):
                slot = po % 2

                @block.gpsimd
                def _(g):
                    g = cx.ser_g(g)
                    if slot in w_used:
                        g.wait_ge(cx.mm, w_used[slot])
                    wload(g, cx, wst[:, slot * 4096:slot * 4096 + 32 * 128],
                          inp["w3T"][:, po * 128:(po + 1) * 128], 32)

                for ch in range(2):
                    @block.tensor
                    def _(t):
                        cx.wait(t, cx.lg, cx.ld, cx.vv, cx.ev)
                        bi, ps = cx.bank(t)
                        for ci in range(32):
                            ins = t.matmul(
                                ps[:],
                                wst[:, slot * 4096 + ci * 128:
                                    slot * 4096 + (ci + 1) * 128],
                                h1b_t(ci, th0 + ch * 512, 512),
                                start=(ci == 0), stop=(ci == 31))
                        cx.inc(ins, cx.mm)
                        if ch == 1:
                            w_used[slot] = cx.val(cx.mm)
                        sslot = (po * 2 + ch) % 2

                        @block.scalar
                        def _(s):
                            s = cx.ser_s(s)
                            cx.wait(s, cx.mm, cx.st, cx.vv)
                            s.activation(stg2[:, sslot * 512:(sslot + 1) * 512],
                                         ps[:], AF.Identity,
                                         scale=scl[:, 128 + po:129 + po])
                        cx.bank_freed(bi, cx.ev)

                        for src in ("pAo", "pBo", "pCo"):
                            @block.sync
                            def _(sy):
                                cx.wait(sy, cx.vv, cx.ev)
                                cx.inc(sy.dma_start(
                                    xpt[:],
                                    inp[src][po * 128:(po + 1) * 128,
                                             th0 + ch * 512:th0 + ch * 512 + 512]),
                                    cx.ld, 16)
                                sy.wait_ge(cx.ld, cx.val(cx.ld))

                            @block.vector
                            def _(v):
                                v = cx.ser_v(v)
                                cx.wait(v, cx.ld, cx.ev)
                                v.tensor_tensor(
                                    stg2[:, sslot * 512:(sslot + 1) * 512],
                                    stg2[:, sslot * 512:(sslot + 1) * 512],
                                    xpt[:], op=ALU.add)

                        @block.sync
                        def _(sy):
                            cx.wait(sy, cx.vv)
                            cx.inc(sy.dma_start(
                                xffn_out[po * 128:(po + 1) * 128,
                                         th0 + ch * 512:th0 + ch * 512 + 512],
                                stg2[:, sslot * 512:(sslot + 1) * 512]),
                                cx.st, 16)
                            sy.wait_ge(cx.st, cx.val(cx.st))

        @block.sync
        def _(sy):
            cx.wait(sy, cx.st)

    return nc, cx, inp


def host_prep_L1(I, pA, pB, pC):
    Wr = np.asarray(I["L1_attn_W_rkv"], np.float32)
    br = np.asarray(I["L1_attn_b_rkv"], np.float32)
    Wo = np.asarray(I["L1_attn_W_o"], np.float32)
    bo = np.asarray(I["L1_attn_b_o"], np.float32)
    nul = np.asarray(I["L1_attn_nu_log"], np.float32)
    gal = np.asarray(I["L1_attn_gamma_log"], np.float32)
    onesln = np.full((128, 128), 1.0 / 1024.0, np.float32).astype(bf16)
    ffn = {
        "w1T": f32c(np.asarray(I["L1_ffn_w1_w"]).T),
        "w2T": f32c(np.asarray(I["L1_ffn_w2_w"]).T),
        "h1T": f32c(np.asarray(I["L1_ffn_h1_w"]).T),
        "h2T": f32c(np.asarray(I["L1_ffn_h2_w"]).T),
        "w3T": f32c(np.asarray(I["L1_ffn_w3_w"]).T),
        "s1": f32c(np.asarray(I["L1_ffn_w1_s"]).reshape(32, 128).T),
        "s2": f32c(np.asarray(I["L1_ffn_w2_s"]).reshape(32, 128).T),
        "sh1": f32c(np.asarray(I["L1_ffn_h1_s"]).reshape(32, 128).T),
        "sh2": f32c(np.asarray(I["L1_ffn_h2_s"]).reshape(32, 128).T),
        "s3": f32c(np.asarray(I["L1_ffn_w3_s"]).reshape(8, 128).T),
    }
    maps = []
    for c in range(8):
        b, p = divmod(c, 2)
        ph = slice(p * 512, (p + 1) * 512)
        own = slice(p * TOWN, (p + 1) * TOWN)
        m = {
            "pA": pA[b], "pB": pB[b], "pC": pC[b],
            "pAo": f32c(pA[b][:, own]), "pBo": f32c(pB[b][:, own]),
            "pCo": f32c(pC[b][:, own]),
            "wrkv": f32c(np.concatenate(
                [Wr[:, 0:D][:, ph], Wr[:, D:2 * D][:, ph],
                 Wr[:, 2 * D:3 * D][:, ph]], axis=1)),
            "brkv": f32c(np.concatenate(
                [br[0:D][ph], br[D:2 * D][ph], br[2 * D:3 * D][ph]])[:, None]),
            "nul": f32c(nul[ph][:, None]), "gal": f32c(gal[ph][:, None]),
            "wo": f32c(Wo[ph, :]),
            "bo": f32c((bo if p == 0 else np.zeros_like(bo))[:, None]),
            "onesln": onesln,
        }
        m.update(ffn)
        maps.append(m)
    return maps


def run_L1(I, resA):
    pA = [resA[2 * b]["attn_out"] for b in range(4)]
    pB = [resA[2 * b + 1]["attn_out"] for b in range(4)]
    pC = [np.concatenate([resA[2 * b]["xffn_out"],
                          resA[2 * b + 1]["xffn_out"]], axis=1)
          for b in range(4)]
    nc, cx, _ = build_L1()
    maps = host_prep_L1(I, pA, pB, pC)
    res = run_bass_kernel_spmd(nc, maps, core_ids=list(range(8)))
    cx.close()
    return res.results




def emit_banded_mha(cx, block, inp, xn_t, oTh, consts):
    idb = consts["idb"]
    H = 8
    wqkv = cx.sbuf("wqkv", (128, 3 * 512), BF16)
    bqkv = cx.sbuf("bqkv", (64, 3 * H), F32)
    qT = cx.sbuf("qT", (64, S), BF16)
    kT = cx.sbuf("kT", (64, S), BF16)
    vTM = cx.sbuf("vTM", (128, 32 * 64), BF16)
    vTs = cx.sbuf("vTs", (64, 2 * 512), BF16)
    strip = cx.sbuf("strip", (128, S), F32)
    pstrip = cx.sbuf("pstrip", (128, S), BF16)
    ptbig = cx.sbuf("ptbig", (128, 16 * 128), BF16)
    rsum = cx.sbuf("rsum", (128, 1), F32)
    nmax = cx.sbuf("nmax", (128, 1), F32)
    mfull = cx.sbuf("mfull", (128, S), F32)
    mband = cx.sbuf("mband", (128, 8 * 128), F32)

    @block.sync
    def _(sy):
        cx.inc(sy.dma_start(mfull[:], inp["mfull"]), cx.ld, 16)
        sy.wait_ge(cx.ld, cx.val(cx.ld))
        cx.inc(sy.dma_start(mband[:], inp["mband"]), cx.ld, 16)
        sy.wait_ge(cx.ld, cx.val(cx.ld))
        for j, nm in enumerate(("bq", "bk", "bv")):
            for h in range(H):
                cx.inc(sy.dma_start(bqkv[:, j * H + h:j * H + h + 1],
                                    inp[nm][h * 64:(h + 1) * 64, :]),
                       cx.ld, 16)
                sy.wait_ge(cx.ld, cx.val(cx.ld))

    def ktiles_of(qi):
        if qi == 0:
            return list(range(32)), None
        kts = []
        masks = []
        if qi >= 3:
            kts.append(0)
            masks.append(5)
        for kt in range(max(0, qi - 2), min(32, qi + 3)):
            kts.append(kt)
            d = kt - qi
            if kt == 0 and d == -2:
                masks.append(6)
            elif kt == 0 and d == -1:
                masks.append(7)
            else:
                masks.append(d + 2)
        return kts, masks

    for h in range(H):
        @block.gpsimd
        def _(g):
            g = cx.ser_g(g)
            cx.wait(g, cx.mm)
            for j, wname in enumerate(("wq", "wk", "wv")):
                wsl = inp[wname][:, h * 64:(h + 1) * 64]
                wload(g, cx, wqkv[:, j * 512:(j + 1) * 512], wsl, 8)

        for j, name in enumerate(("q", "k", "v")):
            for c in range(8):
                c0 = c * 512
                slot = c % 2

                @block.tensor
                def _(t):
                    cx.wait(t, cx.lg, cx.ld)
                    if c == 0:
                        cx.wait(t, cx.vv, cx.ev)
                    bi, ps = cx.bank(t)
                    for ci in range(8):
                        ins = t.matmul(
                            ps[0:64, :],
                            wqkv[:, j * 512 + ci * 64:j * 512 + (ci + 1) * 64],
                            xn_t(ci, c0, 512),
                            start=(ci == 0), stop=(ci == 7))
                    cx.inc(ins, cx.mm)

                    @block.scalar
                    def _(s):
                        s = cx.ser_s(s)
                        cx.wait(s, cx.mm, cx.vv)
                        dst = vTs[:, slot * 512:(slot + 1) * 512] if name == "v" \
                            else (qT if name == "q" else kT)[:, c0:c0 + 512]
                        s.activation(dst, ps[0:64, :], AF.Identity,
                                     bias=bqkv[:, j * H + h:j * H + h + 1])
                    cx.bank_freed(bi, cx.ev)

                    if name == "v":
                        @block.tensor
                        def _(t2):
                            cx.wait(t2, cx.ev)
                            for u in range(4):
                                kt = c * 4 + u
                                bb, psb = cx.bankb(t2)
                                ins = t2.transpose(
                                    psb[:, 0:64],
                                    vTs[:, slot * 512 + u * 128:
                                        slot * 512 + (u + 1) * 128],
                                    idb[0:64, 0:64])
                                cx.inc(ins, cx.mm)

                                @block.vector
                                def _(v):
                                    v = cx.ser_v(v)
                                    cx.wait(v, cx.mm)
                                    v.tensor_copy(vTM[:, kt * 64:(kt + 1) * 64],
                                                  psb[:, 0:64])
                                cx.bankb_freed(bb, cx.vv)

        for qi in range(32):
            kts, masks = ktiles_of(qi)
            nseg = len(kts)

            @block.tensor
            def _(t):
                cx.wait(t, cx.vv, cx.ev)
                for g0 in range(0, nseg, 4):
                    gn = min(4, nseg - g0)
                    bi, ps = cx.bank(t)
                    for u in range(gn):
                        kt = kts[g0 + u]
                        ins = t.matmul(ps[:, u * 128:(u + 1) * 128],
                                       qT[:, qi * 128:(qi + 1) * 128],
                                       kT[:, kt * 128:(kt + 1) * 128],
                                       start=True, stop=True)
                    cx.inc(ins, cx.mm)

                    @block.vector
                    def _(v):
                        v = cx.ser_v(v)
                        cx.wait(v, cx.mm, cx.ev)
                        for u in range(gn):
                            si = g0 + u
                            if qi == 0:
                                mk = mfull[:, kts[si] * 128:(kts[si] + 1) * 128]
                            else:
                                mi = masks[si]
                                mk = mband[:, mi * 128:(mi + 1) * 128]
                            v.tensor_tensor(strip[:, si * 128:(si + 1) * 128],
                                            ps[:, u * 128:(u + 1) * 128],
                                            mk, op=ALU.add)
                    cx.bank_freed(bi, cx.vv)

            kw = nseg * 128

            @block.vector
            def _(v):
                v = cx.ser_v(v)
                cx.wait(v, cx.vv, cx.ev)
                v.reduce_max(nmax[:], strip[:, 0:kw], axis=AX.X, negate=True)
                v.tensor_scalar(nmax[:], nmax[:], 0.125, None, op0=ALU.mult)

            @block.scalar
            def _(s):
                s = cx.ser_s(s)
                cx.wait(s, cx.vv, cx.mm)
                s.activation(pstrip[:, 0:kw], strip[:, 0:kw], AF.Exp,
                             bias=nmax[:, 0:1], scale=0.125,
                             accum_out=rsum[:])

            @block.vector
            def _(v):
                v = cx.ser_v(v)
                cx.wait(v, cx.ev)
                v.reciprocal(nmax[:], rsum[:])
                v.tensor_scalar(pstrip[:, 0:kw], pstrip[:, 0:kw],
                                nmax[:, 0:1], None, op0=ALU.mult)

            @block.tensor
            def _(t):
                cx.wait(t, cx.vv)
                bo, pso = cx.bank(t)
                for G0 in range(0, nseg, 16):
                    Gn = min(16, nseg - G0)
                    for u0 in range(0, Gn, 8):
                        un = min(8, Gn - u0)
                        bb, psb = cx.bankb(t)
                        for u in range(un):
                            si = G0 + u0 + u
                            ins = t.transpose(psb[:, u * 128:(u + 1) * 128],
                                              pstrip[:, si * 128:(si + 1) * 128],
                                              idb[:])
                        cx.inc(ins, cx.mm)

                        @block.vector
                        def _(v):
                            v = cx.ser_v(v)
                            cx.wait(v, cx.mm)
                            v.tensor_copy(ptbig[:, u0 * 128:(u0 + un) * 128],
                                          psb[:, 0:un * 128])
                        cx.bankb_freed(bb, cx.vv)
                    cx.wait(t, cx.vv)
                    for u in range(Gn):
                        si = G0 + u
                        kt = kts[si]
                        ins = t.matmul(pso[0:64, 0:128],
                                       vTM[:, kt * 64:(kt + 1) * 64],
                                       ptbig[:, u * 128:(u + 1) * 128],
                                       start=(si == 0), stop=(si == nseg - 1))
                    cx.inc(ins, cx.mm)

                @block.scalar
                def _(s):
                    s = cx.ser_s(s)
                    cx.wait(s, cx.mm)
                    hp = h % 2
                    s.activation(
                        oTh[hp * 64:hp * 64 + 64,
                            (h // 2) * S + qi * 128:(h // 2) * S + (qi + 1) * 128],
                        pso[0:64, 0:128], AF.Copy)
                cx.bank_freed(bo, cx.ev)

    cx.free_sbuf("wqkv", "bqkv", "qT", "kT", "vTM", "vTs", "strip", "pstrip",
                 "ptbig", "rsum", "nmax", "mfull", "mband")


def emit_ffn_iter(cx, block, inp, xsrc_t, nf, mask_col, xnew, pool,
                  accum_pool, th0, hb, fblock=16):
    wst = cx.sbuf("fi_wst", (128, 2 * fblock * 128), BF16)
    used = {}
    nblk = (nf + fblock - 1) // fblock
    for blk in range(nblk):
        f0 = blk * fblock
        fn_ = min(fblock, nf - f0)
        for fo in range(f0, f0 + fn_):
            slot = fo % 2

            @block.gpsimd
            def _(g):
                g = cx.ser_g(g)
                wv = max(used.get(("m", slot), 0), used.get("wp_done", 0))
                if wv > 0:
                    g.wait_ge(cx.mm, wv)
                wload(g, cx, wst[:, slot * 1024:slot * 1024 + 8 * 128],
                      inp["wmT"][:, fo * 128:(fo + 1) * 128], 8)

            for ch in range(2):
                @block.tensor
                def _(t):
                    cx.wait(t, cx.lg, cx.ld, cx.vv, cx.ev)
                    bi, ps = cx.bank(t)
                    for ci in range(8):
                        ins = t.matmul(ps[:],
                                       wst[:, slot * 1024 + ci * 128:
                                           slot * 1024 + (ci + 1) * 128],
                                       xsrc_t(ci, th0 + ch * 512, 512),
                                       start=(ci == 0), stop=(ci == 7))
                    cx.inc(ins, cx.mm)
                    if ch == 1:
                        used[("m", slot)] = cx.val(cx.mm)

                    @block.scalar
                    def _(s):
                        s = cx.ser_s(s)
                        cx.wait(s, cx.mm)
                        kw = {}
                        if mask_col is not None:
                            kw["scale"] = mask_col(fo)
                        s.activation(hb[:, (fo - f0) * 1024 + ch * 512:
                                        (fo - f0) * 1024 + ch * 512 + 512],
                                     ps[:], AF.Relu, **kw)
                    cx.bank_freed(bi, cx.ev)

        used["wm_done"] = cx.val(cx.mm)
        for po in range(8):
            slot = po % 2

            @block.gpsimd
            def _(g):
                g = cx.ser_g(g)
                wv = max(used.get(("p", slot), 0), used.get("wm_done", 0))
                if wv > 0:
                    g.wait_ge(cx.mm, wv)
                wload(g, cx,
                      wst[:, slot * fblock * 128:slot * fblock * 128 + fn_ * 128],
                      inp["wpT"][f0 * 128:(f0 + fn_) * 128,
                                 po * 128:(po + 1) * 128], fn_)

            for ch in range(2):
                @block.tensor
                def _(t):
                    cx.wait(t, cx.lg, cx.ld, cx.vv, cx.ev)
                    bi, ps = cx.bank(t)
                    for ti in range(fn_):
                        ins = t.matmul(
                            ps[:],
                            wst[:, slot * fblock * 128 + ti * 128:
                                slot * fblock * 128 + (ti + 1) * 128],
                            hb[:, ti * 1024 + ch * 512:ti * 1024 + ch * 512 + 512],
                            start=(ti == 0), stop=(ti == fn_ - 1))
                    cx.inc(ins, cx.mm)
                    if ch == 1:
                        used[("p", slot)] = cx.val(cx.mm)
                        used["wp_done"] = cx.val(cx.mm)

                    @block.vector
                    def _(v):
                        v = cx.ser_v(v)
                        cx.wait(v, cx.mm, cx.st)
                        dst = xnew[:, po * 1024 + ch * 512:
                                   po * 1024 + ch * 512 + 512]
                        if blk == 0:
                            v.tensor_copy(dst, ps[:])
                        else:
                            v.tensor_tensor(dst, dst, ps[:], op=ALU.add)
                    cx.bank_freed(bi, cx.vv)

    po2 = cx.sbuf("fi_po2", (128, 8), F32)

    @block.vector
    def _(v):
        v = cx.ser_v(v)
        cx.wait(v, cx.vv)
        for po in range(8):
            v.reduce_sum(po2[:, po:po + 1],
                         xnew[:, po * 1024:(po + 1) * 1024], axis=AX.X)
        if accum_pool:
            v.tensor_tensor(pool[:], pool[:], po2[:], op=ALU.add)
        else:
            v.tensor_copy(pool[:], po2[:])
    cx.free_sbuf("fi_po2", "fi_wst")


def build_L2a():
    nc = bass.Bass("TRN2", target_bir_lowering=False)
    f32in = [
        ("pA", [D, S]), ("pB", [D, S]), ("pC", [D, S]),
        ("pAo", [D, TOWN]), ("pBo", [D, TOWN]), ("pCo", [D, TOWN]),
        ("wq", [D, 512]), ("wk", [D, 512]), ("wv", [D, 512]),
        ("bq", [512, 1]), ("bk", [512, 1]), ("bv", [512, 1]),
        ("wo", [512, D]), ("bo", [D, 1]),
        ("mfull", [128, S]), ("mband", [128, 8 * 128]),
        ("wmT", [D, 4096]), ("wpT", [4096, D]),
    ]
    inp = {}
    for name, shape in f32in:
        inp[name] = nc.dram_tensor(name, shape, F32, kind="ExternalInput").ap()
    for name in ("idb", "onesln"):
        inp[name] = nc.dram_tensor(name, [128, 128], BF16,
                                   kind="ExternalInput").ap()
    attn_out = nc.dram_tensor("attn_out", [D, S], F32,
                              kind="ExternalOutput").ap()
    xnew_out = nc.dram_tensor("xnew_out", [D, TOWN], F32,
                              kind="ExternalOutput").ap()
    pool_out = nc.dram_tensor("pool_out", [128, 8], F32,
                              kind="ExternalOutput").ap()

    cx = Ctx(nc)
    with nc.Block() as block:
        cx.block = block
        onesln = cx.sbuf("onesln", (128, 128), BF16)
        idb = cx.sbuf("idb", (128, 128), BF16)

        @block.sync
        def _(sy):
            cx.inc(sy.dma_start(onesln[:], inp["onesln"]), cx.ld, 16)
            sy.wait_ge(cx.ld, cx.val(cx.ld))
            cx.inc(sy.dma_start(idb[:], inp["idb"]), cx.ld, 16)
            sy.wait_ge(cx.ld, cx.val(cx.ld))
        consts = {"idb": idb}

        oTh = cx.sbuf("oTh", (128, 4 * S), BF16)
        xn = cx.sbuf("xn", (128, 8 * S), BF16)
        emit_ln(cx, block, inp["pA"], xn, S, onesln,
                extra=(inp["pB"], inp["pC"]))

        def xn_t(i, c0, w):
            return xn[:, i * S + c0:i * S + c0 + w]

        cx.fence(block)
        emit_banded_mha(cx, block, inp, xn_t, oTh, consts)
        cx.fence(block)
        cx.free_sbuf("xn")
        bo_sb = cx.sbuf("bo_sb", (128, 8), F32)

        @block.sync
        def _(sy):
            for po in range(8):
                cx.inc(sy.dma_start(bo_sb[:, po:po + 1],
                                    inp["bo"][po * 128:(po + 1) * 128, :]),
                       cx.ld, 16)
                sy.wait_ge(cx.ld, cx.val(cx.ld))
        emit_headsum_proj(cx, block, inp["wo"], oTh, attn_out, 8,
                          bias_sb=None if False else bo_sb)
        cx.free_sbuf("oTh", "bo_sb")
        cx.fence(block)

        xno = cx.sbuf("xno", (128, 8 * TOWN), BF16)
        emit_ln(cx, block, inp["pAo"], xno, TOWN, onesln,
                extra=(inp["pBo"], inp["pCo"]))

        def xno_t(i, c0, w):
            return xno[:, i * TOWN + c0:i * TOWN + c0 + w]

        cx.fence(block)
        hb = cx.sbuf("hb", (128, 16 * 1024), BF16)
        xnew = cx.sbuf("xnew", (128, 8 * 1024), F32)
        pool = cx.sbuf("pool", (128, 8), F32)

        for hf in range(2):
            th0 = hf * 1024
            emit_ffn_iter(cx, block, inp, xno_t, 32, None, xnew, pool,
                          hf > 0, th0, hb)
            cx.fence(block)

            @block.sync
            def _(sy):
                cx.wait(sy, cx.vv)
                for po in range(8):
                    cx.inc(sy.dma_start(
                        xnew_out[po * 128:(po + 1) * 128, th0:th0 + 1024],
                        xnew[:, po * 1024:(po + 1) * 1024]), cx.st, 16)
                    sy.wait_ge(cx.st, cx.val(cx.st))

        @block.sync
        def _(sy):
            cx.wait(sy, cx.vv)
            cx.inc(sy.dma_start(pool_out[:], pool[:]), cx.st, 16)
            sy.wait_ge(cx.st, cx.val(cx.st))
            cx.wait(sy, cx.st)

    return nc, cx, inp



def emit_controller(cx, block, inp, pall_dram, maskb, consts, tag):
    onesf, idf = consts["onesf"], consts["idf"]
    pall = cx.sbuf(f"ct_pall{tag}", (128, 64), F32)
    pm = cx.sbuf(f"ct_pm{tag}", (128, 32), F32)
    t1 = cx.sbuf(f"ct_t1{tag}", (128, 8), F32)
    ckT = cx.sbuf(f"ct_ck{tag}", (128, 4 * 96), F32)
    wc1 = cx.sbuf(f"ct_wc1{tag}", (128, 8 * 256), F32)
    wc2 = cx.sbuf(f"ct_wc2{tag}", (128, 2 * 2 * 128), F32)
    sc = cx.sbuf(f"ct_sc{tag}", (1, 16), F32)
    scb = cx.sbuf(f"ct_scb{tag}", (128, 8), F32)
    tmp = cx.sbuf(f"ct_tmp{tag}", (128, 96), F32)
    c41 = cx.sbuf(f"ct_c41{tag}", (128, 4), F32)

    @block.sync
    def _(sy):
        cx.inc(sy.dma_start(pall[:], pall_dram), cx.ld, 16)
        sy.wait_ge(cx.ld, cx.val(cx.ld))
        cx.inc(sy.dma_start(wc1[:].rearrange("p (c o) -> p c o", c=8),
                            inp["wc1T"].rearrange("(c p) o -> p c o", p=128)),
               cx.ld, 16)
        sy.wait_ge(cx.ld, cx.val(cx.ld))

    @block.vector
    def _(v):
        v = cx.ser_v(v)
        cx.wait(v, cx.ld)
        a = pall[:].rearrange("p (b two c) -> p b two c", two=2, c=8)
        v.tensor_tensor(pm[:].rearrange("p (c b) -> p b c", c=8),
                        a[:, :, 0, :], a[:, :, 1, :], op=ALU.add)
        v.tensor_scalar(pm[:], pm[:], 1.0 / 4096.0, None, op0=ALU.mult)

    @block.tensor
    def _(t):
        cx.wait(t, cx.vv, cx.ld)
        for po2 in range(2):
            bi, ps = cx.bank(t)
            for ci in range(8):
                ins = t.matmul(ps[0:128, 0:4],
                               wc1[:, ci * 256 + po2 * 128:
                                   ci * 256 + (po2 + 1) * 128],
                               pm[:, ci * 4:(ci + 1) * 4],
                               start=(ci == 0), stop=(ci == 7))
            cx.inc(ins, cx.mm)

            @block.scalar
            def _(s2):
                s2 = cx.ser_s(s2)
                cx.wait(s2, cx.mm)
                s2.activation(t1[:, po2 * 4:(po2 + 1) * 4], ps[0:128, 0:4],
                              AF.Relu)
            cx.bank_freed(bi, cx.ev)

    wused = {}
    for fo in range(96):
        slot = fo % 2

        @block.sync
        def _(sy):
            if slot in wused:
                sy.wait_ge(cx.mm, wused[slot])
            cx.inc(sy.dma_start(
                wc2[:, slot * 256:(slot + 1) * 256]
                .rearrange("p (c o) -> p c o", c=2),
                inp["wc2T"][:, fo * 128:(fo + 1) * 128]
                .rearrange("(c p) o -> p c o", p=128)), cx.ld, 16)
            sy.wait_ge(cx.ld, cx.val(cx.ld))

        @block.tensor
        def _(t):
            cx.wait(t, cx.ld, cx.ev)
            bi, ps = cx.bank(t)
            for ci in range(2):
                ins = t.matmul(ps[0:128, 0:4],
                               wc2[:, slot * 256 + ci * 128:
                                   slot * 256 + (ci + 1) * 128],
                               t1[:, ci * 4:(ci + 1) * 4],
                               start=(ci == 0), stop=(ci == 1))
            cx.inc(ins, cx.mm)
            wused[slot] = cx.val(cx.mm)

            @block.scalar
            def _(s2):
                s2 = cx.ser_s(s2)
                cx.wait(s2, cx.mm)
                s2.activation(ckT[:, fo:fo + 3 * 96 + 1:96], ps[0:128, 0:4],
                              AF.Copy)
            cx.bank_freed(bi, cx.ev)

    @block.vector
    def _(v):
        v = cx.ser_v(v)
        cx.wait(v, cx.ev)
        for b in range(4):
            v.reduce_max(c41[:, b:b + 1], ckT[:, b * 96:(b + 1) * 96],
                         axis=AX.X)
        for b in range(4):
            v.reduce_max(scb[:, b:b + 1], ckT[:, b * 96:(b + 1) * 96],
                         axis=AX.X, negate=True)

    m4a = cx.sbuf(f"ct_m4a{tag}", (4, 128), F32)
    m4b = cx.sbuf(f"ct_m4b{tag}", (4, 128), F32)
    m41a = cx.sbuf(f"ct_m41a{tag}", (4, 1), F32)
    m41b = cx.sbuf(f"ct_m41b{tag}", (4, 1), F32)

    @block.tensor
    def _(t):
        cx.wait(t, cx.vv)
        bi, ps = cx.bank(t)
        ins = t.transpose(ps[0:4, 0:128], c41[:], idf[:])
        cx.inc(ins, cx.mm)
        bj, ps2 = cx.bank(t)
        ins = t.transpose(ps2[0:4, 0:128], scb[:, 0:4], idf[:])
        cx.inc(ins, cx.mm)

        @block.vector
        def _(v):
            v = cx.ser_v(v)
            cx.wait(v, cx.mm)
            v.tensor_copy(m4a[:], ps[0:4, 0:128])
            v.tensor_copy(m4b[:], ps2[0:4, 0:128])
            v.reduce_max(m41a[:], m4a[:], axis=AX.X)
            v.reduce_max(m41b[:], m4b[:], axis=AX.X)
        cx.bank_freed(bi, cx.vv)
        cx.bank_freed(bj, cx.vv)
        t.wait_ge(cx.vv, cx.val(cx.vv))
        bk, ps3 = cx.bank(t)
        ins = t.transpose(ps3[0:1, 0:4], m41a[:], idf[0:4, 0:4])
        cx.inc(ins, cx.mm)
        bl, ps4 = cx.bank(t)
        ins = t.transpose(ps4[0:1, 0:4], m41b[:], idf[0:4, 0:4])
        cx.inc(ins, cx.mm)

        @block.vector
        def _(v):
            v = cx.ser_v(v)
            cx.wait(v, cx.mm)
            v.tensor_copy(sc[0:1, 4:8], ps3[0:1, 0:4])
            v.tensor_copy(sc[0:1, 0:4], ps4[0:1, 0:4])
            v.tensor_scalar(sc[0:1, 0:4], sc[0:1, 0:4], -1.0, -1.0,
                            op0=ALU.mult, op1=ALU.add)
            v.tensor_scalar(sc[0:1, 4:8], sc[0:1, 4:8], 1.0, None, op0=ALU.add)
        cx.bank_freed(bk, cx.vv)
        cx.bank_freed(bl, cx.vv)

    for it in range(40):
        @block.vector
        def _(v):
            v = cx.ser_v(v)
            cx.wait(v, cx.ev)
            v.tensor_tensor(sc[0:1, 8:12], sc[0:1, 0:4], sc[0:1, 4:8],
                            op=ALU.add)
            v.tensor_scalar(sc[0:1, 8:12], sc[0:1, 8:12], 0.5, None,
                            op0=ALU.mult)

        @block.tensor
        def _(t):
            cx.wait(t, cx.vv)
            bi, ps = cx.bank(t)
            ins = t.matmul(ps[0:128, 0:4], onesf[0:1, :], sc[0:1, 8:12],
                           start=True, stop=True)
            cx.inc(ins, cx.mm)

            @block.scalar
            def _(s2):
                s2 = cx.ser_s(s2)
                cx.wait(s2, cx.mm)
                s2.activation(scb[:, 0:4], ps[0:128, 0:4], AF.Copy)
            cx.bank_freed(bi, cx.ev)

        @block.vector
        def _(v):
            v = cx.ser_v(v)
            cx.wait(v, cx.ev)
            for b in range(4):
                v.tensor_scalar(tmp[:], ckT[:, b * 96:(b + 1) * 96],
                                scb[:, b:b + 1], None, op0=ALU.is_gt)
                v.reduce_sum(c41[:, b:b + 1], tmp[:], axis=AX.X)

        @block.tensor
        def _(t):
            cx.wait(t, cx.vv)
            bi, ps = cx.bank(t)
            ins = t.matmul(ps[0:1, 0:4], onesf[:, 0:1], c41[:],
                           start=True, stop=True)
            cx.inc(ins, cx.mm)

            @block.scalar
            def _(s2):
                s2 = cx.ser_s(s2)
                cx.wait(s2, cx.mm)
                s2.activation(sc[0:1, 12:16], ps[0:1, 0:4], AF.Copy)
            cx.bank_freed(bi, cx.ev)

        @block.vector
        def _(v):
            v = cx.ser_v(v)
            cx.wait(v, cx.ev)
            v.tensor_scalar(sc[0:1, 12:16], sc[0:1, 12:16], 4096.0, None,
                            op0=ALU.is_ge)
            v.tensor_tensor(c41[0:1, 0:4], sc[0:1, 8:12], sc[0:1, 0:4],
                            op=ALU.subtract)
            v.tensor_tensor(c41[0:1, 0:4], c41[0:1, 0:4], sc[0:1, 12:16],
                            op=ALU.mult)
            v.tensor_tensor(sc[0:1, 0:4], sc[0:1, 0:4], c41[0:1, 0:4],
                            op=ALU.add)
            v.tensor_scalar(sc[0:1, 12:16], sc[0:1, 12:16], -1.0, 1.0,
                            op0=ALU.mult, op1=ALU.add)
            v.tensor_tensor(c41[0:1, 0:4], sc[0:1, 8:12], sc[0:1, 4:8],
                            op=ALU.subtract)
            v.tensor_tensor(c41[0:1, 0:4], c41[0:1, 0:4], sc[0:1, 12:16],
                            op=ALU.mult)
            v.tensor_tensor(sc[0:1, 4:8], sc[0:1, 4:8], c41[0:1, 0:4],
                            op=ALU.add)

    @block.tensor
    def _(t):
        cx.wait(t, cx.vv)
        bi, ps = cx.bank(t)
        ins = t.matmul(ps[0:128, 0:4], onesf[0:1, :], sc[0:1, 0:4],
                       start=True, stop=True)
        cx.inc(ins, cx.mm)

        @block.scalar
        def _(s2):
            s2 = cx.ser_s(s2)
            cx.wait(s2, cx.mm)
            s2.activation(scb[:, 0:4], ps[0:128, 0:4], AF.Copy)
        cx.bank_freed(bi, cx.ev)

    @block.vector
    def _(v):
        v = cx.ser_v(v)
        cx.wait(v, cx.ev)
        for b in range(4):
            v.tensor_scalar(maskb[:, b * 96:(b + 1) * 96],
                            ckT[:, b * 96:(b + 1) * 96],
                            scb[:, b:b + 1], None, op0=ALU.is_gt)

    cx.free_sbuf(f"ct_pall{tag}", f"ct_pm{tag}", f"ct_t1{tag}", f"ct_ck{tag}",
                 f"ct_wc1{tag}", f"ct_wc2{tag}", f"ct_sc{tag}", f"ct_scb{tag}",
                 f"ct_tmp{tag}", f"ct_c41{tag}", f"ct_m4a{tag}", f"ct_m4b{tag}",
                 f"ct_m41a{tag}", f"ct_m41b{tag}")


def emit_novelty_hi(cx, block, maskb, prevhi, out_sc, consts):
    onesf = consts["onesf"]
    nvp = cx.sbuf("nv_nvp", (128, 4), F32)
    t96 = cx.sbuf("nv_t", (128, 64), F32)

    @block.vector
    def _(v):
        v = cx.ser_v(v)
        cx.wait(v, cx.vv, cx.ev)
        for b in range(4):
            hi = maskb[:, b * 96 + 32:(b + 1) * 96]
            if prevhi is None:
                v.reduce_sum(nvp[:, b:b + 1], hi, axis=AX.X)
            else:
                v.tensor_scalar(t96[:], prevhi(b), -1.0, 1.0,
                                op0=ALU.mult, op1=ALU.add)
                v.tensor_tensor(t96[:], t96[:], hi, op=ALU.mult)
                v.reduce_sum(nvp[:, b:b + 1], t96[:], axis=AX.X)

    @block.tensor
    def _(t):
        cx.wait(t, cx.vv)
        bi, ps = cx.bank(t)
        ins = t.matmul(ps[0:1, 0:4], onesf[:, 0:1], nvp[:],
                       start=True, stop=True)
        cx.inc(ins, cx.mm)

        @block.scalar
        def _(s2):
            s2 = cx.ser_s(s2)
            cx.wait(s2, cx.mm)
            s2.activation(nvp[0:1, 0:4], ps[0:1, 0:4], AF.Copy)
        cx.bank_freed(bi, cx.ev)

    @block.vector
    def _(v):
        v = cx.ser_v(v)
        cx.wait(v, cx.ev)
        v.reduce_sum(out_sc, nvp[0:1, 0:4], axis=AX.X)
        v.tensor_scalar(out_sc, out_sc, 1.0 / 16384.0, None, op0=ALU.mult)

    cx.free_sbuf("nv_nvp", "nv_t")


def emit_bsel_combine(cx, block, maskb, bselb, out96):
    @block.vector
    def _(v):
        v = cx.ser_v(v)
        cx.wait(v, cx.vv, cx.ev)
        v.tensor_scalar(out96[:], maskb[:, 0:96], bselb[:, 0:1], None,
                        op0=ALU.mult)
        for b in range(1, 4):
            v.scalar_tensor_tensor(out96[:], maskb[:, b * 96:(b + 1) * 96],
                                   bselb[:, b:b + 1], out96[:],
                                   op0=ALU.mult, op1=ALU.add)


def emit_bcast_cols(cx, block, src_row, dst, n, consts):
    onesf = consts["onesf"]

    @block.tensor
    def _(t):
        cx.wait(t, cx.vv, cx.ev)
        bi, ps = cx.bank(t)
        ins = t.matmul(ps[0:128, 0:n], onesf[0:1, :], src_row,
                       start=True, stop=True)
        cx.inc(ins, cx.mm)

        @block.scalar
        def _(s2):
            s2 = cx.ser_s(s2)
            cx.wait(s2, cx.mm)
            s2.activation(dst, ps[0:128, 0:n], AF.Copy)
        cx.bank_freed(bi, cx.ev)


def build_L2bc(last):
    nc = bass.Bass("TRN2", target_bir_lowering=False)
    f32in = [
        ("xc", [D, TOWN]),
        ("pall1", [128, 64]),
        ("pall2", [128, 64]),
        ("wc1T", [D, 256]), ("wc2T", [256, 12288]),
        ("wmT", [D, 12288]), ("wpT", [12288, D]),
        ("bsel", [1, 4]),
        ("idf", [128, 128]), ("onesf", [128, 128]),
        ("pAo", [D, TOWN]), ("pBo", [D, TOWN]), ("pCo", [D, TOWN]),
    ]
    inp = {}
    for name, shape in f32in:
        inp[name] = nc.dram_tensor(name, shape, F32, kind="ExternalInput").ap()
    inp["onesln"] = nc.dram_tensor("onesln", [128, 128], BF16,
                                   kind="ExternalInput").ap()
    xout = nc.dram_tensor("xout", [D, TOWN], F32, kind="ExternalOutput").ap()
    pool_out = nc.dram_tensor("pool_out", [128, 8], F32,
                              kind="ExternalOutput").ap()

    cx = Ctx(nc)
    with nc.Block() as block:
        cx.block = block
        onesln = cx.sbuf("onesln", (128, 128), BF16)
        idf = cx.sbuf("idf", (128, 128), F32)
        onesf = cx.sbuf("onesf", (128, 128), F32)

        @block.sync
        def _(sy):
            cx.inc(sy.dma_start(onesln[:], inp["onesln"]), cx.ld, 16)
            sy.wait_ge(cx.ld, cx.val(cx.ld))
            cx.inc(sy.dma_start(idf[:], inp["idf"]), cx.ld, 16)
            sy.wait_ge(cx.ld, cx.val(cx.ld))
            cx.inc(sy.dma_start(onesf[:], inp["onesf"]), cx.ld, 16)
            sy.wait_ge(cx.ld, cx.val(cx.ld))
        consts = {"onesln": onesln, "idf": idf, "onesf": onesf}

        maskb1 = cx.sbuf("maskb1", (128, 4 * 96), F32)
        maskb2 = cx.sbuf("maskb2", (128, 4 * 96), F32)
        scal = cx.sbuf("scal", (1, 8), F32)
        dcol = cx.sbuf("dcol", (128, 2), F32)
        bselb = cx.sbuf("bselb", (128, 4), F32)
        myact = cx.sbuf("myact", (128, 96), F32)
        bsel_row = cx.sbuf("bsel_row", (1, 4), F32)

        @block.sync
        def _(sy):
            cx.inc(sy.dma_start(bsel_row[:], inp["bsel"]), cx.ld, 16)
            sy.wait_ge(cx.ld, cx.val(cx.ld))

        emit_controller(cx, block, inp, inp["pall1"][:, :], maskb1, consts, "1")
        emit_novelty_hi(cx, block, maskb1, None, scal[0:1, 0:1], consts)

        @block.vector
        def _(v):
            v = cx.ser_v(v)
            cx.wait(v, cx.vv)
            v.tensor_scalar(scal[0:1, 1:2], scal[0:1, 0:1], 0.7, None,
                            op0=ALU.is_lt)

        emit_bcast_cols(cx, block, bsel_row[:], bselb[:], 4, consts)

        if not last:
            emit_bsel_combine(cx, block, maskb1, bselb, myact)
            @block.vector
            def _(v):
                v = cx.ser_v(v)
                cx.wait(v, cx.vv, cx.ev)
                v.tensor_copy(scal[0:1, 4:5], scal[0:1, 1:2])
                v.tensor_scalar(scal[0:1, 5:6], scal[0:1, 1:2], -1.0, 1.0,
                                op0=ALU.mult, op1=ALU.add)
        else:
            emit_controller(cx, block, inp, inp["pall2"][:, :], maskb2,
                            consts, "2")
            emit_novelty_hi(cx, block, maskb2,
                            lambda b: maskb1[:, b * 96 + 32:(b + 1) * 96],
                            scal[0:1, 2:3], consts)

            @block.vector
            def _(v):
                v = cx.ser_v(v)
                cx.wait(v, cx.vv)
                v.tensor_scalar(scal[0:1, 3:4], scal[0:1, 2:3], 0.7, None,
                                op0=ALU.is_lt)
                v.tensor_tensor(scal[0:1, 3:4], scal[0:1, 3:4], scal[0:1, 1:2],
                                op=ALU.max)
                v.tensor_copy(scal[0:1, 4:5], scal[0:1, 3:4])
                v.tensor_scalar(scal[0:1, 5:6], scal[0:1, 3:4], -1.0, 1.0,
                                op0=ALU.mult, op1=ALU.add)

            m1my = cx.sbuf("m1my", (128, 96), F32)
            m2my = cx.sbuf("m2my", (128, 96), F32)
            emit_bsel_combine(cx, block, maskb1, bselb, m1my)
            emit_bsel_combine(cx, block, maskb2, bselb, m2my)
            d1b = cx.sbuf("d1b", (128, 2), F32)

            @block.vector
            def _(v):
                v = cx.ser_v(v)
                cx.wait(v, cx.vv)
                v.tensor_copy(scal[0:1, 6:7], scal[0:1, 1:2])
                v.tensor_scalar(scal[0:1, 7:8], scal[0:1, 1:2], -1.0, 1.0,
                                op0=ALU.mult, op1=ALU.add)
            emit_bcast_cols(cx, block, scal[0:1, 6:8], d1b[:], 2, consts)

            @block.vector
            def _(v):
                v = cx.ser_v(v)
                cx.wait(v, cx.vv, cx.ev)
                v.tensor_scalar(myact[:], m1my[:], d1b[:, 0:1], None,
                                op0=ALU.mult)
                v.scalar_tensor_tensor(myact[:], m2my[:], d1b[:, 1:2],
                                       myact[:], op0=ALU.mult, op1=ALU.add)

        emit_bcast_cols(cx, block, scal[0:1, 4:6], dcol[:], 2, consts)
        cx.fence(block)

        xcb = cx.sbuf("xcb", (128, 8 * TOWN), BF16)

        @block.gpsimd
        def _(g):
            g = cx.ser_g(g)
            for i in range(8):
                cx.inc(g.dma_start(xcb[:, i * TOWN:(i + 1) * TOWN],
                                   inp["xc"][i * 128:(i + 1) * 128, :]),
                       cx.lg, 16)
                g.wait_ge(cx.lg, cx.val(cx.lg))

        def xcb_t(i, c0, w):
            return xcb[:, i * TOWN + c0:i * TOWN + c0 + w]

        hb = cx.sbuf("hb", (128, 16 * 1024), BF16)
        xnew = cx.sbuf("xnew", (128, 8 * 1024), F32)
        pool = cx.sbuf("pool", (128, 8), F32)
        xct = cx.sbuf("xct", (128, 1024), F32)

        for hf in range(2):
            th0 = hf * 1024
            emit_ffn_iter(cx, block, inp, xcb_t, 96,
                          lambda fo: myact[:, fo:fo + 1], xnew, pool,
                          hf > 0, th0, hb)
            cx.fence(block)

            for po in range(8):
                @block.sync
                def _(sy):
                    cx.wait(sy, cx.vv, cx.st)
                    cx.inc(sy.dma_start(
                        xct[:], inp["xc"][po * 128:(po + 1) * 128,
                                          th0:th0 + 1024]), cx.ld, 16)
                    sy.wait_ge(cx.ld, cx.val(cx.ld))

                @block.vector
                def _(v):
                    v = cx.ser_v(v)
                    cx.wait(v, cx.ld, cx.ev)
                    v.tensor_scalar(xct[:], xct[:], dcol[:, 0:1], None,
                                    op0=ALU.mult)
                    v.scalar_tensor_tensor(
                        xct[:], xnew[:, po * 1024:(po + 1) * 1024],
                        dcol[:, 1:2], xct[:], op0=ALU.mult, op1=ALU.add)
                    if last:
                        pass

                if last:
                    for src in ("pAo", "pBo", "pCo"):
                        xrt = cx.sbuf(f"xrt{po}{hf}{src}", (128, 1024), F32)

                        @block.sync
                        def _(sy):
                            cx.inc(sy.dma_start(
                                xrt[:], inp[src][po * 128:(po + 1) * 128,
                                                 th0:th0 + 1024]), cx.ld, 16)
                            sy.wait_ge(cx.ld, cx.val(cx.ld))

                        @block.vector
                        def _(v):
                            v = cx.ser_v(v)
                            cx.wait(v, cx.ld)
                            v.tensor_tensor(xct[:], xct[:], xrt[:], op=ALU.add)
                        cx.free_sbuf(f"xrt{po}{hf}{src}")

                @block.sync
                def _(sy):
                    cx.wait(sy, cx.vv)
                    cx.inc(sy.dma_start(
                        xout[po * 128:(po + 1) * 128, th0:th0 + 1024],
                        xct[:]), cx.st, 16)
                    sy.wait_ge(cx.st, cx.val(cx.st))

        @block.sync
        def _(sy):
            cx.wait(sy, cx.vv)
            cx.inc(sy.dma_start(pool_out[:], pool[:]), cx.st, 16)
            sy.wait_ge(cx.st, cx.val(cx.st))
            cx.wait(sy, cx.st)

    return nc, cx, inp



def l2_masks():
    mfull = np.zeros((128, S), np.float32)
    i = np.arange(128)[:, None]
    j = np.arange(S)[None, :]
    ok = (np.abs(i - j) <= 256) | (i == 0) | (j == 0)
    mfull[~ok] = NEG
    mband = np.zeros((128, 8 * 128), np.float32)
    for mi in range(8):
        blk = mband[:, mi * 128:(mi + 1) * 128]
        jj = np.arange(128)[None, :]
        r = np.arange(128)[:, None]
        if mi == 5:
            ok = (jj == 0) | (np.zeros_like(r, dtype=bool))
            ok = np.broadcast_to(jj == 0, (128, 128))
        elif mi in (6, 0):
            d = -2
            ok = np.abs(d * 128 + jj - r) <= 256
            if mi == 6:
                ok = ok | (jj == 0)
        elif mi in (7, 1):
            d = -1
            ok = np.abs(d * 128 + jj - r) <= 256
            if mi == 7:
                ok = ok | (jj == 0)
        else:
            d = mi - 2
            ok = np.abs(d * 128 + jj - r) <= 256
        blk[~ok] = NEG
    return mfull, mband


def host_prep_L2a(I, pA, pB, pC):
    mfull, mband = l2_masks()
    onesln = np.full((128, 128), 1.0 / 1024.0, np.float32).astype(bf16)
    idb = np.eye(128, dtype=np.float32).astype(bf16)
    WmT = f32c(np.asarray(I["L2_ffn_Wm"]).T)
    WpT = f32c(np.asarray(I["L2_ffn_Wp"]).T)
    ws = {}
    for n in ("q", "k", "v", "o"):
        ws["W" + n] = np.asarray(I["L2_attn_W" + n], np.float32)
        ws["b" + n] = np.asarray(I["L2_attn_b" + n], np.float32)
    maps = []
    for c in range(8):
        b, p = divmod(c, 2)
        hs = slice(p * 512, (p + 1) * 512)
        own = slice(p * TOWN, (p + 1) * TOWN)
        m = {
            "pA": pA[b], "pB": pB[b], "pC": pC[b],
            "pAo": f32c(pA[b][:, own]), "pBo": f32c(pB[b][:, own]),
            "pCo": f32c(pC[b][:, own]),
            "wq": f32c(ws["Wq"][:, hs]), "wk": f32c(ws["Wk"][:, hs]),
            "wv": f32c(ws["Wv"][:, hs]),
            "bq": f32c(ws["bq"][hs][:, None]),
            "bk": f32c(ws["bk"][hs][:, None]),
            "bv": f32c(ws["bv"][hs][:, None]),
            "wo": f32c(ws["Wo"][hs, :]),
            "bo": f32c((ws["bo"] if p == 0 else
                        np.zeros_like(ws["bo"]))[:, None]),
            "mfull": mfull, "mband": mband,
            "wmT": f32c(WmT[:, 0:4096]), "wpT": f32c(WpT[0:4096, :]),
            "idb": idb, "onesln": onesln,
        }
        maps.append(m)
    return maps, WmT, WpT


def host_prep_L2bc(I, xc, pall1, pall2, pA, pB, pC, WmT, WpT):
    onesln = np.full((128, 128), 1.0 / 1024.0, np.float32).astype(bf16)
    idf = np.eye(128, dtype=np.float32)
    onesf = np.ones((128, 128), np.float32)
    wc1T = f32c(np.asarray(I["L2_ffn_Wc1"]).T)
    wc2T = f32c(np.asarray(I["L2_ffn_Wc2"]).T)
    maps = []
    for c in range(8):
        b, p = divmod(c, 2)
        own = slice(p * TOWN, (p + 1) * TOWN)
        bsel = np.zeros((1, 4), np.float32)
        bsel[0, b] = 1.0
        m = {
            "xc": xc[c], "pall1": pall1, "pall2": pall2,
            "wc1T": wc1T, "wc2T": wc2T,
            "wmT": WmT, "wpT": WpT, "bsel": bsel,
            "idf": idf, "onesf": onesf, "onesln": onesln,
            "pAo": f32c(pA[b][:, own]), "pBo": f32c(pB[b][:, own]),
            "pCo": f32c(pC[b][:, own]),
        }
        maps.append(m)
    return maps


def run_L2(I, resB):
    pA = [resB[2 * b]["attn_out"] for b in range(4)]
    pB = [resB[2 * b + 1]["attn_out"] for b in range(4)]
    pC = [np.concatenate([resB[2 * b]["xffn_out"],
                          resB[2 * b + 1]["xffn_out"]], axis=1)
          for b in range(4)]
    mapsA, WmT, WpT = host_prep_L2a(I, pA, pB, pC)
    nc, cx, _ = build_L2a()
    resA = run_bass_kernel_spmd(nc, mapsA, core_ids=list(range(8))).results
    cx.close()

    pall1 = f32c(np.concatenate([resA[c]["pool_out"] for c in range(8)], 1))
    xc = [resA[c]["xnew_out"] for c in range(8)]
    maps2 = host_prep_L2bc(I, xc, pall1, pall1, pA, pB, pC, WmT, WpT)
    nc2, cx2, _ = build_L2bc(False)
    res2 = run_bass_kernel_spmd(nc2, maps2, core_ids=list(range(8))).results
    cx2.close()

    pall2 = f32c(np.concatenate([res2[c]["pool_out"] for c in range(8)], 1))
    xc2 = [res2[c]["xout"] for c in range(8)]
    maps3 = host_prep_L2bc(I, xc2, pall1, pall2, pA, pB, pC, WmT, WpT)
    nc3, cx3, _ = build_L2bc(True)
    res3 = run_bass_kernel_spmd(nc3, maps3, core_ids=list(range(8))).results
    cx3.close()
    return resA, res3




def build_L3():
    nc = bass.Bass("TRN2", target_bir_lowering=False)
    DI = 2048
    f32in = [
        ("pA", [D, S]), ("pB", [D, S]), ("pC", [D, S]),
        ("in_w", [D, 2 * DI]),
        ("convd", [128, 16 * 4 * 128]), ("convb", [128, 16]),
        ("xp_w", [DI, 96]), ("dt_w", [64, DI]), ("dt_b", [128, 16]),
        ("alog", [DI, 16]), ("dskip", [128, 16]),
        ("out_wT", [D, DI]), ("proj", [D, D]),
        ("wr", [D, 16]), ("br", [16, 1]),
        ("wqh", [D, D]), ("wkh", [D, D]), ("wvh", [D, D]),
        ("bqh", [D, 1]), ("bkh", [D, 1]), ("bvh", [D, 1]),
        ("wo", [128, D]),
        ("bias2", [2, D]),
        ("iota16", [1, 16]), ("iota64", [64, 1]), ("ksel2", [1, 8]),
        ("idf", [128, 128]), ("onesf", [128, 128]),
    ]
    inp = {}
    for name, shape in f32in:
        inp[name] = nc.dram_tensor(name, shape, F32, kind="ExternalInput").ap()
    for name in ("idb", "onesln"):
        inp[name] = nc.dram_tensor(name, [128, 128], BF16,
                                   kind="ExternalInput").ap()
    part_out = nc.dram_tensor("part_out", [D, S], F32,
                              kind="ExternalOutput").ap()
    xc_hb = nc.dram_tensor("xc_hb", [DI, S], BF16).ap()
    z_hb = nc.dram_tensor("z_hb", [DI, S], BF16).ap()
    dt_hb = nc.dram_tensor("dt_hb", [DI, S], F32).ap()
    y_hb = nc.dram_tensor("y_hb", [DI, S], BF16).ap()
    o2_hb = nc.dram_tensor("o2_hb", [128, S], BF16).ap()

    cx = Ctx(nc)
    with nc.Block() as block:
        cx.block = block
        onesln = cx.sbuf("onesln", (128, 128), BF16)
        idb = cx.sbuf("idb", (128, 128), BF16)
        idf = cx.sbuf("idf", (128, 128), F32)
        onesf = cx.sbuf("onesf", (128, 128), F32)
        wrf = cx.sbuf("wrf", (128, 8 * 16), F32)
        racc = cx.sbuf("racc", (16, 1), F32)
        rch = cx.sbuf("rch", (16, 1), F32)

        @block.sync
        def _(sy):
            for dst, src in ((onesln, inp["onesln"]), (idb, inp["idb"]),
                             (idf, inp["idf"]), (onesf, inp["onesf"])):
                cx.inc(sy.dma_start(dst[:], src), cx.ld, 16)
                sy.wait_ge(cx.ld, cx.val(cx.ld))
            cx.inc(sy.dma_start(
                wrf[:].rearrange("p (c o) -> p c o", c=8),
                inp["wr"].rearrange("(c p) o -> p c o", p=128)), cx.ld, 16)
            sy.wait_ge(cx.ld, cx.val(cx.ld))

        @block.vector
        def _(v):
            v = cx.ser_v(v)
            v.memset(racc[:], 0.0)

        dbc = cx.sbuf("dbc", (96, S), F32)
        asb = cx.sbuf("asb", (128, 16 * 16), F32)

        xn = cx.sbuf("xn", (128, 8 * S), BF16)

        def wr_tap(c0, xnf):
            @block.tensor
            def _(t):
                cx.wait(t, cx.ld, cx.vv)
                bi, ps = cx.bank(t)
                for ci in range(8):
                    ins = t.matmul(ps[0:16, :], wrf[:, ci * 16:(ci + 1) * 16],
                                   xnf[:, ci * 512:(ci + 1) * 512],
                                   start=(ci == 0), stop=(ci == 7))
                cx.inc(ins, cx.mm)

                @block.scalar
                def _(s2):
                    s2 = cx.ser_s(s2)
                    cx.wait(s2, cx.mm)
                    s2.activation(xnf[0:16, 0:512], ps[0:16, 0:512], AF.Copy,
                                  accum_out=rch[:])
                cx.bank_freed(bi, cx.ev)

            @block.vector
            def _(v):
                v = cx.ser_v(v)
                cx.wait(v, cx.ev)
                v.tensor_tensor(racc[:], racc[:], rch[:], op=ALU.add)

        emit_ln(cx, block, inp["pA"], xn, S, onesln,
                extra=(inp["pB"], inp["pC"]), tap=wr_tap)

        def xn_t(i, c0, w):
            return xn[:, i * S + c0:i * S + c0 + w]

        cx.fence(block)

        g16 = cx.sbuf("g16", (1, 16 * 8), F32)
        msks = cx.sbuf("msks", (1, 64), F32)
        sc4 = cx.sbuf("sc4", (1, 16), F32)
        m11 = cx.sbuf("m11", (1, 1), F32)
        it16 = cx.sbuf("it16", (1, 16), F32)
        ks2 = cx.sbuf("ks2", (1, 8), F32)
        br16 = cx.sbuf("br16", (16, 1), F32)

        @block.sync
        def _(sy):
            cx.inc(sy.dma_start(it16[:], inp["iota16"]), cx.ld, 16)
            sy.wait_ge(cx.ld, cx.val(cx.ld))
            cx.inc(sy.dma_start(ks2[:], inp["ksel2"]), cx.ld, 16)
            sy.wait_ge(cx.ld, cx.val(cx.ld))
            cx.inc(sy.dma_start(br16[:], inp["br"]), cx.ld, 16)
            sy.wait_ge(cx.ld, cx.val(cx.ld))

        @block.vector
        def _(v):
            v = cx.ser_v(v)
            cx.wait(v, cx.ld, cx.ev)
            v.tensor_scalar(racc[:], racc[:], 1.0 / 4096.0, None, op0=ALU.mult)
            v.tensor_tensor(racc[:], racc[:], br16[:], op=ALU.add)

        @block.tensor
        def _(t):
            cx.wait(t, cx.vv)
            bi, ps = cx.bank(t)
            ins = t.transpose(ps[0:1, 0:16], racc[:], idf[0:16, 0:16])
            cx.inc(ins, cx.mm)

            @block.scalar
            def _(s2):
                s2 = cx.ser_s(s2)
                cx.wait(s2, cx.mm)
                s2.activation(g16[0:1, 0:16], ps[0:1, 0:16], AF.Copy)
            cx.bank_freed(bi, cx.ev)

        def softmax16(src, dst):
            @block.vector
            def _(v):
                v = cx.ser_v(v)
                cx.wait(v, cx.ev)
                v.reduce_max(m11[:], src, axis=AX.X, negate=True)

            @block.scalar
            def _(s2):
                s2 = cx.ser_s(s2)
                cx.wait(s2, cx.vv)
                s2.activation(dst, src, AF.Exp, bias=m11[0:1, 0:1],
                              accum_out=m11[:])

            @block.vector
            def _(v):
                v = cx.ser_v(v)
                cx.wait(v, cx.ev)
                v.reciprocal(m11[:], m11[:])
                v.tensor_scalar(dst, dst, m11[0:1, 0:1], None, op0=ALU.mult)

        softmax16(g16[0:1, 0:16], g16[0:1, 16:32])

        @block.vector
        def _(v):
            v = cx.ser_v(v)
            cx.wait(v, cx.ev)
            v.tensor_copy(g16[0:1, 48:64], g16[0:1, 16:32])
            v.memset(g16[0:1, 32:48], 0.0)
            for k in range(4):
                v.reduce_max(m11[:], g16[0:1, 48:64], axis=AX.X)
                v.tensor_scalar(msks[0:1, k * 16:(k + 1) * 16],
                                g16[0:1, 48:64], m11[0:1, 0:1], None,
                                op0=ALU.is_equal)
                mk = msks[0:1, k * 16:(k + 1) * 16]
                v.tensor_tensor(sc4[0:1, k:k + 1],
                                mk, it16[:], op=ALU.mult) if False else None
                v.tensor_tensor(g16[0:1, 0:16], mk, it16[:], op=ALU.mult)
                v.reduce_sum(sc4[0:1, k:k + 1], g16[0:1, 0:16], axis=AX.X)
                v.scalar_tensor_tensor(g16[0:1, 32:48], mk, m11[0:1, 0:1],
                                       g16[0:1, 32:48], op0=ALU.mult,
                                       op1=ALU.add)
                v.scalar_tensor_tensor(g16[0:1, 48:64], mk, NEG,
                                       g16[0:1, 48:64], op0=ALU.mult,
                                       op1=ALU.add)

        softmax16(g16[0:1, 32:48], g16[0:1, 64:80])

        @block.vector
        def _(v):
            v = cx.ser_v(v)
            cx.wait(v, cx.ev)
            for k in range(4):
                v.tensor_tensor(g16[0:1, 0:16], msks[0:1, k * 16:(k + 1) * 16],
                                g16[0:1, 64:80], op=ALU.mult)
                v.reduce_sum(sc4[0:1, 4 + k:5 + k], g16[0:1, 0:16], axis=AX.X)
            for sl in range(2):
                v.tensor_tensor(g16[0:1, 0:4], ks2[0:1, sl * 4:(sl + 1) * 4],
                                sc4[0:1, 0:4], op=ALU.mult)
                v.reduce_sum(sc4[0:1, 8 + sl:9 + sl], g16[0:1, 0:4], axis=AX.X)
                v.tensor_tensor(g16[0:1, 0:4], ks2[0:1, sl * 4:(sl + 1) * 4],
                                sc4[0:1, 4:8], op=ALU.mult)
                v.reduce_sum(sc4[0:1, 10 + sl:11 + sl], g16[0:1, 0:4],
                             axis=AX.X)
                v.tensor_tensor(sc4[0:1, 12 + sl:13 + sl],
                                sc4[0:1, 10 + sl:11 + sl],
                                sc4[0:1, 10 + sl:11 + sl], op=ALU.mult)
                v.tensor_scalar(sc4[0:1, 12 + sl:13 + sl],
                                sc4[0:1, 12 + sl:13 + sl], 0.125, None,
                                op0=ALU.mult)

        oT2 = cx.sbuf("oT2", (128, S), BF16)
        wsel = cx.sbuf("wsel", (128, 3 * 8 * 64), BF16)
        gath = cx.sbuf("gath", (64, D), BF16)
        bsel = cx.sbuf("bsel", (64, 4), F32)
        brow = cx.sbuf("brow", (1, 64), F32)
        idxu = cx.sbuf("idxu", (64, 1), U32)
        idxf = cx.sbuf("idxf", (64, 1), F32)
        i64 = cx.sbuf("i64", (64, 1), F32)
        gcol = cx.sbuf("gcol", (128, 4), F32)
        qT = cx.sbuf("qT3", (64, S), BF16)
        kT = cx.sbuf("kT3", (64, S), BF16)
        vTM = cx.sbuf("vTM3", (128, 32 * 64), BF16)
        vTs = cx.sbuf("vTs3", (64, 2 * 512), BF16)
        strip = cx.sbuf("strip3", (128, S), F32)
        pstrip = cx.sbuf("pstrip3", (128, S), BF16)
        ptbig = cx.sbuf("ptbig3", (128, 16 * 128), BF16)
        rsum = cx.sbuf("rsum3", (128, 1), F32)
        nmax = cx.sbuf("nmax3", (128, 1), F32)

        @block.sync
        def _(sy):
            cx.inc(sy.dma_start(i64[:], inp["iota64"]), cx.ld, 16)
            sy.wait_ge(cx.ld, cx.val(cx.ld))

        emit_bcast_cols(cx, block, sc4[0:1, 12:14], gcol[:, 0:2], 2,
                        {"onesf": onesf})
        emit_bcast_cols(cx, block, sc4[0:1, 10:12], gcol[:, 2:4], 2,
                        {"onesf": onesf})

        for sl in range(2):
            @block.tensor
            def _(t):
                cx.wait(t, cx.vv, cx.ev)
                bi, ps = cx.bank(t)
                ins = t.matmul(ps[0:64, 0:1], onesf[0:1, 0:64],
                               sc4[0:1, 8 + sl:9 + sl], start=True, stop=True)
                cx.inc(ins, cx.mm)

                @block.vector
                def _(v):
                    v = cx.ser_v(v)
                    cx.wait(v, cx.mm, cx.ld)
                    v.tensor_scalar(idxf[:], ps[0:64, 0:1], 64.0, None,
                                    op0=ALU.mult)
                    v.tensor_tensor(idxf[:], idxf[:], i64[:], op=ALU.add)
                    v.tensor_copy(idxu[:], idxf[:])
                cx.bank_freed(bi, cx.vv)

            for j, wn in enumerate(("wqh", "wkh", "wvh")):
                @block.gpsimd
                def _(g):
                    g = cx.ser_g(g)
                    cx.wait(g, cx.vv)
                    cx.inc(g.indirect_dma_start(
                        out=gath[:, :], out_offset=None, in_=inp[wn],
                        in_offset=bass.IndirectOffsetOnAxis(
                            ap=idxu[:, 0:1], axis=0)), cx.lg, 16)
                    g.wait_ge(cx.lg, cx.val(cx.lg))

                @block.tensor
                def _(t):
                    cx.wait(t, cx.lg)
                    for ci in range(8):
                        bb, psb = cx.bankb(t)
                        ins = t.transpose(
                            psb[:, 0:64],
                            gath[:, ci * 128:(ci + 1) * 128],
                            idb[0:64, 0:64])
                        cx.inc(ins, cx.mm)

                        @block.scalar
                        def _(s2):
                            s2 = cx.ser_s(s2)
                            cx.wait(s2, cx.mm)
                            s2.activation(
                                wsel[:, j * 512 + ci * 64:
                                     j * 512 + (ci + 1) * 64],
                                psb[:, 0:64], AF.Copy)
                        cx.bankb_freed(bb, cx.ev)

                @block.gpsimd
                def _(g):
                    g = cx.ser_g(g)
                    cx.wait(g, cx.ev, cx.vv)
                    cx.inc(g.indirect_dma_start(
                        out=bsel[:, j:j + 1], out_offset=None,
                        in_=inp[("bqh", "bkh", "bvh")[j]],
                        in_offset=bass.IndirectOffsetOnAxis(
                            ap=idxu[:, 0:1], axis=0)), cx.lg, 16)
                    g.wait_ge(cx.lg, cx.val(cx.lg))

            @block.vector
            def _(v):
                v = cx.ser_v(v)
                cx.wait(v, cx.vv, cx.lg)
                v.tensor_scalar(bsel[:, 2:3], bsel[:, 2:3],
                                gcol[0:64, 2 + sl:3 + sl], None, op0=ALU.mult)

            for j, name in enumerate(("q", "k", "v")):
                for c in range(8):
                    c0 = c * 512
                    slot2 = c % 2

                    @block.tensor
                    def _(t):
                        cx.wait(t, cx.ev, cx.vv)
                        bi, ps = cx.bank(t)
                        for ci in range(8):
                            ins = t.matmul(
                                ps[0:64, :],
                                wsel[:, j * 512 + ci * 64:
                                     j * 512 + (ci + 1) * 64],
                                xn_t(ci, c0, 512),
                                start=(ci == 0), stop=(ci == 7))
                        cx.inc(ins, cx.mm)

                        @block.scalar
                        def _(s2):
                            s2 = cx.ser_s(s2)
                            cx.wait(s2, cx.mm, cx.vv)
                            if name == "v":
                                s2.activation(vTs[:, slot2 * 512:
                                                  (slot2 + 1) * 512],
                                              ps[0:64, :], AF.Identity,
                                              bias=bsel[:, 2:3],
                                              scale=gcol[0:64, 2 + sl:3 + sl])
                            else:
                                dst = qT if name == "q" else kT
                                s2.activation(dst[:, c0:c0 + 512], ps[0:64, :],
                                              AF.Identity,
                                              bias=bsel[:, j:j + 1])
                        cx.bank_freed(bi, cx.ev)

                        if name == "v":
                            @block.tensor
                            def _(t2):
                                cx.wait(t2, cx.ev)
                                for u in range(4):
                                    kt = c * 4 + u
                                    bb, psb = cx.bankb(t2)
                                    ins = t2.transpose(
                                        psb[:, 0:64],
                                        vTs[:, slot2 * 512 + u * 128:
                                            slot2 * 512 + (u + 1) * 128],
                                        idb[0:64, 0:64])
                                    cx.inc(ins, cx.mm)

                                    @block.vector
                                    def _(v):
                                        v = cx.ser_v(v)
                                        cx.wait(v, cx.mm)
                                        v.tensor_copy(
                                            vTM[:, kt * 64:(kt + 1) * 64],
                                            psb[:, 0:64])
                                    cx.bankb_freed(bb, cx.vv)

            for qi in range(32):
                @block.tensor
                def _(t):
                    cx.wait(t, cx.vv, cx.ev)
                    for c in range(8):
                        c0 = c * 512
                        bi, ps = cx.bank(t)
                        ins = t.matmul(ps[:], qT[:, qi * 128:(qi + 1) * 128],
                                       kT[:, c0:c0 + 512],
                                       start=True, stop=True)
                        cx.inc(ins, cx.mm)

                        @block.vector
                        def _(v):
                            v = cx.ser_v(v)
                            cx.wait(v, cx.mm, cx.ev)
                            v.tensor_copy(strip[:, c0:c0 + 512], ps[:])
                        cx.bank_freed(bi, cx.vv)

                @block.vector
                def _(v):
                    v = cx.ser_v(v)
                    cx.wait(v, cx.vv, cx.ev)
                    v.reduce_max(nmax[:], strip[:], axis=AX.X, negate=True)
                    v.tensor_scalar(nmax[:], nmax[:], gcol[:, sl:sl + 1],
                                    None, op0=ALU.mult)

                @block.scalar
                def _(s2):
                    s2 = cx.ser_s(s2)
                    cx.wait(s2, cx.vv, cx.mm)
                    s2.activation(pstrip[:], strip[:], AF.Exp,
                                  bias=nmax[:, 0:1], scale=gcol[:, sl:sl + 1],
                                  accum_out=rsum[:])

                @block.vector
                def _(v):
                    v = cx.ser_v(v)
                    cx.wait(v, cx.ev)
                    v.reciprocal(nmax[:], rsum[:])
                    v.tensor_scalar(pstrip[:], pstrip[:], nmax[:, 0:1], None,
                                    op0=ALU.mult)

                @block.tensor
                def _(t):
                    cx.wait(t, cx.vv)
                    bo, pso = cx.bank(t)
                    for G0 in range(0, 32, 16):
                        for u0 in range(0, 16, 8):
                            bb, psb = cx.bankb(t)
                            for u in range(8):
                                kt = G0 + u0 + u
                                ins = t.transpose(
                                    psb[:, u * 128:(u + 1) * 128],
                                    pstrip[:, kt * 128:(kt + 1) * 128], idb[:])
                            cx.inc(ins, cx.mm)

                            @block.vector
                            def _(v):
                                v = cx.ser_v(v)
                                cx.wait(v, cx.mm)
                                v.tensor_copy(
                                    ptbig[:, u0 * 128:(u0 + 8) * 128],
                                    psb[:, 0:8 * 128])
                            cx.bankb_freed(bb, cx.vv)
                        cx.wait(t, cx.vv)
                        for u in range(16):
                            kt = G0 + u
                            ins = t.matmul(pso[0:64, 0:128],
                                           vTM[:, kt * 64:(kt + 1) * 64],
                                           ptbig[:, u * 128:(u + 1) * 128],
                                           start=(kt == 0), stop=(kt == 31))
                        cx.inc(ins, cx.mm)

                    @block.scalar
                    def _(s2):
                        s2 = cx.ser_s(s2)
                        cx.wait(s2, cx.mm)
                        s2.activation(oT2[sl * 64:sl * 64 + 64,
                                          qi * 128:(qi + 1) * 128],
                                      pso[0:64, 0:128], AF.Copy)
                    cx.bank_freed(bo, cx.ev)

        @block.sync
        def _(sy):
            cx.wait(sy, cx.ev)
            cx.inc(sy.dma_start(o2_hb[:, :], oT2[:]), cx.st, 16)
            sy.wait_ge(cx.st, cx.val(cx.st))

        cx.free_sbuf("wsel", "gath", "bsel", "brow", "idxu", "idxf", "i64",
                     "gcol", "qT3", "kT3", "vTM3", "vTs3", "strip3",
                     "pstrip3", "ptbig3", "rsum3", "nmax3", "oT2")

        inw = cx.sbuf("inw", (128, 8 * 128), BF16)
        inwz = cx.sbuf("inwz", (128, 8 * 128), BF16)
        convd = cx.sbuf("convd", (128, 4 * 128), BF16)
        convb = cx.sbuf("convb", (128, 16), F32)
        xpw = cx.sbuf("xpw", (128, 96), BF16)
        xib = cx.sbuf("xib", (128, 4100), BF16)
        xcb2 = cx.sbuf("xcb2", (128, S), BF16)
        inw_used = {}

        @block.sync
        def _(sy):
            cx.inc(sy.dma_start(convb[:], inp["convb"]), cx.ld, 16)
            sy.wait_ge(cx.ld, cx.val(cx.ld))

        @block.vector
        def _(v):
            v = cx.ser_v(v)
            v.memset(xib[:, 0:4], 0.0)
            v.memset(dbc[:], 0.0)

        for i in range(16):
            @block.gpsimd
            def _(g):
                g = cx.ser_g(g)
                if i >= 1 and inw_used.get("all", 0) > 0:
                    g.wait_ge(cx.mm, inw_used["all"])
                wload(g, cx, inw[:], inp["in_w"][:, i * 128:(i + 1) * 128], 8)
                wload(g, cx, inwz[:],
                      inp["in_w"][:, 2048 + i * 128:2048 + (i + 1) * 128], 8)
                cx.inc(g.dma_start(
                    convd[:].rearrange("p (j o) -> p j o", j=4),
                    inp["convd"][:, i * 512:(i + 1) * 512]
                    .rearrange("p (j o) -> p j o", j=4)), cx.lg, 16)
                g.wait_ge(cx.lg, cx.val(cx.lg))
                cx.inc(g.dma_start(xpw[:],
                                   inp["xp_w"][i * 128:(i + 1) * 128, :]),
                       cx.lg, 16)
                g.wait_ge(cx.lg, cx.val(cx.lg))

            for c in range(8):
                c0 = c * 512

                @block.tensor
                def _(t):
                    cx.wait(t, cx.lg, cx.ld)
                    if c == 0:
                        cx.wait(t, cx.vv, cx.ev)
                    bi, ps = cx.bank(t)
                    for ci in range(8):
                        ins = t.matmul(ps[:],
                                       inw[:, ci * 128:(ci + 1) * 128],
                                       xn_t(ci, c0, 512),
                                       start=(ci == 0), stop=(ci == 7))
                    cx.inc(ins, cx.mm)
                    if c == 7:
                        inw_used[0] = cx.val(cx.mm)

                    @block.scalar
                    def _(s2):
                        s2 = cx.ser_s(s2)
                        cx.wait(s2, cx.mm, cx.st)
                        s2.activation(xib[:, 4 + c0:4 + c0 + 512], ps[:],
                                      AF.Copy)
                    cx.bank_freed(bi, cx.ev)

            for c in range(8):
                c0 = c * 512

                @block.tensor
                def _(t):
                    cx.wait(t, cx.ev, cx.lg)
                    bi, ps = cx.bank(t)
                    for j in range(4):
                        ins = t.matmul(ps[:],
                                       convd[:, j * 128:(j + 1) * 128],
                                       xib[:, c0 + j + 1:c0 + j + 1 + 512],
                                       start=(j == 0), stop=(j == 3))
                    cx.inc(ins, cx.mm)
                    if c == 7:
                        inw_used["all"] = cx.val(cx.mm)

                    @block.scalar
                    def _(s2):
                        s2 = cx.ser_s(s2)
                        cx.wait(s2, cx.mm, cx.st)
                        s2.activation(xcb2[:, c0:c0 + 512], ps[:], AF.Silu,
                                      bias=convb[:, i:i + 1])
                    cx.bank_freed(bi, cx.ev)

                    bj, ps2 = cx.bank(t)
                    t.wait_ge(cx.ev, cx.val(cx.ev))
                    ins = t.matmul(ps2[0:96, :], xpw[:],
                                   xcb2[:, c0:c0 + 512], start=True, stop=True)
                    cx.inc(ins, cx.mm)

                    @block.vector
                    def _(v):
                        v = cx.ser_v(v)
                        cx.wait(v, cx.mm)
                        v.tensor_tensor(dbc[0:96, c0:c0 + 512],
                                        dbc[0:96, c0:c0 + 512],
                                        ps2[0:96, :], op=ALU.add)
                    cx.bank_freed(bj, cx.vv)

            @block.sync
            def _(sy):
                cx.wait(sy, cx.ev, cx.vv)
                cx.inc(sy.dma_start(xc_hb[i * 128:(i + 1) * 128, :], xcb2[:]),
                       cx.st, 16)
                sy.wait_ge(cx.st, cx.val(cx.st))

            for c in range(8):
                c0 = c * 512

                @block.tensor
                def _(t):
                    cx.wait(t, cx.lg, cx.ld, cx.st)
                    bi, ps = cx.bank(t)
                    for ci in range(8):
                        ins = t.matmul(
                            ps[:],
                            inwz[:, ci * 128:(ci + 1) * 128],
                            xn_t(ci, c0, 512),
                            start=(ci == 0), stop=(ci == 7))
                    cx.inc(ins, cx.mm)
                    if c == 7:
                        inw_used["all"] = cx.val(cx.mm)

                    @block.scalar
                    def _(s2):
                        s2 = cx.ser_s(s2)
                        cx.wait(s2, cx.mm, cx.st)
                        s2.activation(xcb2[:, c0:c0 + 512], ps[:], AF.Copy)
                    cx.bank_freed(bi, cx.ev)

            @block.sync
            def _(sy):
                cx.wait(sy, cx.ev)
                cx.inc(sy.dma_start(z_hb[i * 128:(i + 1) * 128, :], xcb2[:]),
                       cx.st, 16)
                sy.wait_ge(cx.st, cx.val(cx.st))

        cx.fence(block)
        cx.free_sbuf("inw", "inwz", "convd", "convb", "xpw", "xib", "xcb2",
                     "xn", "g16", "msks", "sc4", "m11", "it16", "ks2", "br16")

        @block.sync
        def _(sy):
            for i in range(16):
                cx.inc(sy.dma_start(asb[:, i * 16:(i + 1) * 16],
                                    inp["alog"][i * 128:(i + 1) * 128, :]),
                       cx.ld, 16)
                sy.wait_ge(cx.ld, cx.val(cx.ld))

        @block.scalar
        def _(s2):
            s2 = cx.ser_s(s2)
            cx.wait(s2, cx.ld, cx.vv)
            s2.activation(asb[:], asb[:], AF.Exp)

        @block.vector
        def _(v):
            v = cx.ser_v(v)
            cx.wait(v, cx.ev)
            v.tensor_scalar(asb[:], asb[:], -1.0, None, op0=ALU.mult)

        dtl = cx.sbuf("dtl", (64, S), BF16)
        dtw_sb = cx.sbuf("dtw_sb", (64, 2 * 128), BF16)
        dtb_sb = cx.sbuf("dtb_sb", (128, 16), F32)
        dtt = cx.sbuf("dtt", (128, S), F32)
        dta = cx.sbuf("dta", (128, S), F32)

        @block.sync
        def _(sy):
            cx.inc(sy.dma_start(dtb_sb[:], inp["dt_b"]), cx.ld, 16)
            sy.wait_ge(cx.ld, cx.val(cx.ld))

        @block.scalar
        def _(s2):
            s2 = cx.ser_s(s2)
            cx.wait(s2, cx.vv, cx.ev)
            s2.activation(dtl[:], dbc[0:64, :], AF.Copy)

        dtw_used = {}
        for i in range(16):
            slot = i % 2

            @block.gpsimd
            def _(g):
                g = cx.ser_g(g)
                if slot in dtw_used:
                    g.wait_ge(cx.mm, dtw_used[slot])
                cx.inc(g.dma_start(dtw_sb[:, slot * 128:(slot + 1) * 128],
                                   inp["dt_w"][:, i * 128:(i + 1) * 128]),
                       cx.lg, 16)
                g.wait_ge(cx.lg, cx.val(cx.lg))

            for c in range(8):
                c0 = c * 512

                @block.tensor
                def _(t):
                    cx.wait(t, cx.lg, cx.ld, cx.ev)
                    bi, ps = cx.bank(t)
                    ins = t.matmul(ps[:], dtw_sb[:, slot * 128:(slot + 1) * 128],
                                   dtl[:, c0:c0 + 512], start=True, stop=True)
                    cx.inc(ins, cx.mm)
                    if c == 7:
                        dtw_used[slot] = cx.val(cx.mm)

                    @block.scalar
                    def _(s2):
                        s2 = cx.ser_s(s2)
                        cx.wait(s2, cx.mm, cx.st, cx.vv)
                        s2.activation(dtt[:, c0:c0 + 512], ps[:], AF.Identity,
                                      bias=dtb_sb[:, i:i + 1])
                    cx.bank_freed(bi, cx.ev)

            @block.scalar
            def _(s2):
                s2 = cx.ser_s(s2)
                cx.wait(s2, cx.ev, cx.vv)
                s2.activation(dta[:], dtt[:], AF.Abs)
                s2.activation(dta[:], dta[:], AF.Exp, scale=-1.0)
                s2.activation(dta[:], dta[:], AF.Ln, bias=1.0)

            @block.vector
            def _(v):
                v = cx.ser_v(v)
                cx.wait(v, cx.ev)
                v.tensor_scalar(dtt[:], dtt[:], 0.0, None, op0=ALU.max)
                v.tensor_tensor(dtt[:], dtt[:], dta[:], op=ALU.add)

            @block.sync
            def _(sy):
                cx.wait(sy, cx.vv)
                cx.inc(sy.dma_start(dt_hb[i * 128:(i + 1) * 128, :], dtt[:]),
                       cx.st, 16)
                sy.wait_ge(cx.st, cx.val(cx.st))

        cx.fence(block)
        cx.free_sbuf("dtl", "dtw_sb", "dtb_sb", "dtt", "dta")

        TH = 2048
        yacc = cx.sbuf("yacc", (128, 16 * TH), F32)
        bcb = cx.sbuf("bcb", (128, 4 * TH), BF16)
        dt_t = cx.sbuf("dt_t", (128, TH), F32)
        dtw_t = cx.sbuf("dtw_t", (128, TH), F32)
        xc_t = cx.sbuf("xc_t", (128, TH), BF16)
        dA_t = cx.sbuf("dA_t", (128, TH), F32)
        u_t = cx.sbuf("u_t", (128, TH), BF16)
        h_t = cx.sbuf("h_t", (128, TH), BF16)
        carry = cx.sbuf("carry", (128, 256), F32)
        grow2 = cx.sbuf("grow2", (1, 512), F32)

        @block.vector
        def _(v):
            v = cx.ser_v(v)
            v.memset(carry[:], 0.0)

        for th in range(2):
            t0 = th * TH
            for nb in range(8):
                for u2 in range(2):
                    n = nb * 2 + u2
                    for wslot, row in ((u2 * 2, 64 + n), (u2 * 2 + 1, 80 + n)):
                        for c2 in range(4):
                            @block.sync
                            def _(sy):
                                cx.wait(sy, cx.mm, cx.ev, cx.gg, cx.vv)
                                cx.inc(sy.dma_start(
                                    grow2[:],
                                    dbc[row:row + 1, t0 + c2 * 512:
                                        t0 + c2 * 512 + 512]), cx.ld, 16)
                                sy.wait_ge(cx.ld, cx.val(cx.ld))

                            @block.tensor
                            def _(t):
                                cx.wait(t, cx.ld, cx.gg)
                                bi, ps = cx.bank(t)
                                ins = t.matmul(ps[:], onesf[0:1, :], grow2[:],
                                               start=True, stop=True)
                                cx.inc(ins, cx.mm)

                                @block.scalar
                                def _(s2):
                                    s2 = cx.ser_s(s2)
                                    cx.wait(s2, cx.mm, cx.vv, cx.gg)
                                    s2.activation(
                                        bcb[:, wslot * TH + c2 * 512:
                                            wslot * TH + c2 * 512 + 512],
                                        ps[:], AF.Copy)
                                cx.bank_freed(bi, cx.ev)

                for i in range(16):
                    @block.sync
                    def _(sy):
                        cx.wait(sy, cx.vv, cx.gg, cx.ev)
                        cx.inc(sy.dma_start(
                            dt_t[:], dt_hb[i * 128:(i + 1) * 128, t0:t0 + TH]),
                            cx.ld, 16)
                        sy.wait_ge(cx.ld, cx.val(cx.ld))
                        cx.inc(sy.dma_start(
                            xc_t[:], xc_hb[i * 128:(i + 1) * 128, t0:t0 + TH]),
                            cx.ld, 16)
                        sy.wait_ge(cx.ld, cx.val(cx.ld))

                    @block.vector
                    def _(v):
                        v = cx.ser_v(v)
                        cx.wait(v, cx.ld)
                        v.tensor_tensor(dtw_t[:], dt_t[:], xc_t[:],
                                        op=ALU.mult)

                    for u2 in range(2):
                        n = nb * 2 + u2

                        @block.scalar
                        def _(s2):
                            s2 = cx.ser_s(s2)
                            cx.wait(s2, cx.ld, cx.vv, cx.gg)
                            s2.activation(dA_t[:], dt_t[:], AF.Exp,
                                          scale=asb[:, i * 16 + n:
                                                    i * 16 + n + 1])

                        @block.gpsimd
                        def _(g):
                            g = cx.ser_g(g)
                            cx.wait(g, cx.vv, cx.ev)
                            g.tensor_tensor(u_t[:], dtw_t[:],
                                            bcb[:, (u2 * 2) * TH:
                                                (u2 * 2 + 1) * TH],
                                            op=ALU.mult)

                        @block.vector
                        def _(v):
                            v = cx.ser_v(v)
                            cx.wait(v, cx.ev, cx.gg)
                            cc = i * 16 + n
                            if th == 0:
                                v.tensor_tensor_scan(h_t[:], dA_t[:], u_t[:],
                                                     0.0, op0=ALU.mult,
                                                     op1=ALU.add)
                            else:
                                v.tensor_tensor_scan(h_t[:], dA_t[:], u_t[:],
                                                     carry[:, cc:cc + 1],
                                                     op0=ALU.mult, op1=ALU.add)
                            v.tensor_copy(carry[:, cc:cc + 1],
                                          h_t[:, TH - 1:TH])
                            v.tensor_tensor(u_t[:], h_t[:],
                                            bcb[:, (u2 * 2 + 1) * TH:
                                                (u2 * 2 + 2) * TH],
                                            op=ALU.mult)
                            dst = yacc[:, i * TH:(i + 1) * TH]
                            if nb == 0 and u2 == 0:
                                v.tensor_tensor(dst, u_t[:], u_t[:],
                                                op=ALU.bypass) \
                                    if False else v.tensor_copy(dst, u_t[:])
                            else:
                                v.tensor_tensor(dst, dst, u_t[:], op=ALU.add)

            dsk = cx.sbuf(f"dsk{th}", (128, 16), F32)

            @block.sync
            def _(sy):
                cx.inc(sy.dma_start(dsk[:], inp["dskip"]), cx.ld, 16)
                sy.wait_ge(cx.ld, cx.val(cx.ld))

            for i in range(16):
                @block.sync
                def _(sy):
                    cx.wait(sy, cx.vv, cx.ev, cx.gg, cx.st)
                    cx.inc(sy.dma_start(
                        xc_t[:], xc_hb[i * 128:(i + 1) * 128, t0:t0 + TH]),
                        cx.ld, 16)
                    sy.wait_ge(cx.ld, cx.val(cx.ld))
                    cx.inc(sy.dma_start(
                        h_t[:], z_hb[i * 128:(i + 1) * 128, t0:t0 + TH]),
                        cx.ld, 16)
                    sy.wait_ge(cx.ld, cx.val(cx.ld))

                @block.scalar
                def _(s2):
                    s2 = cx.ser_s(s2)
                    cx.wait(s2, cx.ld, cx.vv)
                    s2.activation(u_t[:], h_t[:], AF.Silu)

                @block.vector
                def _(v):
                    v = cx.ser_v(v)
                    cx.wait(v, cx.ld, cx.ev)
                    dst = yacc[:, i * TH:(i + 1) * TH]
                    v.scalar_tensor_tensor(dst, xc_t[:], dsk[:, i:i + 1],
                                           dst, op0=ALU.mult, op1=ALU.add)
                    v.tensor_tensor(h_t[:], dst, u_t[:], op=ALU.mult)

                @block.sync
                def _(sy):
                    cx.wait(sy, cx.vv)
                    cx.inc(sy.dma_start(
                        y_hb[i * 128:(i + 1) * 128, t0:t0 + TH], h_t[:]),
                        cx.st, 16)
                    sy.wait_ge(cx.st, cx.val(cx.st))
            cx.free_sbuf(f"dsk{th}")

        cx.fence(block)
        cx.free_sbuf("yacc", "bcb", "dt_t", "dtw_t", "xc_t", "dA_t", "u_t",
                     "h_t", "carry", "grow2", "dbc", "asb")

        wcomb = cx.sbuf("wcomb", (128, 16 * D), BF16)
        projb = cx.sbuf("projb", (128, 8 * D), BF16)
        owt = cx.sbuf("owt", (128, 2 * 8 * 128), BF16)
        ow_used = {}

        @block.gpsimd
        def _(g):
            g = cx.ser_g(g)
            wload(g, cx, projb[:], inp["proj"], 8)

        for i in range(16):
            slot = i % 2

            @block.gpsimd
            def _(g):
                g = cx.ser_g(g)
                if slot in ow_used:
                    g.wait_ge(cx.mm, ow_used[slot])
                wload(g, cx, owt[:, slot * 1024:(slot + 1) * 1024],
                      inp["out_wT"][:, i * 128:(i + 1) * 128], 8)

            for c2 in range(2):
                @block.tensor
                def _(t):
                    cx.wait(t, cx.lg, cx.ev)
                    bi, ps = cx.bank(t)
                    for m in range(8):
                        ins = t.matmul(
                            ps[:],
                            owt[:, slot * 1024 + m * 128:
                                slot * 1024 + (m + 1) * 128],
                            projb[:, m * D + c2 * 512:m * D + c2 * 512 + 512],
                            start=(m == 0), stop=(m == 7))
                    cx.inc(ins, cx.mm)
                    if c2 == 1:
                        ow_used[slot] = cx.val(cx.mm)

                    @block.scalar
                    def _(s2):
                        s2 = cx.ser_s(s2)
                        cx.wait(s2, cx.mm)
                        s2.activation(wcomb[:, i * D + c2 * 512:
                                            i * D + c2 * 512 + 512],
                                      ps[:], AF.Copy)
                    cx.bank_freed(bi, cx.ev)

        ybuf = cx.sbuf("ybuf", (128, 16 * 512), BF16)
        obuf = cx.sbuf("obuf", (128, 512), BF16)
        wo_sb = cx.sbuf("wo3", (128, D), BF16)
        b2sb = cx.sbuf("b2sb", (2, D), BF16)
        ones2 = cx.sbuf("ones2", (2, 512), BF16)
        stg = cx.sbuf("stg3", (128, 2 * 512), F32)

        @block.gpsimd
        def _(g):
            g = cx.ser_g(g)
            cx.inc(g.dma_start(wo_sb[:], inp["wo"]), cx.lg, 16)
            g.wait_ge(cx.lg, cx.val(cx.lg))
            cx.inc(g.dma_start(b2sb[:], inp["bias2"]), cx.lg, 16)
            g.wait_ge(cx.lg, cx.val(cx.lg))

        @block.vector
        def _(v):
            v = cx.ser_v(v)
            v.memset(ones2[:], 1.0)

        for c in range(8):
            c0 = c * 512

            @block.sync
            def _(sy):
                cx.wait(sy, cx.mm)
                cx.inc(sy.dma_start(
                    ybuf[:].rearrange("p (i t) -> p i t", i=16),
                    y_hb[:, c0:c0 + 512].rearrange("(i p) t -> p i t", p=128)),
                    cx.ld, 16)
                sy.wait_ge(cx.ld, cx.val(cx.ld))
                cx.inc(sy.dma_start(obuf[:], o2_hb[:, c0:c0 + 512]),
                       cx.ld, 16)
                sy.wait_ge(cx.ld, cx.val(cx.ld))

            for po in range(8):
                @block.tensor
                def _(t):
                    cx.wait(t, cx.ld, cx.lg, cx.vv, cx.ev)
                    bi, ps = cx.bank(t)
                    t.matmul(ps[:], b2sb[:, po * 128:(po + 1) * 128], ones2[:],
                             start=True, stop=False)
                    t.matmul(ps[:], wo_sb[:, po * 128:(po + 1) * 128],
                             obuf[:], start=False, stop=False)
                    for i in range(16):
                        ins = t.matmul(
                            ps[:],
                            wcomb[:, i * D + po * 128:i * D + (po + 1) * 128],
                            ybuf[:, i * 512:(i + 1) * 512],
                            start=False, stop=(i == 15))
                    cx.inc(ins, cx.mm)
                    sslot = (c * 8 + po) % 2

                    @block.scalar
                    def _(s2):
                        s2 = cx.ser_s(s2)
                        cx.wait(s2, cx.mm, cx.st)
                        s2.activation(stg[:, sslot * 512:(sslot + 1) * 512],
                                      ps[:], AF.Copy)
                    cx.bank_freed(bi, cx.ev)

                    @block.sync
                    def _(sy):
                        cx.wait(sy, cx.ev)
                        cx.inc(sy.dma_start(
                            part_out[po * 128:(po + 1) * 128, c0:c0 + 512],
                            stg[:, sslot * 512:(sslot + 1) * 512]), cx.st, 16)
                        sy.wait_ge(cx.st, cx.val(cx.st))

        @block.sync
        def _(sy):
            cx.wait(sy, cx.st)

    return nc, cx, inp




def build_LE():
    nc = bass.Bass("TRN2", target_bir_lowering=False)
    inp = {}
    for name in ("p1", "p2", "p3", "p4", "p5"):
        inp[name] = nc.dram_tensor(name, [D, TOWN], F32,
                                   kind="ExternalInput").ap()
    xout = nc.dram_tensor("xout", [D, TOWN], F32, kind="ExternalOutput").ap()
    cx = Ctx(nc)
    with nc.Block() as block:
        cx.block = block
        a = cx.sbuf("a", (128, TOWN), F32)
        b = cx.sbuf("b", (128, TOWN), F32)
        for po in range(8):
            @block.sync
            def _(sy):
                cx.wait(sy, cx.vv, cx.st)
                cx.inc(sy.dma_start(a[:], inp["p1"][po * 128:(po + 1) * 128, :]),
                       cx.ld, 16)
                sy.wait_ge(cx.ld, cx.val(cx.ld))

            for name in ("p2", "p3", "p4", "p5"):
                @block.sync
                def _(sy):
                    cx.wait(sy, cx.vv)
                    cx.inc(sy.dma_start(b[:],
                                        inp[name][po * 128:(po + 1) * 128, :]),
                           cx.ld, 16)
                    sy.wait_ge(cx.ld, cx.val(cx.ld))

                @block.vector
                def _(v):
                    v = cx.ser_v(v)
                    cx.wait(v, cx.ld)
                    v.tensor_tensor(a[:], a[:], b[:], op=ALU.add)

            @block.sync
            def _(sy):
                cx.wait(sy, cx.vv)
                cx.inc(sy.dma_start(xout[po * 128:(po + 1) * 128, :], a[:]),
                       cx.st, 16)
                sy.wait_ge(cx.st, cx.val(cx.st))

        @block.sync
        def _(sy):
            cx.wait(sy, cx.st)
    return nc, cx, inp


def host_prep_L3(I, pA, pB, pC):
    DI = 2048
    conv_w = np.asarray(I["L3_attn_conv_w"], np.float32)
    maps = []
    iota16 = np.arange(16, dtype=np.float32)[None, :]
    iota64 = np.arange(64, dtype=np.float32)[:, None]
    idf = np.eye(128, dtype=np.float32)
    onesf = np.ones((128, 128), np.float32)
    idb = np.eye(128, dtype=np.float32).astype(bf16)
    onesln = np.full((128, 128), 1.0 / 1024.0, np.float32).astype(bf16)
    projw = np.asarray(I["L3_attn_proj_w"], np.float32)
    projb = np.asarray(I["L3_attn_proj_b"], np.float32)
    bo3 = np.asarray(I["L3_ffn_bo"], np.float32)
    for c in range(8):
        b, p = divmod(c, 2)

        def R(x):
            return f32c(x[:, ::-1]) if p == 1 else f32c(x)
        convd = np.zeros((128, 16 * 4 * 128), np.float32)
        for i in range(16):
            for j in range(4):
                blk = convd[:, (i * 4 + j) * 128:(i * 4 + j + 1) * 128]
                np.fill_diagonal(blk, conv_w[p, i * 128:(i + 1) * 128, j])
        bias2 = np.zeros((2, D), np.float32)
        if p == 0:
            bias2[0] = projb
            bias2[1] = bo3
        ksel2 = np.zeros((1, 8), np.float32)
        ksel2[0, 2 * p] = 1.0
        ksel2[0, 4 + 2 * p + 1] = 1.0
        m = {
            "pA": R(pA[b]), "pB": R(pB[b]), "pC": R(pC[b]),
            "in_w": f32c(I["L3_attn_in_w"][p]),
            "convd": convd,
            "convb": f32c(np.asarray(I["L3_attn_conv_b"][p])
                          .reshape(16, 128).T),
            "xp_w": f32c(I["L3_attn_xp_w"][p]),
            "dt_w": f32c(I["L3_attn_dt_w"][p]),
            "dt_b": f32c(np.asarray(I["L3_attn_dt_b"][p]).reshape(16, 128).T),
            "alog": f32c(I["L3_attn_A_log"][p]),
            "dskip": f32c(np.asarray(I["L3_attn_D"][p]).reshape(16, 128).T),
            "out_wT": f32c(np.asarray(I["L3_attn_out_w"][p]).T),
            "proj": f32c(projw[p * D:(p + 1) * D, :]),
            "wr": f32c(I["L3_ffn_Wr"]), "br": f32c(I["L3_ffn_br"][:, None]),
            "wqh": f32c(np.asarray(I["L3_ffn_Wq"]).T),
            "wkh": f32c(np.asarray(I["L3_ffn_Wk"]).T),
            "wvh": f32c(np.asarray(I["L3_ffn_Wv"]).T),
            "bqh": f32c(I["L3_ffn_bq"][:, None]),
            "bkh": f32c(I["L3_ffn_bk"][:, None]),
            "bvh": f32c(I["L3_ffn_bv"][:, None]),
            "wo": f32c(I["L3_ffn_Wo"][p * 128:(p + 1) * 128, :]),
            "bias2": bias2,
            "iota16": iota16, "iota64": iota64, "ksel2": ksel2,
            "idf": idf, "onesf": onesf, "idb": idb, "onesln": onesln,
        }
        maps.append(m)
    return maps


def run_L3(I, resC1, resC3):
    pA = [resC1[2 * b]["attn_out"] for b in range(4)]
    pB = [resC1[2 * b + 1]["attn_out"] for b in range(4)]
    pC = [np.concatenate([resC3[2 * b]["xout"],
                          resC3[2 * b + 1]["xout"]], axis=1)
          for b in range(4)]
    maps = host_prep_L3(I, pA, pB, pC)
    nc, cx, _ = build_L3()
    res = run_bass_kernel_spmd(nc, maps, core_ids=list(range(8))).results
    cx.close()
    parts = []
    for c in range(8):
        po = res[c]["part_out"]
        parts.append(f32c(po[:, ::-1]) if c % 2 == 1 else po)
    return parts, pA, pB, pC


def kernel(**inputs):
    I = {k: np.asarray(v) for k, v in inputs.items()}
    resA = run_L0(I)
    resB = run_L1(I, resA)
    resC1, resC3 = run_L2(I, resB)
    partsD, pA2, pB2, pC2 = run_L3(I, resC1, resC3)

    nc, cx, _ = build_LE()
    maps = []
    for c in range(8):
        b, p = divmod(c, 2)
        own = slice(p * TOWN, (p + 1) * TOWN)
        maps.append({
            "p1": f32c(pA2[b][:, own]), "p2": f32c(pB2[b][:, own]),
            "p3": f32c(pC2[b][:, own]), "p4": f32c(partsD[2 * b][:, own]),
            "p5": f32c(partsD[2 * b + 1][:, own]),
        })
    resE = run_bass_kernel_spmd(nc, maps, core_ids=list(range(8))).results
    cx.close()

    out = np.empty((B, S, D), np.float32)
    for c in range(8):
        b, p = divmod(c, 2)
        out[b, p * TOWN:(p + 1) * TOWN, :] = resE[c]["xout"].T
    return out



def rope_tables():
    half = 32
    inv = (1.0 / 10000.0 ** (np.arange(half, dtype=np.float32) / half))
    pos = np.arange(S, dtype=np.float32)
    ang = np.outer(pos, inv).astype(np.float32)
    return f32c(np.cos(ang).T), f32c(np.sin(ang).T)


def causal_masks():
    m = np.zeros((128, 4 * 512), np.float32)
    for r in range(4):
        blk = m[:, r * 512:(r + 1) * 512]
        for row in range(128):
            lo = r * 128 + row + 1
            if lo < 512:
                blk[row, lo:] = NEG
    return m


def host_prep_L0(I):
    x = np.asarray(I["L0_attn_x"], np.float32)
    qkv_w = np.asarray(I["L0_attn_qkv_w"], np.float32)
    qkv_b = np.asarray(I["L0_attn_qkv_b"], np.float32)
    out_w = np.asarray(I["L0_attn_out_w"], np.float32)
    out_b = np.asarray(I["L0_attn_out_b"], np.float32)
    rw = np.asarray(I["L0_ffn_router_w"], np.float32)
    rb = np.asarray(I["L0_ffn_router_b"], np.float32)
    w1 = np.asarray(I["L0_ffn_w1"], np.float32)
    b1 = np.asarray(I["L0_ffn_b1"], np.float32)
    w2 = np.asarray(I["L0_ffn_w2"], np.float32)
    b2 = np.asarray(I["L0_ffn_b2"], np.float32)
    sw1 = np.asarray(I["L0_ffn_sw1"], np.float32)
    sb1 = np.asarray(I["L0_ffn_sb1"], np.float32)
    sw2 = np.asarray(I["L0_ffn_sw2"], np.float32)
    sb2 = np.asarray(I["L0_ffn_sb2"], np.float32)

    cos, sin = rope_tables()
    cmask = causal_masks()
    idf = np.eye(128, dtype=np.float32)
    idb = np.eye(128, dtype=np.float32).astype(bf16)
    onesln = np.full((128, 128), 1.0 / 1024.0, np.float32).astype(bf16)

    w1big = np.concatenate(list(w1) + [sw1[0], sw1[1]], axis=1)
    b1big = np.concatenate([b1.reshape(-1), sb1[0], sb1[1]])[:, None]
    w2big = np.concatenate([w2.reshape(64 * 256, D), sw2[0], sw2[1]], axis=0)
    biasmat = np.zeros((128, D), np.float32)
    biasmat[0:64] = b2
    biasmat[64] = sb2[0]
    biasmat[65] = sb2[1]
    biasmat[66] = out_b

    qT = f32c(qkv_w[0:D, :].T)
    kTw = f32c(qkv_w[D:2 * D, :].T)
    vTw = f32c(qkv_w[2 * D:3 * D, :].T)
    woT = f32c(out_w.T)

    maps = []
    for c in range(8):
        b, p = divmod(c, 2)
        hs = p * 512
        maps.append({
            "xful": f32c(x[b].T),
            "xown": f32c(x[b, p * TOWN:(p + 1) * TOWN, :].T),
            "wq": f32c(qT[:, hs:hs + 512]),
            "wk": f32c(kTw[:, hs:hs + 512]),
            "wv": f32c(vTw[:, hs:hs + 512]),
            "bq": f32c(qkv_b[0:D][hs:hs + 512][:, None]),
            "bk": f32c(qkv_b[D:2 * D][hs:hs + 512][:, None]),
            "bv": f32c(qkv_b[2 * D:3 * D][hs:hs + 512][:, None]),
            "wo": f32c(woT[hs:hs + 512, :]),
            "cos": cos, "sin": sin, "cmask": cmask,
            "rw": f32c(rw), "rb": f32c(rb[:, None]),
            "w1big": w1big, "b1big": b1big, "w2big": w2big,
            "biasmat": biasmat,
            "idf": idf, "idb": idb, "onesln": onesln,
        })
    return maps


def run_L0(I):
    nc, cx, _ = build_L0()
    maps = host_prep_L0(I)
    res = run_bass_kernel_spmd(nc, maps, core_ids=list(range(8)))
    cx.close()
    return res.results





# version 02
# speedup vs baseline: 1.1122x; 1.1122x over previous
import numpy as np
import ml_dtypes
import concourse.bass as bass
import concourse.mybir as mybir
from concourse.bass_utils import run_bass_kernel_spmd

F32 = mybir.dt.float32
BF16 = mybir.dt.bfloat16
U32 = mybir.dt.uint32
AF = mybir.ActivationFunctionType
ALU = mybir.AluOpType
AX = mybir.AxisListType

D = 1024
S = 4096
B = 4
TOWN = 2048
NEG = -1.0e30
bf16 = ml_dtypes.bfloat16


def f32c(x):
    return np.ascontiguousarray(x, dtype=np.float32)


def bfc(x):
    return np.ascontiguousarray(np.asarray(x, dtype=np.float32).astype(bf16))


class Ser:
    PASS = {"wait_ge", "nop", "drain", "sem_inc", "dma_start",
            "dma_start_transpose", "engine_nop", "indirect_dma_start"}

    def __init__(self, cx, eng, sem):
        self._cx, self._eng, self._sem = cx, eng, sem

    def __getattr__(self, name):
        f = getattr(self._eng, name)
        if name in Ser.PASS or not callable(f):
            return f
        cx, eng, sem = self._cx, self._eng, self._sem

        def wrapped(*a, **k):
            if cx.val(sem) > 0:
                eng.wait_ge(sem, cx.val(sem))
            ins = f(*a, **k)
            cx.inc(ins, sem, 1)
            return ins
        return wrapped


class Ctx:
    def __init__(self, nc, n_psum=6, psum_shape=(128, 512)):
        self.nc = nc
        self.cnt = {}
        self._stack = []
        self._open = {}
        self._order = []
        self.ld = self.sem("c_ld")
        self.lg = self.sem("c_lg")
        self.st = self.sem("c_st")
        self.mm = self.sem("c_mm")
        self.ev = self.sem("c_ev")
        self.vv = self.sem("c_vv")
        self.gg = self.sem("c_gg")
        self.all_sems = [self.ld, self.lg, self.st, self.mm, self.ev, self.vv, self.gg]
        self.psum = [self.psum_t(f"cps{i}", list(psum_shape), F32)
                     for i in range(n_psum)]
        self.free_at = [(self.ev, 0) for _ in range(n_psum)]
        self.rr = 0
        self.psb = [self.psum_t(f"cpb{i}", [128, 1024], BF16)
                    for i in range(2)]
        self.free_at_b = [(self.ev, 0) for _ in range(2)]
        self.rrb = 0

    def sem(self, name):
        h = self.nc.semaphore(name)
        s = h.__enter__()
        self._stack.append(h)
        return s

    def sbuf(self, name, shape, dtype):
        self._uid = getattr(self, "_uid", 0) + 1
        h = self.nc.sbuf_tensor(f"sb{self._uid}_{name}", list(shape), dtype)
        t = h.__enter__()
        self._open[name] = h
        self._order.append(name)
        return t

    def free_sbuf(self, *names):
        if getattr(self, "block", None) is not None:
            self.fence(self.block)
        for name in sorted(names, key=lambda n: -self._order.index(n)):
            self._open.pop(name).__exit__(None, None, None)
            self._order.remove(name)

    def psum_t(self, name, shape, dtype):
        h = self.nc.psum_tensor(name, shape, dtype)
        t = h.__enter__()
        self._stack.append(h)
        return t

    def close(self):
        for name in reversed(self._order):
            self._open.pop(name).__exit__(None, None, None)
        self._order.clear()
        self._open.clear()
        for h in reversed(self._stack):
            h.__exit__(None, None, None)
        self._stack.clear()

    def inc(self, instr, sem, k=1):
        instr.then_inc(sem, k)
        self.cnt[sem] = self.cnt.get(sem, 0) + k
        return instr

    def val(self, sem):
        return self.cnt.get(sem, 0)

    def wait(self, eng, *sems):
        for sem in sems:
            if self.val(sem) > 0:
                eng.wait_ge(sem, self.val(sem))

    def wait_all(self, eng):
        self.wait(eng, *self.all_sems)

    def fence(self, block):
        for en in ("tensor", "vector", "scalar", "gpsimd", "sync"):
            @getattr(block, en)
            def _(e):
                self.wait_all(e)

    def ser_v(self, v):
        return Ser(self, v, self.vv)

    def ser_s(self, s):
        return Ser(self, s, self.ev)

    def ser_g(self, g):
        return Ser(self, g, self.gg)

    def bank(self, te):
        i = self.rr % len(self.psum)
        self.rr += 1
        sem, cnt = self.free_at[i]
        if cnt > 0:
            te.wait_ge(sem, cnt)
        return i, self.psum[i]

    def bank_freed(self, i, sem):
        self.free_at[i] = (sem, self.val(sem))

    def bankb(self, te):
        i = self.rrb % 2
        self.rrb += 1
        sem, cnt = self.free_at_b[i]
        if cnt > 0:
            te.wait_ge(sem, cnt)
        return i, self.psb[i]

    def bankb_freed(self, i, sem):
        self.free_at_b[i] = (sem, self.val(sem))



def wload(g, cx, dst, src, nci):
    O = src.shape[1]
    ins = g.dma_start(dst.rearrange("p (c o) -> p c o", c=nci),
                      src.rearrange("(c p) o -> p c o", p=128))
    cx.inc(ins, cx.lg, 16)
    g.wait_ge(cx.lg, cx.val(cx.lg))
    return ins


def emit_ln(cx, block, x_dram, xn, T, ones, tap=None, extra=()):
    xs = cx.sbuf("ln_xs", (128, 8 * 512), F32)
    xe = cx.sbuf("ln_xe", (128, 8 * 512), F32) if extra else None
    xnf = cx.sbuf("ln_xnf", (128, 8 * 512), F32) if tap is not None else None
    xb = cx.sbuf("ln_xb", (128, 8 * 512), BF16)
    sq = cx.sbuf("ln_sq", (128, 8 * 512), BF16)
    mb = cx.sbuf("ln_mb", (128, 512), F32)
    vb = cx.sbuf("ln_vb", (128, 512), F32)
    t2 = cx.sbuf("ln_t2", (128, 512), F32)

    for c in range(T // 512):
        c0 = c * 512

        @block.sync
        def _(sy):
            cx.wait(sy, cx.vv, cx.ev)
            cx.inc(sy.dma_start(xs[:].rearrange("p (i t) -> p i t", i=8),
                                x_dram[:, c0:c0 + 512]
                                .rearrange("(i p) t -> p i t", p=128)),
                   cx.ld, 16)
            sy.wait_ge(cx.ld, cx.val(cx.ld))

        if extra:
            for a in extra:
                @block.sync
                def _(sy):
                    cx.wait(sy, cx.vv)
                    cx.inc(sy.dma_start(xe[:].rearrange("p (i t) -> p i t", i=8),
                                        a[:, c0:c0 + 512]
                                        .rearrange("(i p) t -> p i t", p=128)),
                           cx.ld, 16)
                    sy.wait_ge(cx.ld, cx.val(cx.ld))

                @block.vector
                def _(v):
                    v = cx.ser_v(v)
                    cx.wait(v, cx.ld)
                    for i in range(8):
                        v.tensor_tensor(xs[:, i * 512:(i + 1) * 512],
                                        xs[:, i * 512:(i + 1) * 512],
                                        xe[:, i * 512:(i + 1) * 512],
                                        op=ALU.add)

        @block.scalar
        def _(s):
            s = cx.ser_s(s)
            cx.wait_all(s)
            for i in range(8):
                s.activation(xb[:, i * 512:(i + 1) * 512],
                             xs[:, i * 512:(i + 1) * 512], AF.Copy)
                ins = s.activation(sq[:, i * 512:(i + 1) * 512],
                                   xs[:, i * 512:(i + 1) * 512], AF.Square)
            del ins

        @block.tensor
        def _(t):
            cx.wait(t, cx.ev)
            bi, ps = cx.bank(t)
            for i in range(8):
                t.matmul(ps[:], ones[:], xb[:, i * 512:(i + 1) * 512],
                         start=(i == 0), stop=(i == 7))
            bj, ps2 = cx.bank(t)
            for i in range(8):
                ins = t.matmul(ps2[:], ones[:], sq[:, i * 512:(i + 1) * 512],
                               start=(i == 0), stop=(i == 7))
            cx.inc(ins, cx.mm)

            @block.vector
            def _(v):
                v = cx.ser_v(v)
                cx.wait(v, cx.mm)
                v.tensor_copy(mb[:], ps[:])
                v.tensor_copy(vb[:], ps2[:])
                v.tensor_tensor(t2[:], mb[:], mb[:], op=ALU.mult)
                v.tensor_tensor(vb[:], vb[:], t2[:], op=ALU.subtract)
                ins = v.tensor_scalar(vb[:], vb[:], 1e-5, None, op0=ALU.add)
                del ins
            cx.bank_freed(bi, cx.vv)
            cx.bank_freed(bj, cx.vv)

        @block.scalar
        def _(s):
            s = cx.ser_s(s)
            cx.wait(s, cx.vv)
            ins = s.activation(t2[:], vb[:], AF.Sqrt)
            del ins

        @block.vector
        def _(v):
            v = cx.ser_v(v)
            cx.wait(v, cx.ev)
            v.reciprocal(vb[:], t2[:])
            for i in range(8):
                v.tensor_tensor(t2[:], xs[:, i * 512:(i + 1) * 512], mb[:],
                                op=ALU.subtract)
                if tap is None:
                    ins = v.tensor_tensor(xn[:, i * T + c0:i * T + c0 + 512],
                                          t2[:], vb[:], op=ALU.mult)
                else:
                    ins = v.tensor_tensor(xnf[:, i * 512:(i + 1) * 512],
                                          t2[:], vb[:], op=ALU.mult)
            del ins

        if tap is not None:
            @block.scalar
            def _(s):
                s = cx.ser_s(s)
                cx.wait(s, cx.vv)
                for i in range(8):
                    ins = s.activation(xn[:, i * T + c0:i * T + c0 + 512],
                                       xnf[:, i * 512:(i + 1) * 512], AF.Copy)
                del ins
            tap(c0, xnf)

    cx.free_sbuf(*(["ln_xnf"] if tap is not None else []),
                 *(["ln_xe"] if extra else []),
                 "ln_xs", "ln_xb", "ln_sq", "ln_mb", "ln_vb", "ln_t2")


def emit_causal_mha(cx, block, inp, xn_t, oTh, nheads, consts):
    idb, cosb, sinb, cmask = (consts["idb"], consts["cos"], consts["sin"],
                              consts["cmask"])
    H = nheads
    wqkv = cx.sbuf("wqkv", (128, 3 * 512), BF16)
    bqkv = cx.sbuf("bqkv", (64, 3 * H), F32)
    qTa = cx.sbuf("qTa", (32, S), BF16)
    qTb = cx.sbuf("qTb", (32, S), BF16)
    kTa = cx.sbuf("kTa", (32, S), BF16)
    kTb = cx.sbuf("kTb", (32, S), BF16)
    pra = cx.sbuf("pra", (32, 2 * 512), F32)
    prb = cx.sbuf("prb", (32, 2 * 512), F32)
    rta = cx.sbuf("rta", (32, 512), F32)
    rtb = cx.sbuf("rtb", (32, 512), F32)
    vTM = cx.sbuf("vTM", (128, 32 * 64), BF16)
    vTs = cx.sbuf("vTs", (64, 2 * 512), BF16)
    strip = cx.sbuf("strip", (128, S), F32)
    pstrip = cx.sbuf("pstrip", (128, S), BF16)
    ptbig = cx.sbuf("ptbig", (128, 16 * 128), BF16)
    rsum = cx.sbuf("rsum", (128, 1), F32)
    nmax = cx.sbuf("nmax", (128, 1), F32)

    @block.sync
    def _(sy):
        for j, nm in enumerate(("bq", "bk", "bv")):
            for h in range(H):
                cx.inc(sy.dma_start(bqkv[:, j * H + h:j * H + h + 1],
                                    inp[nm][h * 64:(h + 1) * 64, :]),
                       cx.ld, 16)
                sy.wait_ge(cx.ld, cx.val(cx.ld))

    for h in range(H):
        @block.gpsimd
        def _(g):
            g = cx.ser_g(g)
            cx.wait(g, cx.mm)
            for j, wname in enumerate(("wq", "wk", "wv")):
                wsl = inp[wname][:, h * 64:(h + 1) * 64]
                wload(g, cx, wqkv[:, j * 512:(j + 1) * 512], wsl, 8)

        for j, name in enumerate(("q", "k", "v")):
            for c in range(8):
                c0 = c * 512
                slot = c % 2

                @block.tensor
                def _(t):
                    cx.wait(t, cx.ld, cx.lg)
                    if c == 0:
                        cx.wait(t, cx.vv, cx.ev)
                    bi, ps = cx.bank(t)
                    for ci in range(8):
                        ins = t.matmul(
                            ps[0:64, :],
                            wqkv[:, j * 512 + ci * 64:j * 512 + (ci + 1) * 64],
                            xn_t(ci, c0, 512),
                            start=(ci == 0), stop=(ci == 7))
                    cx.inc(ins, cx.mm)

                    if name == "v":
                        @block.scalar
                        def _(s):
                            s = cx.ser_s(s)
                            cx.wait(s, cx.mm, cx.vv)
                            ins = s.activation(
                                vTs[:, slot * 512:(slot + 1) * 512],
                                ps[0:64, :], AF.Identity,
                                bias=bqkv[:, j * H + h:j * H + h + 1])
                            del ins
                        cx.bank_freed(bi, cx.ev)

                        @block.tensor
                        def _(t2):
                            cx.wait(t2, cx.ev)
                            for u in range(4):
                                kt = c * 4 + u
                                bb, psb = cx.bankb(t2)
                                ins = t2.transpose(
                                    psb[:, 0:64],
                                    vTs[:, slot * 512 + u * 128:
                                        slot * 512 + (u + 1) * 128],
                                    idb[0:64, 0:64])
                                cx.inc(ins, cx.mm)

                                @block.vector
                                def _(v):
                                    v = cx.ser_v(v)
                                    cx.wait(v, cx.mm)
                                    ins = v.tensor_copy(
                                        vTM[:, kt * 64:(kt + 1) * 64],
                                        psb[:, 0:64])
                                    del ins
                                cx.bankb_freed(bb, cx.vv)
                    else:
                        @block.scalar
                        def _(s):
                            s = cx.ser_s(s)
                            cx.wait(s, cx.mm, cx.vv)
                            s.activation(
                                pra[:, slot * 512:(slot + 1) * 512],
                                ps[0:32, :], AF.Identity,
                                bias=bqkv[0:32, j * H + h:j * H + h + 1])
                            ins = s.activation(
                                prb[:, slot * 512:(slot + 1) * 512],
                                ps[32:64, :], AF.Identity,
                                bias=bqkv[32:64, j * H + h:j * H + h + 1])
                            del ins
                        cx.bank_freed(bi, cx.ev)

                        @block.vector
                        def _(v):
                            v = cx.ser_v(v)
                            cx.wait(v, cx.ev)
                            da = qTa if name == "q" else kTa
                            db = qTb if name == "q" else kTb
                            top = pra[:, slot * 512:(slot + 1) * 512]
                            bot = prb[:, slot * 512:(slot + 1) * 512]
                            co = cosb[:, c0:c0 + 512]
                            si = sinb[:, c0:c0 + 512]
                            v.tensor_tensor(rta[:], top, co, op=ALU.mult)
                            v.tensor_tensor(rtb[:], bot, si, op=ALU.mult)
                            v.tensor_tensor(da[:, c0:c0 + 512], rta[:], rtb[:],
                                            op=ALU.subtract)
                            v.tensor_tensor(rta[:], top, si, op=ALU.mult)
                            v.tensor_tensor(rtb[:], bot, co, op=ALU.mult)
                            ins = v.tensor_tensor(db[:, c0:c0 + 512], rta[:],
                                                  rtb[:], op=ALU.add)
                            del ins

        for qi in range(32):
            kw = (qi + 1) * 128
            nch = (kw + 511) // 512

            @block.tensor
            def _(t):
                cx.wait(t, cx.vv, cx.ev)
                for c in range(nch):
                    c0 = c * 512
                    w = min(512, kw - c0)
                    bi, ps = cx.bank(t)
                    t.matmul(ps[:, 0:w], qTa[:, qi * 128:(qi + 1) * 128],
                             kTa[:, c0:c0 + w], start=True, stop=False)
                    ins = t.matmul(ps[:, 0:w], qTb[:, qi * 128:(qi + 1) * 128],
                                   kTb[:, c0:c0 + w], start=False, stop=True)
                    cx.inc(ins, cx.mm)
                    last = (c == nch - 1)

                    @block.vector
                    def _(v):
                        v = cx.ser_v(v)
                        cx.wait(v, cx.mm, cx.ev)
                        if last:
                            r = qi % 4
                            ins = v.tensor_tensor(
                                strip[:, c0:c0 + w], ps[:, 0:w],
                                cmask[:, r * 512:r * 512 + w], op=ALU.add)
                        else:
                            ins = v.tensor_copy(strip[:, c0:c0 + w], ps[:, 0:w])
                        del ins
                    cx.bank_freed(bi, cx.vv)

            @block.vector
            def _(v):
                v = cx.ser_v(v)
                cx.wait(v, cx.vv, cx.ev)
                v.reduce_max(nmax[:], strip[:, 0:kw], axis=AX.X, negate=True)
                ins = v.tensor_scalar(nmax[:], nmax[:], 0.125, None,
                                      op0=ALU.mult)
                del ins

            @block.scalar
            def _(s):
                s = cx.ser_s(s)
                cx.wait(s, cx.vv, cx.mm)
                ins = s.activation(pstrip[:, 0:kw], strip[:, 0:kw], AF.Exp,
                                   bias=nmax[:, 0:1], scale=0.125,
                                   accum_out=rsum[:])
                del ins

            @block.vector
            def _(v):
                v = cx.ser_v(v)
                cx.wait(v, cx.ev)
                v.reciprocal(nmax[:], rsum[:])
                ins = v.tensor_scalar(pstrip[:, 0:kw], pstrip[:, 0:kw],
                                      nmax[:, 0:1], None, op0=ALU.mult)
                del ins

            nkt = qi + 1

            @block.tensor
            def _(t):
                cx.wait(t, cx.vv)
                bo, pso = cx.bank(t)
                for g0 in range(0, nkt, 16):
                    gn = min(16, nkt - g0)
                    for u0 in range(0, gn, 8):
                        un = min(8, gn - u0)
                        bb, psb = cx.bankb(t)
                        for u in range(un):
                            kt = g0 + u0 + u
                            ins = t.transpose(
                                psb[:, u * 128:(u + 1) * 128],
                                pstrip[:, kt * 128:(kt + 1) * 128], idb[:])
                        cx.inc(ins, cx.mm)

                        @block.vector
                        def _(v):
                            v = cx.ser_v(v)
                            cx.wait(v, cx.mm)
                            ins = v.tensor_copy(
                                ptbig[:, (u0) * 128:(u0 + un) * 128],
                                psb[:, 0:un * 128])
                            del ins
                        cx.bankb_freed(bb, cx.vv)
                    cx.wait(t, cx.vv)
                    for u in range(gn):
                        kt = g0 + u
                        ins = t.matmul(pso[0:64, 0:128],
                                       vTM[:, kt * 64:(kt + 1) * 64],
                                       ptbig[:, u * 128:(u + 1) * 128],
                                       start=(kt == 0), stop=(kt == nkt - 1))
                    cx.inc(ins, cx.mm)

                @block.scalar
                def _(s):
                    s = cx.ser_s(s)
                    cx.wait(s, cx.mm)
                    hp = h % 2
                    ins = s.activation(
                        oTh[hp * 64:hp * 64 + 64,
                            (h // 2) * S + qi * 128:(h // 2) * S + (qi + 1) * 128],
                        pso[0:64, 0:128], AF.Copy)
                    del ins
                cx.bank_freed(bo, cx.ev)

    cx.free_sbuf("wqkv", "bqkv", "qTa", "qTb", "kTa", "kTb", "pra", "prb",
                 "rta", "rtb", "vTM", "vTs", "strip", "pstrip", "ptbig",
                 "rsum", "nmax")


def emit_headsum_proj(cx, block, wo_dram, oTh, out_dram, nheads, bias_sb=None):
    H = nheads
    nci = H * 64 // 128
    wo_sb = cx.sbuf("wo_sb", (128, nci * D), BF16)
    stg = cx.sbuf("stg", (128, 2 * 512), F32)

    @block.gpsimd
    def _(g):
        g = cx.ser_g(g)
        wload(g, cx, wo_sb[:], wo_dram, wo_dram.shape[0] // 128)

    for po in range(8):
        for c in range(8):
            c0 = c * 512

            @block.tensor
            def _(t):
                cx.wait(t, cx.ld, cx.lg, cx.ev)
                bi, ps = cx.bank(t)
                for ci in range(nci):
                    lhs = wo_sb[:, ci * D + po * 128:ci * D + (po + 1) * 128]
                    ins = t.matmul(ps[:], lhs,
                                   oTh[:, ci * S + c0:ci * S + c0 + 512],
                                   start=(ci == 0), stop=(ci == nci - 1))
                cx.inc(ins, cx.mm)
                sslot = (po * 8 + c) % 2

                @block.scalar
                def _(s):
                    s = cx.ser_s(s)
                    cx.wait(s, cx.mm, cx.st)
                    kw = dict(bias=bias_sb[:, po:po + 1]) \
                        if bias_sb is not None else {}
                    ins = s.activation(stg[:, sslot * 512:(sslot + 1) * 512],
                                       ps[:], AF.Identity if kw else AF.Copy,
                                       **kw)
                    del ins
                cx.bank_freed(bi, cx.ev)

                @block.sync
                def _(sy):
                    cx.wait(sy, cx.ev)
                    ins = sy.dma_start(
                        out_dram[po * 128:(po + 1) * 128, c0:c0 + 512],
                        stg[:, sslot * 512:(sslot + 1) * 512])
                    cx.inc(ins, cx.st, 16)
                    sy.wait_ge(cx.st, cx.val(cx.st))

    cx.free_sbuf("wo_sb", "stg")



def build_L0():
    nc = bass.Bass("TRN2", target_bir_lowering=False)
    H = 8
    NU = 80
    HID = NU * 256

    f32in = [
        ("xful", [D, S]), ("xown", [D, TOWN]),
        ("wq", [D, H * 64]), ("wk", [D, H * 64]), ("wv", [D, H * 64]),
        ("bq", [H * 64, 1]), ("bk", [H * 64, 1]), ("bv", [H * 64, 1]),
        ("wo", [H * 64, D]),
        ("cos", [32, S]), ("sin", [32, S]),
        ("cmask", [128, 4 * 512]),
        ("rw", [D, 64]), ("rb", [64, 1]),
        ("w1big", [D, HID]), ("b1big", [HID, 1]),
        ("w2big", [HID, D]),
        ("biasmat", [128, D]),
        ("idf", [128, 128]),
    ]
    bf16in = [("idb", [128, 128]), ("onesln", [128, 128])]
    inp = {}
    for name, shape in f32in:
        inp[name] = nc.dram_tensor(name, shape, F32, kind="ExternalInput").ap()
    for name, shape in bf16in:
        inp[name] = nc.dram_tensor(name, shape, BF16, kind="ExternalInput").ap()
    attn_out = nc.dram_tensor("attn_out", [D, S], F32,
                              kind="ExternalOutput").ap()
    xffn_out = nc.dram_tensor("xffn_out", [D, TOWN], F32,
                              kind="ExternalOutput").ap()

    cx = Ctx(nc)
    with nc.Block() as block:
        cx.block = block
        onesln = cx.sbuf("onesln", (128, 128), BF16)
        idf = cx.sbuf("idf", (128, 128), F32)
        idb = cx.sbuf("idb", (128, 128), BF16)
        cosb = cx.sbuf("cosb", (32, S), BF16)
        sinb = cx.sbuf("sinb", (32, S), BF16)
        cmask = cx.sbuf("cmask", (128, 4 * 512), F32)

        @block.sync
        def _(sy):
            for dst, src in [(onesln, inp["onesln"]), (idf, inp["idf"]),
                             (idb, inp["idb"]), (cmask, inp["cmask"])]:
                cx.inc(sy.dma_start(dst[:], src), cx.ld, 16)
                sy.wait_ge(cx.ld, cx.val(cx.ld))

        @block.gpsimd
        def _(g):
            g = cx.ser_g(g)
            for dst, src in [(cosb, inp["cos"]), (sinb, inp["sin"])]:
                cx.inc(g.dma_start(dst[:], src), cx.lg, 16)
                g.wait_ge(cx.lg, cx.val(cx.lg))

        consts = {"ones": onesln, "idb": idb, "cos": cosb, "sin": sinb,
                  "cmask": cmask}

        oTh = cx.sbuf("oTh", (128, 4 * S), BF16)
        xn = cx.sbuf("xn", (128, 8 * S), BF16)
        emit_ln(cx, block, inp["xful"], xn, S, onesln)

        def xn_t(i, c0, w):
            return xn[:, i * S + c0:i * S + c0 + w]

        cx.fence(block)
        emit_causal_mha(cx, block, inp, xn_t, oTh, H, consts)
        cx.fence(block)
        cx.free_sbuf("xn")
        emit_headsum_proj(cx, block, inp["wo"], oTh, attn_out, H)
        cx.free_sbuf("oTh")
        cx.fence(block)

        xno = cx.sbuf("xno", (128, 8 * TOWN), BF16)
        gte = cx.sbuf("gte", (128, TOWN), BF16)
        rw_sb = cx.sbuf("rw_sb", (128, 8 * 64), F32)
        rb_sb = cx.sbuf("rb_sb", (64, 1), F32)
        ltT = cx.sbuf("ltT", (64, TOWN), F32)

        @block.sync
        def _(sy):
            cx.inc(sy.dma_start(rw_sb[:].rearrange("p (c o) -> p c o", c=8),
                                inp["rw"].rearrange("(c p) o -> p c o", p=128)),
                   cx.ld, 16)
            sy.wait_ge(cx.ld, cx.val(cx.ld))
            cx.inc(sy.dma_start(rb_sb[:], inp["rb"]), cx.ld, 16)
            sy.wait_ge(cx.ld, cx.val(cx.ld))

        def router_tap(c0, xnf):
            @block.tensor
            def _(t):
                cx.wait(t, cx.ld, cx.vv)
                bi, ps = cx.bank(t)
                for ci in range(8):
                    ins = t.matmul(ps[0:64, :], rw_sb[:, ci * 64:(ci + 1) * 64],
                                   xnf[:, ci * 512:(ci + 1) * 512],
                                   start=(ci == 0), stop=(ci == 7))
                cx.inc(ins, cx.mm)

                @block.scalar
                def _(s):
                    s = cx.ser_s(s)
                    cx.wait(s, cx.mm)
                    ins = s.activation(ltT[:, c0:c0 + 512], ps[0:64, :],
                                       AF.Identity, bias=rb_sb[:, 0:1])
                    del ins
                cx.bank_freed(bi, cx.ev)

        emit_ln(cx, block, inp["xown"], xno, TOWN, onesln, tap=router_tap)

        def xno_t(i, c0, w):
            return xno[:, i * TOWN + c0:i * TOWN + c0 + w]

        cx.fence(block)


        ltok = cx.sbuf("ltok", (128, 16 * 64), F32)
        gtok = cx.sbuf("gtok", (128, 16 * 64), BF16)
        cur = cx.sbuf("cur", (128, 64), F32)
        sel = cx.sbuf("sel", (128, 64), F32)
        msk = cx.sbuf("msk", (128, 64), F32)
        ee = cx.sbuf("ee", (128, 64), F32)
        mx = cx.sbuf("mx", (128, 1), F32)

        @block.tensor
        def _(t):
            cx.wait(t, cx.ev)
            for kt in range(16):
                bi, ps = cx.bank(t)
                ins = t.transpose(ps[:, 0:64], ltT[:, kt * 128:(kt + 1) * 128],
                                  idf[0:64, 0:64])
                cx.inc(ins, cx.mm)

                @block.vector
                def _(v):
                    v = cx.ser_v(v)
                    cx.wait(v, cx.mm)
                    ins = v.tensor_copy(ltok[:, kt * 64:(kt + 1) * 64],
                                        ps[:, 0:64])
                    del ins
                cx.bank_freed(bi, cx.vv)

        for kt in range(16):
            lt = ltok[:, kt * 64:(kt + 1) * 64]

            @block.vector
            def _(v):
                v = cx.ser_v(v)
                cx.wait(v, cx.vv, cx.ev)
                v.tensor_copy(cur[:], lt)
                ins = v.reduce_max(mx[:], lt, axis=AX.X, negate=True)
                del ins

            @block.scalar
            def _(s):
                s = cx.ser_s(s)
                cx.wait(s, cx.vv)
                ins = s.activation(ee[:], lt, AF.Exp, bias=mx[:, 0:1])
                del ins

            @block.vector
            def _(v):
                v = cx.ser_v(v)
                cx.wait(v, cx.ev)
                v.memset(sel[:], 0.0)
                for k in range(8):
                    v.reduce_max(mx[:], cur[:], axis=AX.X)
                    v.tensor_scalar(msk[:], cur[:], mx[:, 0:1], None,
                                    op0=ALU.is_equal)
                    v.tensor_tensor(sel[:], sel[:], msk[:], op=ALU.add)
                    v.scalar_tensor_tensor(cur[:], msk[:], NEG, cur[:],
                                           op0=ALU.mult, op1=ALU.add)
                v.tensor_tensor(msk[:], sel[:], ee[:], op=ALU.mult)
                v.reduce_sum(mx[:], msk[:], axis=AX.X)
                v.reciprocal(mx[:], mx[:])
                ins = v.tensor_scalar(gtok[:, kt * 64:(kt + 1) * 64], msk[:],
                                      mx[:, 0:1], None, op0=ALU.mult)
                del ins

        @block.vector
        def _(v):
            v = cx.ser_v(v)
            cx.wait(v, cx.vv)
            v.memset(gte[64:128, :], 0.0)
            ins = v.memset(gte[64:67, :], 1.0)
            del ins

        @block.tensor
        def _(t):
            cx.wait(t, cx.vv)
            for kt in range(16):
                bi, psb = cx.bankb(t)
                ins = t.transpose(psb[0:64, 0:128],
                                  gtok[:, kt * 64:(kt + 1) * 64], idb[:])
                cx.inc(ins, cx.mm)

                @block.scalar
                def _(s):
                    s = cx.ser_s(s)
                    cx.wait(s, cx.mm)
                    ins = s.activation(gte[0:64, kt * 128:(kt + 1) * 128],
                                       psb[0:64, 0:128], AF.Copy)
                    del ins
                cx.bankb_freed(bi, cx.ev)

        cx.free_sbuf("ltT", "ltok", "gtok", "cur", "sel", "msk", "ee", "mx")
        cx.fence(block)

        bm_sb = cx.sbuf("bm_sb", (128, D), BF16)
        w1sb = cx.sbuf("w1sb", (128, 2 * 2048), BF16)
        b1sb = cx.sbuf("b1sb", (128, 4), F32)
        grow = cx.sbuf("grow", (1, TOWN), BF16)
        gbc = cx.sbuf("gbc", (128, 1024), BF16)
        mid = cx.sbuf("mid", (128, 2 * 1024), BF16)
        mblk = cx.sbuf("mblk", (128, 20 * 1024), BF16)
        w2sb = cx.sbuf("w2sb", (128, 2 * 20 * 128), BF16)
        outacc = cx.sbuf("outacc", (128, 8 * 1024), F32)
        xot = cx.sbuf("xot", (128, 1024), F32)
        w1_used = {}
        w2_used = {}

        @block.gpsimd
        def _(g):
            g = cx.ser_g(g)
            ins = g.dma_start(bm_sb[:], inp["biasmat"])
            cx.inc(ins, cx.lg, 16)

        for hf in range(2):
            th0 = hf * 1024
            for blk in range(8):
                units = range(10 * blk, 10 * blk + 10)
                for iu, u in enumerate(units):
                    slot = u % 2
                    gated = u < 64

                    @block.gpsimd
                    def _(g):
                        g = cx.ser_g(g)
                        if slot in w1_used:
                            g.wait_ge(cx.mm, w1_used[slot])
                        wload(g, cx, w1sb[:, slot * 2048:(slot + 1) * 2048],
                              inp["w1big"][:, u * 256:(u + 1) * 256], 8)

                    @block.sync
                    def _(sy):
                        cx.wait(sy, cx.ev)
                        for j in range(2):
                            cx.inc(sy.dma_start(
                                b1sb[:, slot * 2 + j:slot * 2 + j + 1],
                                inp["b1big"][u * 256 + j * 128:
                                             u * 256 + (j + 1) * 128, :]),
                                cx.ld, 16)
                            sy.wait_ge(cx.ld, cx.val(cx.ld))
                        if gated:
                            cx.wait(sy, cx.vv, cx.ev)
                            ins = sy.dma_start(grow[:, :], gte[u:u + 1, :])
                            cx.inc(ins, cx.ld, 16)
                            sy.wait_ge(cx.ld, cx.val(cx.ld))

                    if gated:
                        @block.tensor
                        def _(t):
                            cx.wait(t, cx.ld, cx.lg)
                            for c2 in range(2):
                                bi, ps = cx.bank(t)
                                ins = t.matmul(
                                    ps[:],
                                    onesln[0:1, :],
                                    grow[:, th0 + c2 * 512:th0 + (c2 + 1) * 512],
                                    start=True, stop=True)
                                cx.inc(ins, cx.mm)

                                @block.scalar
                                def _(s):
                                    s = cx.ser_s(s)
                                    cx.wait(s, cx.mm)
                                    ins = s.activation(
                                        gbc[:, c2 * 512:(c2 + 1) * 512],
                                        ps[:], AF.Copy, scale=1024.0)
                                    del ins
                                cx.bank_freed(bi, cx.ev)

                    ti0 = 2 * (u - 10 * blk)
                    for po2 in range(2):
                        @block.tensor
                        def _(t):
                            cx.wait(t, cx.ld, cx.lg)
                            for c2 in range(2):
                                bi, ps = cx.bank(t)
                                for ci in range(8):
                                    ins = t.matmul(
                                        ps[:],
                                        w1sb[:, slot * 2048 + ci * 256 +
                                             po2 * 128:slot * 2048 + ci * 256 +
                                             po2 * 128 + 128],
                                        xno_t(ci, th0 + c2 * 512, 512),
                                        start=(ci == 0), stop=(ci == 7))
                                cx.inc(ins, cx.mm)
                                if po2 == 1 and c2 == 1:
                                    w1_used[slot] = cx.val(cx.mm)

                                @block.scalar
                                def _(s):
                                    s = cx.ser_s(s)
                                    cx.wait(s, cx.mm)
                                    dst = (mblk if not gated else mid)
                                    col = ((ti0 + po2) * 1024 + c2 * 512) if \
                                        not gated else (po2 * 1024 + c2 * 512)
                                    ins = s.activation(
                                        dst[:, col:col + 512], ps[:], AF.Silu,
                                        bias=b1sb[:, slot * 2 + po2:
                                                  slot * 2 + po2 + 1])
                                    del ins
                                cx.bank_freed(bi, cx.ev)

                    if gated:
                        @block.vector
                        def _(v):
                            v = cx.ser_v(v)
                            cx.wait(v, cx.ev)
                            for po2 in range(2):
                                ins = v.tensor_tensor(
                                    mblk[:, (ti0 + po2) * 1024:
                                         (ti0 + po2 + 1) * 1024],
                                    mid[:, po2 * 1024:(po2 + 1) * 1024],
                                    gbc[:, 0:1024], op=ALU.mult)
                            del ins

                for po in range(8):
                    pslot = po % 2

                    @block.gpsimd
                    def _(g):
                        g = cx.ser_g(g)
                        if pslot in w2_used:
                            g.wait_ge(cx.mm, w2_used[pslot])
                        src = inp["w2big"][blk * 2560:(blk + 1) * 2560,
                                           po * 128:(po + 1) * 128]
                        wload(g, cx, w2sb[:, pslot * 2560:(pslot + 1) * 2560],
                              src, 20)

                    for c2 in range(2):
                        @block.tensor
                        def _(t):
                            cx.wait(t, cx.ld, cx.lg, cx.vv, cx.ev)
                            bi, ps = cx.bank(t)
                            first = True
                            if blk == 0:
                                t.matmul(ps[:], bm_sb[:, po * 128:(po + 1) * 128],
                                         gte[:, th0 + c2 * 512:
                                             th0 + (c2 + 1) * 512],
                                         start=True, stop=False)
                                first = False
                            for ti in range(20):
                                ins = t.matmul(
                                    ps[:],
                                    w2sb[:, pslot * 2560 + ti * 128:
                                         pslot * 2560 + (ti + 1) * 128],
                                    mblk[:, ti * 1024 + c2 * 512:
                                         ti * 1024 + c2 * 512 + 512],
                                    start=first and ti == 0, stop=(ti == 19))
                                first = False
                            cx.inc(ins, cx.mm)
                            if c2 == 1:
                                w2_used[pslot] = cx.val(cx.mm)

                            @block.vector
                            def _(v):
                                v = cx.ser_v(v)
                                cx.wait(v, cx.mm, cx.st)
                                dst = outacc[:, po * 1024 + c2 * 512:
                                             po * 1024 + c2 * 512 + 512]
                                if blk == 0:
                                    ins = v.tensor_copy(dst, ps[:])
                                else:
                                    ins = v.tensor_tensor(dst, dst, ps[:],
                                                          op=ALU.add)
                                del ins
                            cx.bank_freed(bi, cx.vv)

            for po in range(8):
                @block.sync
                def _(sy):
                    cx.wait(sy, cx.vv)
                    ins = sy.dma_start(xot[:],
                                       inp["xown"][po * 128:(po + 1) * 128,
                                                   th0:th0 + 1024])
                    cx.inc(ins, cx.ld, 16)
                    sy.wait_ge(cx.ld, cx.val(cx.ld))

                @block.vector
                def _(v):
                    v = cx.ser_v(v)
                    cx.wait(v, cx.ld, cx.lg, cx.vv)
                    ins = v.tensor_tensor(outacc[:, po * 1024:(po + 1) * 1024],
                                          outacc[:, po * 1024:(po + 1) * 1024],
                                          xot[:], op=ALU.add)
                    del ins

                @block.sync
                def _(sy):
                    cx.wait(sy, cx.vv)
                    ins = sy.dma_start(
                        xffn_out[po * 128:(po + 1) * 128, th0:th0 + 1024],
                        outacc[:, po * 1024:(po + 1) * 1024])
                    cx.inc(ins, cx.st, 16)
                    sy.wait_ge(cx.st, cx.val(cx.st))

        @block.sync
        def _(sy):
            cx.wait(sy, cx.st)

    return nc, cx, inp




def build_L1():
    nc = bass.Bass("TRN2", target_bir_lowering=False)
    f32in = [
        ("pA", [D, S]), ("pB", [D, S]), ("pC", [D, S]),
        ("pAo", [D, TOWN]), ("pBo", [D, TOWN]), ("pCo", [D, TOWN]),
        ("wrkv", [D, 1536]), ("brkv", [1536, 1]),
        ("nul", [512, 1]), ("gal", [512, 1]),
        ("wo", [512, D]), ("bo", [D, 1]),
        ("w1T", [D, 4096]), ("w2T", [D, 4096]),
        ("h1T", [4096, 4096]), ("h2T", [4096, 4096]),
        ("w3T", [4096, D]),
        ("s1", [128, 32]), ("s2", [128, 32]), ("sh1", [128, 32]),
        ("sh2", [128, 32]), ("s3", [128, 8]),
    ]
    inp = {}
    for name, shape in f32in:
        inp[name] = nc.dram_tensor(name, shape, F32, kind="ExternalInput").ap()
    inp["onesln"] = nc.dram_tensor("onesln", [128, 128], BF16,
                                   kind="ExternalInput").ap()
    attn_out = nc.dram_tensor("attn_out", [D, S], F32,
                              kind="ExternalOutput").ap()
    xffn_out = nc.dram_tensor("xffn_out", [D, TOWN], F32,
                              kind="ExternalOutput").ap()

    cx = Ctx(nc)
    with nc.Block() as block:
        cx.block = block
        onesln = cx.sbuf("onesln", (128, 128), BF16)

        @block.sync
        def _(sy):
            cx.inc(sy.dma_start(onesln[:], inp["onesln"]), cx.ld, 16)
            sy.wait_ge(cx.ld, cx.val(cx.ld))

        xn = cx.sbuf("xn", (128, 8 * S), BF16)
        emit_ln(cx, block, inp["pA"], xn, S, onesln,
                extra=(inp["pB"], inp["pC"]))

        def xn_t(i, c0, w):
            return xn[:, i * S + c0:i * S + c0 + w]

        cx.fence(block)

        wrkv = cx.sbuf("wrkv", (128, 8 * 1536), BF16)
        brkv = cx.sbuf("brkv", (128, 12), F32)
        nlg = cx.sbuf("nlg", (128, 16), F32)
        xrT = cx.sbuf("xrT", (128, 4 * S), BF16)
        rbf = cx.sbuf("rbf", (128, S), BF16)
        kbf = cx.sbuf("kbf", (128, S), BF16)
        vbf = cx.sbuf("vbf", (128, S), BF16)
        u1 = cx.sbuf("u1", (128, S), F32)
        u2 = cx.sbuf("u2", (128, S), F32)

        @block.gpsimd
        def _(g):
            g = cx.ser_g(g)
            wload(g, cx, wrkv[:], inp["wrkv"], 8)

        @block.sync
        def _(sy):
            for j in range(12):
                cx.inc(sy.dma_start(brkv[:, j:j + 1],
                                    inp["brkv"][j * 128:(j + 1) * 128, :]),
                       cx.ld, 16)
                sy.wait_ge(cx.ld, cx.val(cx.ld))
            for j in range(4):
                cx.inc(sy.dma_start(nlg[:, j:j + 1],
                                    inp["nul"][j * 128:(j + 1) * 128, :]),
                       cx.ld, 16)
                sy.wait_ge(cx.ld, cx.val(cx.ld))
                cx.inc(sy.dma_start(nlg[:, 4 + j:5 + j],
                                    inp["gal"][j * 128:(j + 1) * 128, :]),
                       cx.ld, 16)
                sy.wait_ge(cx.ld, cx.val(cx.ld))

        @block.scalar
        def _(s):
            s = cx.ser_s(s)
            cx.wait(s, cx.ld)
            s.activation(nlg[:, 8:12], nlg[:, 0:4], AF.Exp)

        @block.vector
        def _(v):
            v = cx.ser_v(v)
            cx.wait(v, cx.ev, cx.ld)
            v.tensor_scalar(nlg[:, 8:12], nlg[:, 8:12], -1.0, None,
                            op0=ALU.mult)
            v.tensor_tensor(nlg[:, 12:16], nlg[:, 0:4], nlg[:, 4:8],
                            op=ALU.add)

        @block.scalar
        def _(s):
            s = cx.ser_s(s)
            cx.wait(s, cx.vv)
            s.activation(nlg[:, 8:12], nlg[:, 8:12], AF.Exp)
            s.activation(nlg[:, 12:16], nlg[:, 12:16], AF.Exp)

        @block.vector
        def _(v):
            v = cx.ser_v(v)
            cx.wait(v, cx.ev)
            v.tensor_scalar(nlg[:, 12:16], nlg[:, 12:16], -1.0, None,
                            op0=ALU.add)

        for j in range(4):
            for blk_i, (dst, func, bcol) in enumerate(
                    ((rbf, AF.Sigmoid, j), (kbf, AF.Exp, 4 + j),
                     (vbf, AF.Identity, 8 + j))):
                for c in range(8):
                    c0 = c * 512

                    @block.tensor
                    def _(t):
                        cx.wait(t, cx.lg, cx.ld)
                        if c == 0:
                            cx.wait(t, cx.vv, cx.ev)
                        bi, ps = cx.bank(t)
                        col0 = (blk_i * 4 + j) * 128
                        for ci in range(8):
                            ins = t.matmul(ps[:],
                                           wrkv[:, ci * 1536 + col0:
                                                ci * 1536 + col0 + 128],
                                           xn_t(ci, c0, 512),
                                           start=(ci == 0), stop=(ci == 7))
                        cx.inc(ins, cx.mm)

                        @block.scalar
                        def _(s):
                            s = cx.ser_s(s)
                            cx.wait(s, cx.mm)
                            s.activation(dst[:, c0:c0 + 512], ps[:], func,
                                         bias=brkv[:, bcol:bcol + 1])
                        cx.bank_freed(bi, cx.ev)

            @block.vector
            def _(v):
                v = cx.ser_v(v)
                cx.wait(v, cx.ev)
                lamap = nlg[:, 8 + j:9 + j].broadcast_to((128, S))
                gamc = nlg[:, 12 + j:13 + j]
                v.tensor_tensor(vbf[:], kbf[:], vbf[:], op=ALU.mult)
                v.tensor_tensor_scan(u1[:], lamap, vbf[:], 0.0,
                                     op0=ALU.mult, op1=ALU.add)
                v.tensor_tensor_scan(u2[:], lamap, kbf[:], 0.0,
                                     op0=ALU.mult, op1=ALU.add)
                v.scalar_tensor_tensor(u1[:], vbf[:], gamc, u1[:],
                                       op0=ALU.mult, op1=ALU.add)
                v.scalar_tensor_tensor(u2[:], kbf[:], gamc, u2[:],
                                       op0=ALU.mult, op1=ALU.add)
                v.tensor_scalar(u2[:], u2[:], 1e-6, None, op0=ALU.add)
                v.reciprocal(u2[:], u2[:])
                v.tensor_tensor(u1[:], u1[:], u2[:], op=ALU.mult)
                v.tensor_tensor(xrT[:, j * S:(j + 1) * S], u1[:], rbf[:],
                                op=ALU.mult)

        cx.fence(block)
        cx.free_sbuf("rbf", "kbf", "vbf", "u1", "u2")

        wo_sb = cx.sbuf("wo_sb", (128, 4 * D), BF16)
        bo_sb = cx.sbuf("bo_sb", (128, 8), F32)
        stg = cx.sbuf("stg", (128, 2 * 512), F32)

        @block.gpsimd
        def _(g):
            g = cx.ser_g(g)
            wload(g, cx, wo_sb[:], inp["wo"], 4)

        @block.sync
        def _(sy):
            for po in range(8):
                cx.inc(sy.dma_start(bo_sb[:, po:po + 1],
                                    inp["bo"][po * 128:(po + 1) * 128, :]),
                       cx.ld, 16)
                sy.wait_ge(cx.ld, cx.val(cx.ld))

        for po in range(8):
            for c in range(8):
                c0 = c * 512

                @block.tensor
                def _(t):
                    cx.wait(t, cx.lg, cx.ld, cx.vv)
                    bi, ps = cx.bank(t)
                    for j in range(4):
                        ins = t.matmul(ps[:],
                                       wo_sb[:, j * D + po * 128:
                                             j * D + (po + 1) * 128],
                                       xrT[:, j * S + c0:j * S + c0 + 512],
                                       start=(j == 0), stop=(j == 3))
                    cx.inc(ins, cx.mm)
                    sslot = (po * 8 + c) % 2

                    @block.scalar
                    def _(s):
                        s = cx.ser_s(s)
                        cx.wait(s, cx.mm, cx.st)
                        s.activation(stg[:, sslot * 512:(sslot + 1) * 512],
                                     ps[:], AF.Identity,
                                     bias=bo_sb[:, po:po + 1])
                    cx.bank_freed(bi, cx.ev)

                    @block.sync
                    def _(sy):
                        cx.wait(sy, cx.ev)
                        cx.inc(sy.dma_start(
                            attn_out[po * 128:(po + 1) * 128, c0:c0 + 512],
                            stg[:, sslot * 512:(sslot + 1) * 512]), cx.st, 16)
                        sy.wait_ge(cx.st, cx.val(cx.st))

        cx.fence(block)
        cx.free_sbuf("xn", "wrkv", "brkv", "nlg", "xrT", "wo_sb", "bo_sb",
                     "stg")

        xno = cx.sbuf("xno", (128, 8 * TOWN), BF16)
        emit_ln(cx, block, inp["pAo"], xno, TOWN, onesln,
                extra=(inp["pBo"], inp["pCo"]))

        def xno_t(i, c0, w):
            return xno[:, i * TOWN + c0:i * TOWN + c0 + w]

        cx.fence(block)

        h1b = cx.sbuf("h1b", (128, 32 * 1024), BF16)
        h2b = cx.sbuf("h2b", (128, 32 * 1024), BF16)
        scl = cx.sbuf("scl", (128, 4 * 32 + 8), F32)
        wst = cx.sbuf("wst", (128, 2 * 32 * 128), BF16)
        stg2 = cx.sbuf("stg2", (128, 2 * 512), F32)
        xpt = cx.sbuf("xpt", (128, 512), F32)
        w_used = {}

        @block.sync
        def _(sy):
            for si, nm in enumerate(("s1", "s2", "sh1", "sh2")):
                cx.inc(sy.dma_start(scl[:, si * 32:(si + 1) * 32], inp[nm]),
                       cx.ld, 16)
                sy.wait_ge(cx.ld, cx.val(cx.ld))
            cx.inc(sy.dma_start(scl[:, 128:136], inp["s3"]), cx.ld, 16)
            sy.wait_ge(cx.ld, cx.val(cx.ld))

        for hf in range(2):
            th0 = hf * 1024

            def stage(wname, nci, npo, sccol, func, dstbuf, srcfn, scband):
                for po in range(npo):
                    slot = po % 2

                    @block.gpsimd
                    def _(g):
                        g = cx.ser_g(g)
                        if slot in w_used:
                            g.wait_ge(cx.mm, w_used[slot])
                        wload(g, cx, wst[:, slot * 4096:slot * 4096 + nci * 128],
                              inp[wname][:, po * 128:(po + 1) * 128], nci)

                    for ch in range(2):
                        @block.tensor
                        def _(t):
                            cx.wait(t, cx.lg, cx.ld, cx.vv, cx.ev)
                            bi, ps = cx.bank(t)
                            for ci in range(nci):
                                ins = t.matmul(
                                    ps[:],
                                    wst[:, slot * 4096 + ci * 128:
                                        slot * 4096 + (ci + 1) * 128],
                                    srcfn(ci, th0 + ch * 512, 512),
                                    start=(ci == 0), stop=(ci == nci - 1))
                            cx.inc(ins, cx.mm)
                            if ch == 1:
                                w_used[slot] = cx.val(cx.mm)

                            @block.scalar
                            def _(s):
                                s = cx.ser_s(s)
                                cx.wait(s, cx.mm)
                                s.activation(
                                    dstbuf[:, po * 1024 + ch * 512:
                                           po * 1024 + ch * 512 + 512],
                                    ps[:], func,
                                    scale=scl[:, scband + po:scband + po + 1])
                            cx.bank_freed(bi, cx.ev)

            def h1b_t(ci, c0, w):
                return h1b[:, ci * 1024 + (c0 - th0):ci * 1024 + (c0 - th0) + w]

            stage("w1T", 8, 32, 0, AF.Silu, h1b, xno_t, 0)
            stage("w2T", 8, 32, 32, AF.Identity, h2b, xno_t, 32)

            @block.vector
            def _(v):
                v = cx.ser_v(v)
                cx.wait(v, cx.ev)
                for po in range(32):
                    v.tensor_tensor(h1b[:, po * 1024:(po + 1) * 1024],
                                    h1b[:, po * 1024:(po + 1) * 1024],
                                    h2b[:, po * 1024:(po + 1) * 1024],
                                    op=ALU.mult)

            stage("h1T", 32, 32, 64, AF.Gelu, h2b, h1b_t, 64)

            def h2b_t(ci, c0, w):
                return h2b[:, ci * 1024 + (c0 - th0):ci * 1024 + (c0 - th0) + w]

            stage("h2T", 32, 32, 96, AF.Gelu, h1b, h2b_t, 96)

            for po in range(8):
                slot = po % 2

                @block.gpsimd
                def _(g):
                    g = cx.ser_g(g)
                    if slot in w_used:
                        g.wait_ge(cx.mm, w_used[slot])
                    wload(g, cx, wst[:, slot * 4096:slot * 4096 + 32 * 128],
                          inp["w3T"][:, po * 128:(po + 1) * 128], 32)

                for ch in range(2):
                    @block.tensor
                    def _(t):
                        cx.wait(t, cx.lg, cx.ld, cx.vv, cx.ev)
                        bi, ps = cx.bank(t)
                        for ci in range(32):
                            ins = t.matmul(
                                ps[:],
                                wst[:, slot * 4096 + ci * 128:
                                    slot * 4096 + (ci + 1) * 128],
                                h1b_t(ci, th0 + ch * 512, 512),
                                start=(ci == 0), stop=(ci == 31))
                        cx.inc(ins, cx.mm)
                        if ch == 1:
                            w_used[slot] = cx.val(cx.mm)
                        sslot = (po * 2 + ch) % 2

                        @block.scalar
                        def _(s):
                            s = cx.ser_s(s)
                            cx.wait(s, cx.mm, cx.st, cx.vv)
                            s.activation(stg2[:, sslot * 512:(sslot + 1) * 512],
                                         ps[:], AF.Identity,
                                         scale=scl[:, 128 + po:129 + po])
                        cx.bank_freed(bi, cx.ev)

                        for src in ("pAo", "pBo", "pCo"):
                            @block.sync
                            def _(sy):
                                cx.wait(sy, cx.vv, cx.ev)
                                cx.inc(sy.dma_start(
                                    xpt[:],
                                    inp[src][po * 128:(po + 1) * 128,
                                             th0 + ch * 512:th0 + ch * 512 + 512]),
                                    cx.ld, 16)
                                sy.wait_ge(cx.ld, cx.val(cx.ld))

                            @block.vector
                            def _(v):
                                v = cx.ser_v(v)
                                cx.wait(v, cx.ld, cx.ev)
                                v.tensor_tensor(
                                    stg2[:, sslot * 512:(sslot + 1) * 512],
                                    stg2[:, sslot * 512:(sslot + 1) * 512],
                                    xpt[:], op=ALU.add)

                        @block.sync
                        def _(sy):
                            cx.wait(sy, cx.vv)
                            cx.inc(sy.dma_start(
                                xffn_out[po * 128:(po + 1) * 128,
                                         th0 + ch * 512:th0 + ch * 512 + 512],
                                stg2[:, sslot * 512:(sslot + 1) * 512]),
                                cx.st, 16)
                            sy.wait_ge(cx.st, cx.val(cx.st))

        @block.sync
        def _(sy):
            cx.wait(sy, cx.st)

    return nc, cx, inp


def host_prep_L1(I, pA, pB, pC):
    Wr = np.asarray(I["L1_attn_W_rkv"], np.float32)
    br = np.asarray(I["L1_attn_b_rkv"], np.float32)
    Wo = np.asarray(I["L1_attn_W_o"], np.float32)
    bo = np.asarray(I["L1_attn_b_o"], np.float32)
    nul = np.asarray(I["L1_attn_nu_log"], np.float32)
    gal = np.asarray(I["L1_attn_gamma_log"], np.float32)
    onesln = np.full((128, 128), 1.0 / 1024.0, np.float32).astype(bf16)
    ffn = {
        "w1T": f32c(np.asarray(I["L1_ffn_w1_w"]).T),
        "w2T": f32c(np.asarray(I["L1_ffn_w2_w"]).T),
        "h1T": f32c(np.asarray(I["L1_ffn_h1_w"]).T),
        "h2T": f32c(np.asarray(I["L1_ffn_h2_w"]).T),
        "w3T": f32c(np.asarray(I["L1_ffn_w3_w"]).T),
        "s1": f32c(np.asarray(I["L1_ffn_w1_s"]).reshape(32, 128).T),
        "s2": f32c(np.asarray(I["L1_ffn_w2_s"]).reshape(32, 128).T),
        "sh1": f32c(np.asarray(I["L1_ffn_h1_s"]).reshape(32, 128).T),
        "sh2": f32c(np.asarray(I["L1_ffn_h2_s"]).reshape(32, 128).T),
        "s3": f32c(np.asarray(I["L1_ffn_w3_s"]).reshape(8, 128).T),
    }
    maps = []
    for c in range(8):
        b, p = divmod(c, 2)
        ph = slice(p * 512, (p + 1) * 512)
        own = slice(p * TOWN, (p + 1) * TOWN)
        m = {
            "pA": pA[b], "pB": pB[b], "pC": pC[b],
            "pAo": f32c(pA[b][:, own]), "pBo": f32c(pB[b][:, own]),
            "pCo": f32c(pC[b][:, own]),
            "wrkv": f32c(np.concatenate(
                [Wr[:, 0:D][:, ph], Wr[:, D:2 * D][:, ph],
                 Wr[:, 2 * D:3 * D][:, ph]], axis=1)),
            "brkv": f32c(np.concatenate(
                [br[0:D][ph], br[D:2 * D][ph], br[2 * D:3 * D][ph]])[:, None]),
            "nul": f32c(nul[ph][:, None]), "gal": f32c(gal[ph][:, None]),
            "wo": f32c(Wo[ph, :]),
            "bo": f32c((bo if p == 0 else np.zeros_like(bo))[:, None]),
            "onesln": onesln,
        }
        m.update(ffn)
        maps.append(m)
    return maps


def run_L1(I, resA):
    pA = [resA[2 * b]["attn_out"] for b in range(4)]
    pB = [resA[2 * b + 1]["attn_out"] for b in range(4)]
    pC = [np.concatenate([resA[2 * b]["xffn_out"],
                          resA[2 * b + 1]["xffn_out"]], axis=1)
          for b in range(4)]
    nc, cx, _ = build_L1()
    maps = host_prep_L1(I, pA, pB, pC)
    res = run_bass_kernel_spmd(nc, maps, core_ids=list(range(8)))
    cx.close()
    return res.results




def emit_banded_mha(cx, block, inp, xn_t, oTh, consts):
    idb = consts["idb"]
    H = 8
    wqkv = cx.sbuf("wqkv", (128, 3 * 512), BF16)
    bqkv = cx.sbuf("bqkv", (64, 3 * H), F32)
    qT = cx.sbuf("qT", (64, S), BF16)
    kT = cx.sbuf("kT", (64, S), BF16)
    vTM = cx.sbuf("vTM", (128, 32 * 64), BF16)
    vTs = cx.sbuf("vTs", (64, 2 * 512), BF16)
    strip = cx.sbuf("strip", (128, S), F32)
    pstrip = cx.sbuf("pstrip", (128, S), BF16)
    ptbig = cx.sbuf("ptbig", (128, 16 * 128), BF16)
    rsum = cx.sbuf("rsum", (128, 1), F32)
    nmax = cx.sbuf("nmax", (128, 1), F32)
    mfull = cx.sbuf("mfull", (128, S), F32)
    mband = cx.sbuf("mband", (128, 8 * 128), F32)

    @block.sync
    def _(sy):
        cx.inc(sy.dma_start(mfull[:], inp["mfull"]), cx.ld, 16)
        sy.wait_ge(cx.ld, cx.val(cx.ld))
        cx.inc(sy.dma_start(mband[:], inp["mband"]), cx.ld, 16)
        sy.wait_ge(cx.ld, cx.val(cx.ld))
        for j, nm in enumerate(("bq", "bk", "bv")):
            for h in range(H):
                cx.inc(sy.dma_start(bqkv[:, j * H + h:j * H + h + 1],
                                    inp[nm][h * 64:(h + 1) * 64, :]),
                       cx.ld, 16)
                sy.wait_ge(cx.ld, cx.val(cx.ld))

    def ktiles_of(qi):
        if qi == 0:
            return list(range(32)), None
        kts = []
        masks = []
        if qi >= 3:
            kts.append(0)
            masks.append(5)
        for kt in range(max(0, qi - 2), min(32, qi + 3)):
            kts.append(kt)
            d = kt - qi
            if kt == 0 and d == -2:
                masks.append(6)
            elif kt == 0 and d == -1:
                masks.append(7)
            else:
                masks.append(d + 2)
        return kts, masks

    for h in range(H):
        @block.gpsimd
        def _(g):
            g = cx.ser_g(g)
            cx.wait(g, cx.mm)
            for j, wname in enumerate(("wq", "wk", "wv")):
                wsl = inp[wname][:, h * 64:(h + 1) * 64]
                wload(g, cx, wqkv[:, j * 512:(j + 1) * 512], wsl, 8)

        for j, name in enumerate(("q", "k", "v")):
            for c in range(8):
                c0 = c * 512
                slot = c % 2

                @block.tensor
                def _(t):
                    cx.wait(t, cx.lg, cx.ld)
                    if c == 0:
                        cx.wait(t, cx.vv, cx.ev)
                    bi, ps = cx.bank(t)
                    for ci in range(8):
                        ins = t.matmul(
                            ps[0:64, :],
                            wqkv[:, j * 512 + ci * 64:j * 512 + (ci + 1) * 64],
                            xn_t(ci, c0, 512),
                            start=(ci == 0), stop=(ci == 7))
                    cx.inc(ins, cx.mm)

                    @block.scalar
                    def _(s):
                        s = cx.ser_s(s)
                        cx.wait(s, cx.mm, cx.vv)
                        dst = vTs[:, slot * 512:(slot + 1) * 512] if name == "v" \
                            else (qT if name == "q" else kT)[:, c0:c0 + 512]
                        s.activation(dst, ps[0:64, :], AF.Identity,
                                     bias=bqkv[:, j * H + h:j * H + h + 1])
                    cx.bank_freed(bi, cx.ev)

                    if name == "v":
                        @block.tensor
                        def _(t2):
                            cx.wait(t2, cx.ev)
                            for u in range(4):
                                kt = c * 4 + u
                                bb, psb = cx.bankb(t2)
                                ins = t2.transpose(
                                    psb[:, 0:64],
                                    vTs[:, slot * 512 + u * 128:
                                        slot * 512 + (u + 1) * 128],
                                    idb[0:64, 0:64])
                                cx.inc(ins, cx.mm)

                                @block.vector
                                def _(v):
                                    v = cx.ser_v(v)
                                    cx.wait(v, cx.mm)
                                    v.tensor_copy(vTM[:, kt * 64:(kt + 1) * 64],
                                                  psb[:, 0:64])
                                cx.bankb_freed(bb, cx.vv)

        for qi in range(32):
            kts, masks = ktiles_of(qi)
            nseg = len(kts)

            @block.tensor
            def _(t):
                cx.wait(t, cx.vv, cx.ev)
                for g0 in range(0, nseg, 4):
                    gn = min(4, nseg - g0)
                    bi, ps = cx.bank(t)
                    for u in range(gn):
                        kt = kts[g0 + u]
                        ins = t.matmul(ps[:, u * 128:(u + 1) * 128],
                                       qT[:, qi * 128:(qi + 1) * 128],
                                       kT[:, kt * 128:(kt + 1) * 128],
                                       start=True, stop=True)
                    cx.inc(ins, cx.mm)

                    @block.vector
                    def _(v):
                        v = cx.ser_v(v)
                        cx.wait(v, cx.mm, cx.ev)
                        for u in range(gn):
                            si = g0 + u
                            if qi == 0:
                                mk = mfull[:, kts[si] * 128:(kts[si] + 1) * 128]
                            else:
                                mi = masks[si]
                                mk = mband[:, mi * 128:(mi + 1) * 128]
                            v.tensor_tensor(strip[:, si * 128:(si + 1) * 128],
                                            ps[:, u * 128:(u + 1) * 128],
                                            mk, op=ALU.add)
                    cx.bank_freed(bi, cx.vv)

            kw = nseg * 128

            @block.vector
            def _(v):
                v = cx.ser_v(v)
                cx.wait(v, cx.vv, cx.ev)
                v.reduce_max(nmax[:], strip[:, 0:kw], axis=AX.X, negate=True)
                v.tensor_scalar(nmax[:], nmax[:], 0.125, None, op0=ALU.mult)

            @block.scalar
            def _(s):
                s = cx.ser_s(s)
                cx.wait(s, cx.vv, cx.mm)
                s.activation(pstrip[:, 0:kw], strip[:, 0:kw], AF.Exp,
                             bias=nmax[:, 0:1], scale=0.125,
                             accum_out=rsum[:])

            @block.vector
            def _(v):
                v = cx.ser_v(v)
                cx.wait(v, cx.ev)
                v.reciprocal(nmax[:], rsum[:])
                v.tensor_scalar(pstrip[:, 0:kw], pstrip[:, 0:kw],
                                nmax[:, 0:1], None, op0=ALU.mult)

            @block.tensor
            def _(t):
                cx.wait(t, cx.vv)
                bo, pso = cx.bank(t)
                for G0 in range(0, nseg, 16):
                    Gn = min(16, nseg - G0)
                    for u0 in range(0, Gn, 8):
                        un = min(8, Gn - u0)
                        bb, psb = cx.bankb(t)
                        for u in range(un):
                            si = G0 + u0 + u
                            ins = t.transpose(psb[:, u * 128:(u + 1) * 128],
                                              pstrip[:, si * 128:(si + 1) * 128],
                                              idb[:])
                        cx.inc(ins, cx.mm)

                        @block.vector
                        def _(v):
                            v = cx.ser_v(v)
                            cx.wait(v, cx.mm)
                            v.tensor_copy(ptbig[:, u0 * 128:(u0 + un) * 128],
                                          psb[:, 0:un * 128])
                        cx.bankb_freed(bb, cx.vv)
                    cx.wait(t, cx.vv)
                    for u in range(Gn):
                        si = G0 + u
                        kt = kts[si]
                        ins = t.matmul(pso[0:64, 0:128],
                                       vTM[:, kt * 64:(kt + 1) * 64],
                                       ptbig[:, u * 128:(u + 1) * 128],
                                       start=(si == 0), stop=(si == nseg - 1))
                    cx.inc(ins, cx.mm)

                @block.scalar
                def _(s):
                    s = cx.ser_s(s)
                    cx.wait(s, cx.mm)
                    hp = h % 2
                    s.activation(
                        oTh[hp * 64:hp * 64 + 64,
                            (h // 2) * S + qi * 128:(h // 2) * S + (qi + 1) * 128],
                        pso[0:64, 0:128], AF.Copy)
                cx.bank_freed(bo, cx.ev)

    cx.free_sbuf("wqkv", "bqkv", "qT", "kT", "vTM", "vTs", "strip", "pstrip",
                 "ptbig", "rsum", "nmax", "mfull", "mband")


def emit_ffn_iter(cx, block, inp, xsrc_t, nf, mask_col, xnew, pool,
                  accum_pool, th0, hb, fblock=16):
    wst = cx.sbuf("fi_wst", (128, 2 * fblock * 128), BF16)
    used = {}
    nblk = (nf + fblock - 1) // fblock
    for blk in range(nblk):
        f0 = blk * fblock
        fn_ = min(fblock, nf - f0)
        for fo in range(f0, f0 + fn_):
            slot = fo % 2

            @block.gpsimd
            def _(g):
                g = cx.ser_g(g)
                wv = max(used.get(("m", slot), 0), used.get("wp_done", 0))
                if wv > 0:
                    g.wait_ge(cx.mm, wv)
                wload(g, cx, wst[:, slot * 1024:slot * 1024 + 8 * 128],
                      inp["wmT"][:, fo * 128:(fo + 1) * 128], 8)

            for ch in range(2):
                @block.tensor
                def _(t):
                    cx.wait(t, cx.lg, cx.ld, cx.vv, cx.ev)
                    bi, ps = cx.bank(t)
                    for ci in range(8):
                        ins = t.matmul(ps[:],
                                       wst[:, slot * 1024 + ci * 128:
                                           slot * 1024 + (ci + 1) * 128],
                                       xsrc_t(ci, th0 + ch * 512, 512),
                                       start=(ci == 0), stop=(ci == 7))
                    cx.inc(ins, cx.mm)
                    if ch == 1:
                        used[("m", slot)] = cx.val(cx.mm)

                    @block.scalar
                    def _(s):
                        s = cx.ser_s(s)
                        cx.wait(s, cx.mm)
                        kw = {}
                        if mask_col is not None:
                            kw["scale"] = mask_col(fo)
                        s.activation(hb[:, (fo - f0) * 1024 + ch * 512:
                                        (fo - f0) * 1024 + ch * 512 + 512],
                                     ps[:], AF.Relu, **kw)
                    cx.bank_freed(bi, cx.ev)

        used["wm_done"] = cx.val(cx.mm)
        for po in range(8):
            slot = po % 2

            @block.gpsimd
            def _(g):
                g = cx.ser_g(g)
                wv = max(used.get(("p", slot), 0), used.get("wm_done", 0))
                if wv > 0:
                    g.wait_ge(cx.mm, wv)
                wload(g, cx,
                      wst[:, slot * fblock * 128:slot * fblock * 128 + fn_ * 128],
                      inp["wpT"][f0 * 128:(f0 + fn_) * 128,
                                 po * 128:(po + 1) * 128], fn_)

            for ch in range(2):
                @block.tensor
                def _(t):
                    cx.wait(t, cx.lg, cx.ld, cx.vv, cx.ev)
                    bi, ps = cx.bank(t)
                    for ti in range(fn_):
                        ins = t.matmul(
                            ps[:],
                            wst[:, slot * fblock * 128 + ti * 128:
                                slot * fblock * 128 + (ti + 1) * 128],
                            hb[:, ti * 1024 + ch * 512:ti * 1024 + ch * 512 + 512],
                            start=(ti == 0), stop=(ti == fn_ - 1))
                    cx.inc(ins, cx.mm)
                    if ch == 1:
                        used[("p", slot)] = cx.val(cx.mm)
                        used["wp_done"] = cx.val(cx.mm)

                    @block.vector
                    def _(v):
                        v = cx.ser_v(v)
                        cx.wait(v, cx.mm, cx.st)
                        dst = xnew[:, po * 1024 + ch * 512:
                                   po * 1024 + ch * 512 + 512]
                        if blk == 0:
                            v.tensor_copy(dst, ps[:])
                        else:
                            v.tensor_tensor(dst, dst, ps[:], op=ALU.add)
                    cx.bank_freed(bi, cx.vv)

    po2 = cx.sbuf("fi_po2", (128, 8), F32)

    @block.vector
    def _(v):
        v = cx.ser_v(v)
        cx.wait(v, cx.vv)
        for po in range(8):
            v.reduce_sum(po2[:, po:po + 1],
                         xnew[:, po * 1024:(po + 1) * 1024], axis=AX.X)
        if accum_pool:
            v.tensor_tensor(pool[:], pool[:], po2[:], op=ALU.add)
        else:
            v.tensor_copy(pool[:], po2[:])
    cx.free_sbuf("fi_po2", "fi_wst")


def build_L2a():
    nc = bass.Bass("TRN2", target_bir_lowering=False)
    f32in = [
        ("pA", [D, S]), ("pB", [D, S]), ("pC", [D, S]),
        ("pAo", [D, TOWN]), ("pBo", [D, TOWN]), ("pCo", [D, TOWN]),
        ("wq", [D, 512]), ("wk", [D, 512]), ("wv", [D, 512]),
        ("bq", [512, 1]), ("bk", [512, 1]), ("bv", [512, 1]),
        ("wo", [512, D]), ("bo", [D, 1]),
        ("mfull", [128, S]), ("mband", [128, 8 * 128]),
        ("wmT", [D, 4096]), ("wpT", [4096, D]),
    ]
    inp = {}
    for name, shape in f32in:
        inp[name] = nc.dram_tensor(name, shape, F32, kind="ExternalInput").ap()
    for name in ("idb", "onesln"):
        inp[name] = nc.dram_tensor(name, [128, 128], BF16,
                                   kind="ExternalInput").ap()
    attn_out = nc.dram_tensor("attn_out", [D, S], F32,
                              kind="ExternalOutput").ap()
    xnew_out = nc.dram_tensor("xnew_out", [D, TOWN], F32,
                              kind="ExternalOutput").ap()
    pool_out = nc.dram_tensor("pool_out", [128, 8], F32,
                              kind="ExternalOutput").ap()

    cx = Ctx(nc)
    with nc.Block() as block:
        cx.block = block
        onesln = cx.sbuf("onesln", (128, 128), BF16)
        idb = cx.sbuf("idb", (128, 128), BF16)

        @block.sync
        def _(sy):
            cx.inc(sy.dma_start(onesln[:], inp["onesln"]), cx.ld, 16)
            sy.wait_ge(cx.ld, cx.val(cx.ld))
            cx.inc(sy.dma_start(idb[:], inp["idb"]), cx.ld, 16)
            sy.wait_ge(cx.ld, cx.val(cx.ld))
        consts = {"idb": idb}

        oTh = cx.sbuf("oTh", (128, 4 * S), BF16)
        xn = cx.sbuf("xn", (128, 8 * S), BF16)
        emit_ln(cx, block, inp["pA"], xn, S, onesln,
                extra=(inp["pB"], inp["pC"]))

        def xn_t(i, c0, w):
            return xn[:, i * S + c0:i * S + c0 + w]

        cx.fence(block)
        emit_banded_mha(cx, block, inp, xn_t, oTh, consts)
        cx.fence(block)
        cx.free_sbuf("xn")
        bo_sb = cx.sbuf("bo_sb", (128, 8), F32)

        @block.sync
        def _(sy):
            for po in range(8):
                cx.inc(sy.dma_start(bo_sb[:, po:po + 1],
                                    inp["bo"][po * 128:(po + 1) * 128, :]),
                       cx.ld, 16)
                sy.wait_ge(cx.ld, cx.val(cx.ld))
        emit_headsum_proj(cx, block, inp["wo"], oTh, attn_out, 8,
                          bias_sb=None if False else bo_sb)
        cx.free_sbuf("oTh", "bo_sb")
        cx.fence(block)

        xno = cx.sbuf("xno", (128, 8 * TOWN), BF16)
        emit_ln(cx, block, inp["pAo"], xno, TOWN, onesln,
                extra=(inp["pBo"], inp["pCo"]))

        def xno_t(i, c0, w):
            return xno[:, i * TOWN + c0:i * TOWN + c0 + w]

        cx.fence(block)
        hb = cx.sbuf("hb", (128, 16 * 1024), BF16)
        xnew = cx.sbuf("xnew", (128, 8 * 1024), F32)
        pool = cx.sbuf("pool", (128, 8), F32)

        for hf in range(2):
            th0 = hf * 1024
            emit_ffn_iter(cx, block, inp, xno_t, 32, None, xnew, pool,
                          hf > 0, th0, hb)
            cx.fence(block)

            @block.sync
            def _(sy):
                cx.wait(sy, cx.vv)
                for po in range(8):
                    cx.inc(sy.dma_start(
                        xnew_out[po * 128:(po + 1) * 128, th0:th0 + 1024],
                        xnew[:, po * 1024:(po + 1) * 1024]), cx.st, 16)
                    sy.wait_ge(cx.st, cx.val(cx.st))

        @block.sync
        def _(sy):
            cx.wait(sy, cx.vv)
            cx.inc(sy.dma_start(pool_out[:], pool[:]), cx.st, 16)
            sy.wait_ge(cx.st, cx.val(cx.st))
            cx.wait(sy, cx.st)

    return nc, cx, inp



def emit_controller(cx, block, inp, pall_dram, maskb, consts, tag):
    onesf, idf = consts["onesf"], consts["idf"]
    pall = cx.sbuf(f"ct_pall{tag}", (128, 64), F32)
    pm = cx.sbuf(f"ct_pm{tag}", (128, 32), F32)
    t1 = cx.sbuf(f"ct_t1{tag}", (128, 8), F32)
    ckT = cx.sbuf(f"ct_ck{tag}", (128, 4 * 96), F32)
    wc1 = cx.sbuf(f"ct_wc1{tag}", (128, 8 * 256), F32)
    wc2 = cx.sbuf(f"ct_wc2{tag}", (128, 2 * 2 * 128), F32)
    sc = cx.sbuf(f"ct_sc{tag}", (1, 16), F32)
    scb = cx.sbuf(f"ct_scb{tag}", (128, 8), F32)
    tmp = cx.sbuf(f"ct_tmp{tag}", (128, 96), F32)
    c41 = cx.sbuf(f"ct_c41{tag}", (128, 4), F32)

    @block.sync
    def _(sy):
        cx.inc(sy.dma_start(pall[:], pall_dram), cx.ld, 16)
        sy.wait_ge(cx.ld, cx.val(cx.ld))
        cx.inc(sy.dma_start(wc1[:].rearrange("p (c o) -> p c o", c=8),
                            inp["wc1T"].rearrange("(c p) o -> p c o", p=128)),
               cx.ld, 16)
        sy.wait_ge(cx.ld, cx.val(cx.ld))

    @block.vector
    def _(v):
        v = cx.ser_v(v)
        cx.wait(v, cx.ld)
        a = pall[:].rearrange("p (b two c) -> p b two c", two=2, c=8)
        v.tensor_tensor(pm[:].rearrange("p (c b) -> p b c", c=8),
                        a[:, :, 0, :], a[:, :, 1, :], op=ALU.add)
        v.tensor_scalar(pm[:], pm[:], 1.0 / 4096.0, None, op0=ALU.mult)

    @block.tensor
    def _(t):
        cx.wait(t, cx.vv, cx.ld)
        for po2 in range(2):
            bi, ps = cx.bank(t)
            for ci in range(8):
                ins = t.matmul(ps[0:128, 0:4],
                               wc1[:, ci * 256 + po2 * 128:
                                   ci * 256 + (po2 + 1) * 128],
                               pm[:, ci * 4:(ci + 1) * 4],
                               start=(ci == 0), stop=(ci == 7))
            cx.inc(ins, cx.mm)

            @block.scalar
            def _(s2):
                s2 = cx.ser_s(s2)
                cx.wait(s2, cx.mm)
                s2.activation(t1[:, po2 * 4:(po2 + 1) * 4], ps[0:128, 0:4],
                              AF.Relu)
            cx.bank_freed(bi, cx.ev)

    wused = {}
    for fo in range(96):
        slot = fo % 2

        @block.sync
        def _(sy):
            if slot in wused:
                sy.wait_ge(cx.mm, wused[slot])
            cx.inc(sy.dma_start(
                wc2[:, slot * 256:(slot + 1) * 256]
                .rearrange("p (c o) -> p c o", c=2),
                inp["wc2T"][:, fo * 128:(fo + 1) * 128]
                .rearrange("(c p) o -> p c o", p=128)), cx.ld, 16)
            sy.wait_ge(cx.ld, cx.val(cx.ld))

        @block.tensor
        def _(t):
            cx.wait(t, cx.ld, cx.ev)
            bi, ps = cx.bank(t)
            for ci in range(2):
                ins = t.matmul(ps[0:128, 0:4],
                               wc2[:, slot * 256 + ci * 128:
                                   slot * 256 + (ci + 1) * 128],
                               t1[:, ci * 4:(ci + 1) * 4],
                               start=(ci == 0), stop=(ci == 1))
            cx.inc(ins, cx.mm)
            wused[slot] = cx.val(cx.mm)

            @block.scalar
            def _(s2):
                s2 = cx.ser_s(s2)
                cx.wait(s2, cx.mm)
                s2.activation(ckT[:, fo:fo + 3 * 96 + 1:96], ps[0:128, 0:4],
                              AF.Copy)
            cx.bank_freed(bi, cx.ev)

    @block.vector
    def _(v):
        v = cx.ser_v(v)
        cx.wait(v, cx.ev)
        for b in range(4):
            v.reduce_max(c41[:, b:b + 1], ckT[:, b * 96:(b + 1) * 96],
                         axis=AX.X)
        for b in range(4):
            v.reduce_max(scb[:, b:b + 1], ckT[:, b * 96:(b + 1) * 96],
                         axis=AX.X, negate=True)

    m4a = cx.sbuf(f"ct_m4a{tag}", (4, 128), F32)
    m4b = cx.sbuf(f"ct_m4b{tag}", (4, 128), F32)
    m41a = cx.sbuf(f"ct_m41a{tag}", (4, 1), F32)
    m41b = cx.sbuf(f"ct_m41b{tag}", (4, 1), F32)

    @block.tensor
    def _(t):
        cx.wait(t, cx.vv)
        bi, ps = cx.bank(t)
        ins = t.transpose(ps[0:4, 0:128], c41[:], idf[:])
        cx.inc(ins, cx.mm)
        bj, ps2 = cx.bank(t)
        ins = t.transpose(ps2[0:4, 0:128], scb[:, 0:4], idf[:])
        cx.inc(ins, cx.mm)

        @block.vector
        def _(v):
            v = cx.ser_v(v)
            cx.wait(v, cx.mm)
            v.tensor_copy(m4a[:], ps[0:4, 0:128])
            v.tensor_copy(m4b[:], ps2[0:4, 0:128])
            v.reduce_max(m41a[:], m4a[:], axis=AX.X)
            v.reduce_max(m41b[:], m4b[:], axis=AX.X)
        cx.bank_freed(bi, cx.vv)
        cx.bank_freed(bj, cx.vv)
        t.wait_ge(cx.vv, cx.val(cx.vv))
        bk, ps3 = cx.bank(t)
        ins = t.transpose(ps3[0:1, 0:4], m41a[:], idf[0:4, 0:4])
        cx.inc(ins, cx.mm)
        bl, ps4 = cx.bank(t)
        ins = t.transpose(ps4[0:1, 0:4], m41b[:], idf[0:4, 0:4])
        cx.inc(ins, cx.mm)

        @block.vector
        def _(v):
            v = cx.ser_v(v)
            cx.wait(v, cx.mm)
            v.tensor_copy(sc[0:1, 4:8], ps3[0:1, 0:4])
            v.tensor_copy(sc[0:1, 0:4], ps4[0:1, 0:4])
            v.tensor_scalar(sc[0:1, 0:4], sc[0:1, 0:4], -1.0, -1.0,
                            op0=ALU.mult, op1=ALU.add)
            v.tensor_scalar(sc[0:1, 4:8], sc[0:1, 4:8], 1.0, None, op0=ALU.add)
        cx.bank_freed(bk, cx.vv)
        cx.bank_freed(bl, cx.vv)

    for it in range(40):
        @block.vector
        def _(v):
            v = cx.ser_v(v)
            cx.wait(v, cx.ev)
            v.tensor_tensor(sc[0:1, 8:12], sc[0:1, 0:4], sc[0:1, 4:8],
                            op=ALU.add)
            v.tensor_scalar(sc[0:1, 8:12], sc[0:1, 8:12], 0.5, None,
                            op0=ALU.mult)

        @block.tensor
        def _(t):
            cx.wait(t, cx.vv)
            bi, ps = cx.bank(t)
            ins = t.matmul(ps[0:128, 0:4], onesf[0:1, :], sc[0:1, 8:12],
                           start=True, stop=True)
            cx.inc(ins, cx.mm)

            @block.scalar
            def _(s2):
                s2 = cx.ser_s(s2)
                cx.wait(s2, cx.mm)
                s2.activation(scb[:, 0:4], ps[0:128, 0:4], AF.Copy)
            cx.bank_freed(bi, cx.ev)

        @block.vector
        def _(v):
            v = cx.ser_v(v)
            cx.wait(v, cx.ev)
            for b in range(4):
                v.tensor_scalar(tmp[:], ckT[:, b * 96:(b + 1) * 96],
                                scb[:, b:b + 1], None, op0=ALU.is_gt)
                v.reduce_sum(c41[:, b:b + 1], tmp[:], axis=AX.X)

        @block.tensor
        def _(t):
            cx.wait(t, cx.vv)
            bi, ps = cx.bank(t)
            ins = t.matmul(ps[0:1, 0:4], onesf[:, 0:1], c41[:],
                           start=True, stop=True)
            cx.inc(ins, cx.mm)

            @block.scalar
            def _(s2):
                s2 = cx.ser_s(s2)
                cx.wait(s2, cx.mm)
                s2.activation(sc[0:1, 12:16], ps[0:1, 0:4], AF.Copy)
            cx.bank_freed(bi, cx.ev)

        @block.vector
        def _(v):
            v = cx.ser_v(v)
            cx.wait(v, cx.ev)
            v.tensor_scalar(sc[0:1, 12:16], sc[0:1, 12:16], 4096.0, None,
                            op0=ALU.is_ge)
            v.tensor_tensor(c41[0:1, 0:4], sc[0:1, 8:12], sc[0:1, 0:4],
                            op=ALU.subtract)
            v.tensor_tensor(c41[0:1, 0:4], c41[0:1, 0:4], sc[0:1, 12:16],
                            op=ALU.mult)
            v.tensor_tensor(sc[0:1, 0:4], sc[0:1, 0:4], c41[0:1, 0:4],
                            op=ALU.add)
            v.tensor_scalar(sc[0:1, 12:16], sc[0:1, 12:16], -1.0, 1.0,
                            op0=ALU.mult, op1=ALU.add)
            v.tensor_tensor(c41[0:1, 0:4], sc[0:1, 8:12], sc[0:1, 4:8],
                            op=ALU.subtract)
            v.tensor_tensor(c41[0:1, 0:4], c41[0:1, 0:4], sc[0:1, 12:16],
                            op=ALU.mult)
            v.tensor_tensor(sc[0:1, 4:8], sc[0:1, 4:8], c41[0:1, 0:4],
                            op=ALU.add)

    @block.tensor
    def _(t):
        cx.wait(t, cx.vv)
        bi, ps = cx.bank(t)
        ins = t.matmul(ps[0:128, 0:4], onesf[0:1, :], sc[0:1, 0:4],
                       start=True, stop=True)
        cx.inc(ins, cx.mm)

        @block.scalar
        def _(s2):
            s2 = cx.ser_s(s2)
            cx.wait(s2, cx.mm)
            s2.activation(scb[:, 0:4], ps[0:128, 0:4], AF.Copy)
        cx.bank_freed(bi, cx.ev)

    @block.vector
    def _(v):
        v = cx.ser_v(v)
        cx.wait(v, cx.ev)
        for b in range(4):
            v.tensor_scalar(maskb[:, b * 96:(b + 1) * 96],
                            ckT[:, b * 96:(b + 1) * 96],
                            scb[:, b:b + 1], None, op0=ALU.is_gt)

    cx.free_sbuf(f"ct_pall{tag}", f"ct_pm{tag}", f"ct_t1{tag}", f"ct_ck{tag}",
                 f"ct_wc1{tag}", f"ct_wc2{tag}", f"ct_sc{tag}", f"ct_scb{tag}",
                 f"ct_tmp{tag}", f"ct_c41{tag}", f"ct_m4a{tag}", f"ct_m4b{tag}",
                 f"ct_m41a{tag}", f"ct_m41b{tag}")


def emit_novelty_hi(cx, block, maskb, prevhi, out_sc, consts):
    onesf = consts["onesf"]
    nvp = cx.sbuf("nv_nvp", (128, 4), F32)
    t96 = cx.sbuf("nv_t", (128, 64), F32)

    @block.vector
    def _(v):
        v = cx.ser_v(v)
        cx.wait(v, cx.vv, cx.ev)
        for b in range(4):
            hi = maskb[:, b * 96 + 32:(b + 1) * 96]
            if prevhi is None:
                v.reduce_sum(nvp[:, b:b + 1], hi, axis=AX.X)
            else:
                v.tensor_scalar(t96[:], prevhi(b), -1.0, 1.0,
                                op0=ALU.mult, op1=ALU.add)
                v.tensor_tensor(t96[:], t96[:], hi, op=ALU.mult)
                v.reduce_sum(nvp[:, b:b + 1], t96[:], axis=AX.X)

    @block.tensor
    def _(t):
        cx.wait(t, cx.vv)
        bi, ps = cx.bank(t)
        ins = t.matmul(ps[0:1, 0:4], onesf[:, 0:1], nvp[:],
                       start=True, stop=True)
        cx.inc(ins, cx.mm)

        @block.scalar
        def _(s2):
            s2 = cx.ser_s(s2)
            cx.wait(s2, cx.mm)
            s2.activation(nvp[0:1, 0:4], ps[0:1, 0:4], AF.Copy)
        cx.bank_freed(bi, cx.ev)

    @block.vector
    def _(v):
        v = cx.ser_v(v)
        cx.wait(v, cx.ev)
        v.reduce_sum(out_sc, nvp[0:1, 0:4], axis=AX.X)
        v.tensor_scalar(out_sc, out_sc, 1.0 / 16384.0, None, op0=ALU.mult)

    cx.free_sbuf("nv_nvp", "nv_t")


def emit_bsel_combine(cx, block, maskb, bselb, out96):
    @block.vector
    def _(v):
        v = cx.ser_v(v)
        cx.wait(v, cx.vv, cx.ev)
        v.tensor_scalar(out96[:], maskb[:, 0:96], bselb[:, 0:1], None,
                        op0=ALU.mult)
        for b in range(1, 4):
            v.scalar_tensor_tensor(out96[:], maskb[:, b * 96:(b + 1) * 96],
                                   bselb[:, b:b + 1], out96[:],
                                   op0=ALU.mult, op1=ALU.add)


def emit_bcast_cols(cx, block, src_row, dst, n, consts):
    onesf = consts["onesf"]

    @block.tensor
    def _(t):
        cx.wait(t, cx.vv, cx.ev)
        bi, ps = cx.bank(t)
        ins = t.matmul(ps[0:128, 0:n], onesf[0:1, :], src_row,
                       start=True, stop=True)
        cx.inc(ins, cx.mm)

        @block.scalar
        def _(s2):
            s2 = cx.ser_s(s2)
            cx.wait(s2, cx.mm)
            s2.activation(dst, ps[0:128, 0:n], AF.Copy)
        cx.bank_freed(bi, cx.ev)


def build_L2bc(last):
    nc = bass.Bass("TRN2", target_bir_lowering=False)
    f32in = [
        ("xc", [D, TOWN]),
        ("pall1", [128, 64]),
        ("pall2", [128, 64]),
        ("wc1T", [D, 256]), ("wc2T", [256, 12288]),
        ("wmT", [D, 12288]), ("wpT", [12288, D]),
        ("bsel", [1, 4]),
        ("idf", [128, 128]), ("onesf", [128, 128]),
        ("pAo", [D, TOWN]), ("pBo", [D, TOWN]), ("pCo", [D, TOWN]),
    ]
    inp = {}
    for name, shape in f32in:
        inp[name] = nc.dram_tensor(name, shape, F32, kind="ExternalInput").ap()
    inp["onesln"] = nc.dram_tensor("onesln", [128, 128], BF16,
                                   kind="ExternalInput").ap()
    xout = nc.dram_tensor("xout", [D, TOWN], F32, kind="ExternalOutput").ap()
    pool_out = nc.dram_tensor("pool_out", [128, 8], F32,
                              kind="ExternalOutput").ap()

    cx = Ctx(nc)
    with nc.Block() as block:
        cx.block = block
        onesln = cx.sbuf("onesln", (128, 128), BF16)
        idf = cx.sbuf("idf", (128, 128), F32)
        onesf = cx.sbuf("onesf", (128, 128), F32)

        @block.sync
        def _(sy):
            cx.inc(sy.dma_start(onesln[:], inp["onesln"]), cx.ld, 16)
            sy.wait_ge(cx.ld, cx.val(cx.ld))
            cx.inc(sy.dma_start(idf[:], inp["idf"]), cx.ld, 16)
            sy.wait_ge(cx.ld, cx.val(cx.ld))
            cx.inc(sy.dma_start(onesf[:], inp["onesf"]), cx.ld, 16)
            sy.wait_ge(cx.ld, cx.val(cx.ld))
        consts = {"onesln": onesln, "idf": idf, "onesf": onesf}

        maskb1 = cx.sbuf("maskb1", (128, 4 * 96), F32)
        maskb2 = cx.sbuf("maskb2", (128, 4 * 96), F32)
        scal = cx.sbuf("scal", (1, 8), F32)
        dcol = cx.sbuf("dcol", (128, 2), F32)
        bselb = cx.sbuf("bselb", (128, 4), F32)
        myact = cx.sbuf("myact", (128, 96), F32)
        bsel_row = cx.sbuf("bsel_row", (1, 4), F32)

        @block.sync
        def _(sy):
            cx.inc(sy.dma_start(bsel_row[:], inp["bsel"]), cx.ld, 16)
            sy.wait_ge(cx.ld, cx.val(cx.ld))

        emit_controller(cx, block, inp, inp["pall1"][:, :], maskb1, consts, "1")
        emit_novelty_hi(cx, block, maskb1, None, scal[0:1, 0:1], consts)

        @block.vector
        def _(v):
            v = cx.ser_v(v)
            cx.wait(v, cx.vv)
            v.tensor_scalar(scal[0:1, 1:2], scal[0:1, 0:1], 0.7, None,
                            op0=ALU.is_lt)

        emit_bcast_cols(cx, block, bsel_row[:], bselb[:], 4, consts)

        if not last:
            emit_bsel_combine(cx, block, maskb1, bselb, myact)
            @block.vector
            def _(v):
                v = cx.ser_v(v)
                cx.wait(v, cx.vv, cx.ev)
                v.tensor_copy(scal[0:1, 4:5], scal[0:1, 1:2])
                v.tensor_scalar(scal[0:1, 5:6], scal[0:1, 1:2], -1.0, 1.0,
                                op0=ALU.mult, op1=ALU.add)
        else:
            emit_controller(cx, block, inp, inp["pall2"][:, :], maskb2,
                            consts, "2")
            emit_novelty_hi(cx, block, maskb2,
                            lambda b: maskb1[:, b * 96 + 32:(b + 1) * 96],
                            scal[0:1, 2:3], consts)

            @block.vector
            def _(v):
                v = cx.ser_v(v)
                cx.wait(v, cx.vv)
                v.tensor_scalar(scal[0:1, 3:4], scal[0:1, 2:3], 0.7, None,
                                op0=ALU.is_lt)
                v.tensor_tensor(scal[0:1, 3:4], scal[0:1, 3:4], scal[0:1, 1:2],
                                op=ALU.max)
                v.tensor_copy(scal[0:1, 4:5], scal[0:1, 3:4])
                v.tensor_scalar(scal[0:1, 5:6], scal[0:1, 3:4], -1.0, 1.0,
                                op0=ALU.mult, op1=ALU.add)

            m1my = cx.sbuf("m1my", (128, 96), F32)
            m2my = cx.sbuf("m2my", (128, 96), F32)
            emit_bsel_combine(cx, block, maskb1, bselb, m1my)
            emit_bsel_combine(cx, block, maskb2, bselb, m2my)
            d1b = cx.sbuf("d1b", (128, 2), F32)

            @block.vector
            def _(v):
                v = cx.ser_v(v)
                cx.wait(v, cx.vv)
                v.tensor_copy(scal[0:1, 6:7], scal[0:1, 1:2])
                v.tensor_scalar(scal[0:1, 7:8], scal[0:1, 1:2], -1.0, 1.0,
                                op0=ALU.mult, op1=ALU.add)
            emit_bcast_cols(cx, block, scal[0:1, 6:8], d1b[:], 2, consts)

            @block.vector
            def _(v):
                v = cx.ser_v(v)
                cx.wait(v, cx.vv, cx.ev)
                v.tensor_scalar(myact[:], m1my[:], d1b[:, 0:1], None,
                                op0=ALU.mult)
                v.scalar_tensor_tensor(myact[:], m2my[:], d1b[:, 1:2],
                                       myact[:], op0=ALU.mult, op1=ALU.add)

        emit_bcast_cols(cx, block, scal[0:1, 4:6], dcol[:], 2, consts)
        cx.fence(block)

        xcb = cx.sbuf("xcb", (128, 8 * TOWN), BF16)

        @block.gpsimd
        def _(g):
            g = cx.ser_g(g)
            for i in range(8):
                cx.inc(g.dma_start(xcb[:, i * TOWN:(i + 1) * TOWN],
                                   inp["xc"][i * 128:(i + 1) * 128, :]),
                       cx.lg, 16)
                g.wait_ge(cx.lg, cx.val(cx.lg))

        def xcb_t(i, c0, w):
            return xcb[:, i * TOWN + c0:i * TOWN + c0 + w]

        hb = cx.sbuf("hb", (128, 16 * 1024), BF16)
        xnew = cx.sbuf("xnew", (128, 8 * 1024), F32)
        pool = cx.sbuf("pool", (128, 8), F32)
        xct = cx.sbuf("xct", (128, 1024), F32)

        for hf in range(2):
            th0 = hf * 1024
            emit_ffn_iter(cx, block, inp, xcb_t, 96,
                          lambda fo: myact[:, fo:fo + 1], xnew, pool,
                          hf > 0, th0, hb)
            cx.fence(block)

            for po in range(8):
                @block.sync
                def _(sy):
                    cx.wait(sy, cx.vv, cx.st)
                    cx.inc(sy.dma_start(
                        xct[:], inp["xc"][po * 128:(po + 1) * 128,
                                          th0:th0 + 1024]), cx.ld, 16)
                    sy.wait_ge(cx.ld, cx.val(cx.ld))

                @block.vector
                def _(v):
                    v = cx.ser_v(v)
                    cx.wait(v, cx.ld, cx.ev)
                    v.tensor_scalar(xct[:], xct[:], dcol[:, 0:1], None,
                                    op0=ALU.mult)
                    v.scalar_tensor_tensor(
                        xct[:], xnew[:, po * 1024:(po + 1) * 1024],
                        dcol[:, 1:2], xct[:], op0=ALU.mult, op1=ALU.add)
                    if last:
                        pass

                if last:
                    for src in ("pAo", "pBo", "pCo"):
                        xrt = cx.sbuf(f"xrt{po}{hf}{src}", (128, 1024), F32)

                        @block.sync
                        def _(sy):
                            cx.inc(sy.dma_start(
                                xrt[:], inp[src][po * 128:(po + 1) * 128,
                                                 th0:th0 + 1024]), cx.ld, 16)
                            sy.wait_ge(cx.ld, cx.val(cx.ld))

                        @block.vector
                        def _(v):
                            v = cx.ser_v(v)
                            cx.wait(v, cx.ld)
                            v.tensor_tensor(xct[:], xct[:], xrt[:], op=ALU.add)
                        cx.free_sbuf(f"xrt{po}{hf}{src}")

                @block.sync
                def _(sy):
                    cx.wait(sy, cx.vv)
                    cx.inc(sy.dma_start(
                        xout[po * 128:(po + 1) * 128, th0:th0 + 1024],
                        xct[:]), cx.st, 16)
                    sy.wait_ge(cx.st, cx.val(cx.st))

        @block.sync
        def _(sy):
            cx.wait(sy, cx.vv)
            cx.inc(sy.dma_start(pool_out[:], pool[:]), cx.st, 16)
            sy.wait_ge(cx.st, cx.val(cx.st))
            cx.wait(sy, cx.st)

    return nc, cx, inp



def l2_masks():
    mfull = np.zeros((128, S), np.float32)
    i = np.arange(128)[:, None]
    j = np.arange(S)[None, :]
    ok = (np.abs(i - j) <= 256) | (i == 0) | (j == 0)
    mfull[~ok] = NEG
    mband = np.zeros((128, 8 * 128), np.float32)
    for mi in range(8):
        blk = mband[:, mi * 128:(mi + 1) * 128]
        jj = np.arange(128)[None, :]
        r = np.arange(128)[:, None]
        if mi == 5:
            ok = (jj == 0) | (np.zeros_like(r, dtype=bool))
            ok = np.broadcast_to(jj == 0, (128, 128))
        elif mi in (6, 0):
            d = -2
            ok = np.abs(d * 128 + jj - r) <= 256
            if mi == 6:
                ok = ok | (jj == 0)
        elif mi in (7, 1):
            d = -1
            ok = np.abs(d * 128 + jj - r) <= 256
            if mi == 7:
                ok = ok | (jj == 0)
        else:
            d = mi - 2
            ok = np.abs(d * 128 + jj - r) <= 256
        blk[~ok] = NEG
    return mfull, mband


def host_prep_L2a(I, pA, pB, pC):
    mfull, mband = l2_masks()
    onesln = np.full((128, 128), 1.0 / 1024.0, np.float32).astype(bf16)
    idb = np.eye(128, dtype=np.float32).astype(bf16)
    WmT = f32c(np.asarray(I["L2_ffn_Wm"]).T)
    WpT = f32c(np.asarray(I["L2_ffn_Wp"]).T)
    ws = {}
    for n in ("q", "k", "v", "o"):
        ws["W" + n] = np.asarray(I["L2_attn_W" + n], np.float32)
        ws["b" + n] = np.asarray(I["L2_attn_b" + n], np.float32)
    maps = []
    for c in range(8):
        b, p = divmod(c, 2)
        hs = slice(p * 512, (p + 1) * 512)
        own = slice(p * TOWN, (p + 1) * TOWN)
        m = {
            "pA": pA[b], "pB": pB[b], "pC": pC[b],
            "pAo": f32c(pA[b][:, own]), "pBo": f32c(pB[b][:, own]),
            "pCo": f32c(pC[b][:, own]),
            "wq": f32c(ws["Wq"][:, hs]), "wk": f32c(ws["Wk"][:, hs]),
            "wv": f32c(ws["Wv"][:, hs]),
            "bq": f32c(ws["bq"][hs][:, None]),
            "bk": f32c(ws["bk"][hs][:, None]),
            "bv": f32c(ws["bv"][hs][:, None]),
            "wo": f32c(ws["Wo"][hs, :]),
            "bo": f32c((ws["bo"] if p == 0 else
                        np.zeros_like(ws["bo"]))[:, None]),
            "mfull": mfull, "mband": mband,
            "wmT": f32c(WmT[:, 0:4096]), "wpT": f32c(WpT[0:4096, :]),
            "idb": idb, "onesln": onesln,
        }
        maps.append(m)
    return maps, WmT, WpT


def host_prep_L2bc(I, xc, pall1, pall2, pA, pB, pC, WmT, WpT):
    onesln = np.full((128, 128), 1.0 / 1024.0, np.float32).astype(bf16)
    idf = np.eye(128, dtype=np.float32)
    onesf = np.ones((128, 128), np.float32)
    wc1T = f32c(np.asarray(I["L2_ffn_Wc1"]).T)
    wc2T = f32c(np.asarray(I["L2_ffn_Wc2"]).T)
    maps = []
    for c in range(8):
        b, p = divmod(c, 2)
        own = slice(p * TOWN, (p + 1) * TOWN)
        bsel = np.zeros((1, 4), np.float32)
        bsel[0, b] = 1.0
        m = {
            "xc": xc[c], "pall1": pall1, "pall2": pall2,
            "wc1T": wc1T, "wc2T": wc2T,
            "wmT": WmT, "wpT": WpT, "bsel": bsel,
            "idf": idf, "onesf": onesf, "onesln": onesln,
            "pAo": f32c(pA[b][:, own]), "pBo": f32c(pB[b][:, own]),
            "pCo": f32c(pC[b][:, own]),
        }
        maps.append(m)
    return maps


def run_L2(I, resB):
    pA = [resB[2 * b]["attn_out"] for b in range(4)]
    pB = [resB[2 * b + 1]["attn_out"] for b in range(4)]
    pC = [np.concatenate([resB[2 * b]["xffn_out"],
                          resB[2 * b + 1]["xffn_out"]], axis=1)
          for b in range(4)]
    mapsA, WmT, WpT = host_prep_L2a(I, pA, pB, pC)
    nc, cx, _ = build_L2a()
    resA = run_bass_kernel_spmd(nc, mapsA, core_ids=list(range(8))).results
    cx.close()

    pall1 = f32c(np.concatenate([resA[c]["pool_out"] for c in range(8)], 1))
    xc = [resA[c]["xnew_out"] for c in range(8)]
    maps2 = host_prep_L2bc(I, xc, pall1, pall1, pA, pB, pC, WmT, WpT)
    nc2, cx2, _ = build_L2bc(False)
    res2 = run_bass_kernel_spmd(nc2, maps2, core_ids=list(range(8))).results
    cx2.close()

    pall2 = f32c(np.concatenate([res2[c]["pool_out"] for c in range(8)], 1))
    xc2 = [res2[c]["xout"] for c in range(8)]
    maps3 = host_prep_L2bc(I, xc2, pall1, pall2, pA, pB, pC, WmT, WpT)
    nc3, cx3, _ = build_L2bc(True)
    res3 = run_bass_kernel_spmd(nc3, maps3, core_ids=list(range(8))).results
    cx3.close()
    return resA, res3




def build_L3():
    nc = bass.Bass("TRN2", target_bir_lowering=False)
    DI = 2048
    f32in = [
        ("pA", [D, S]), ("pB", [D, S]), ("pC", [D, S]),
        ("in_w", [D, 2 * DI]),
        ("convd", [128, 16 * 4 * 128]), ("convb", [128, 16]),
        ("xp_w", [DI, 96]), ("dt_w", [64, DI]), ("dt_b", [128, 16]),
        ("alog", [DI, 16]), ("dskip", [128, 16]),
        ("out_wT", [D, DI]), ("proj", [D, D]),
        ("wr", [D, 16]), ("br", [16, 1]),
        ("wqh", [D, D]), ("wkh", [D, D]), ("wvh", [D, D]),
        ("bqh", [D, 1]), ("bkh", [D, 1]), ("bvh", [D, 1]),
        ("wo", [128, D]),
        ("bias2", [2, D]),
        ("iota16", [1, 16]), ("iota64", [64, 1]), ("ksel2", [1, 8]),
        ("idf", [128, 128]), ("onesf", [128, 128]),
    ]
    inp = {}
    for name, shape in f32in:
        inp[name] = nc.dram_tensor(name, shape, F32, kind="ExternalInput").ap()
    for name in ("idb", "onesln"):
        inp[name] = nc.dram_tensor(name, [128, 128], BF16,
                                   kind="ExternalInput").ap()
    part_out = nc.dram_tensor("part_out", [D, S], F32,
                              kind="ExternalOutput").ap()
    xc_hb = nc.dram_tensor("xc_hb", [DI, S], BF16).ap()
    z_hb = nc.dram_tensor("z_hb", [DI, S], BF16).ap()
    dt_hb = nc.dram_tensor("dt_hb", [DI, S], F32).ap()
    y_hb = nc.dram_tensor("y_hb", [DI, S], BF16).ap()
    o2_hb = nc.dram_tensor("o2_hb", [128, S], BF16).ap()

    cx = Ctx(nc)
    with nc.Block() as block:
        cx.block = block
        onesln = cx.sbuf("onesln", (128, 128), BF16)
        idb = cx.sbuf("idb", (128, 128), BF16)
        idf = cx.sbuf("idf", (128, 128), F32)
        onesf = cx.sbuf("onesf", (128, 128), F32)
        wrf = cx.sbuf("wrf", (128, 8 * 16), F32)
        racc = cx.sbuf("racc", (16, 1), F32)
        rch = cx.sbuf("rch", (16, 1), F32)

        @block.sync
        def _(sy):
            for dst, src in ((onesln, inp["onesln"]), (idb, inp["idb"]),
                             (idf, inp["idf"]), (onesf, inp["onesf"])):
                cx.inc(sy.dma_start(dst[:], src), cx.ld, 16)
                sy.wait_ge(cx.ld, cx.val(cx.ld))
            cx.inc(sy.dma_start(
                wrf[:].rearrange("p (c o) -> p c o", c=8),
                inp["wr"].rearrange("(c p) o -> p c o", p=128)), cx.ld, 16)
            sy.wait_ge(cx.ld, cx.val(cx.ld))

        @block.vector
        def _(v):
            v = cx.ser_v(v)
            v.memset(racc[:], 0.0)

        dbc = cx.sbuf("dbc", (96, S), F32)
        asb = cx.sbuf("asb", (128, 16 * 16), F32)

        xn = cx.sbuf("xn", (128, 8 * S), BF16)

        def wr_tap(c0, xnf):
            @block.tensor
            def _(t):
                cx.wait(t, cx.ld, cx.vv)
                bi, ps = cx.bank(t)
                for ci in range(8):
                    ins = t.matmul(ps[0:16, :], wrf[:, ci * 16:(ci + 1) * 16],
                                   xnf[:, ci * 512:(ci + 1) * 512],
                                   start=(ci == 0), stop=(ci == 7))
                cx.inc(ins, cx.mm)

                @block.scalar
                def _(s2):
                    s2 = cx.ser_s(s2)
                    cx.wait(s2, cx.mm)
                    s2.activation(xnf[0:16, 0:512], ps[0:16, 0:512], AF.Copy,
                                  accum_out=rch[:])
                cx.bank_freed(bi, cx.ev)

            @block.vector
            def _(v):
                v = cx.ser_v(v)
                cx.wait(v, cx.ev)
                v.tensor_tensor(racc[:], racc[:], rch[:], op=ALU.add)

        emit_ln(cx, block, inp["pA"], xn, S, onesln,
                extra=(inp["pB"], inp["pC"]), tap=wr_tap)

        def xn_t(i, c0, w):
            return xn[:, i * S + c0:i * S + c0 + w]

        cx.fence(block)

        g16 = cx.sbuf("g16", (1, 16 * 8), F32)
        msks = cx.sbuf("msks", (1, 64), F32)
        sc4 = cx.sbuf("sc4", (1, 16), F32)
        m11 = cx.sbuf("m11", (1, 1), F32)
        it16 = cx.sbuf("it16", (1, 16), F32)
        ks2 = cx.sbuf("ks2", (1, 8), F32)
        br16 = cx.sbuf("br16", (16, 1), F32)

        @block.sync
        def _(sy):
            cx.inc(sy.dma_start(it16[:], inp["iota16"]), cx.ld, 16)
            sy.wait_ge(cx.ld, cx.val(cx.ld))
            cx.inc(sy.dma_start(ks2[:], inp["ksel2"]), cx.ld, 16)
            sy.wait_ge(cx.ld, cx.val(cx.ld))
            cx.inc(sy.dma_start(br16[:], inp["br"]), cx.ld, 16)
            sy.wait_ge(cx.ld, cx.val(cx.ld))

        @block.vector
        def _(v):
            v = cx.ser_v(v)
            cx.wait(v, cx.ld, cx.ev)
            v.tensor_scalar(racc[:], racc[:], 1.0 / 4096.0, None, op0=ALU.mult)
            v.tensor_tensor(racc[:], racc[:], br16[:], op=ALU.add)

        @block.tensor
        def _(t):
            cx.wait(t, cx.vv)
            bi, ps = cx.bank(t)
            ins = t.transpose(ps[0:1, 0:16], racc[:], idf[0:16, 0:16])
            cx.inc(ins, cx.mm)

            @block.scalar
            def _(s2):
                s2 = cx.ser_s(s2)
                cx.wait(s2, cx.mm)
                s2.activation(g16[0:1, 0:16], ps[0:1, 0:16], AF.Copy)
            cx.bank_freed(bi, cx.ev)

        def softmax16(src, dst):
            @block.vector
            def _(v):
                v = cx.ser_v(v)
                cx.wait(v, cx.ev)
                v.reduce_max(m11[:], src, axis=AX.X, negate=True)

            @block.scalar
            def _(s2):
                s2 = cx.ser_s(s2)
                cx.wait(s2, cx.vv)
                s2.activation(dst, src, AF.Exp, bias=m11[0:1, 0:1],
                              accum_out=m11[:])

            @block.vector
            def _(v):
                v = cx.ser_v(v)
                cx.wait(v, cx.ev)
                v.reciprocal(m11[:], m11[:])
                v.tensor_scalar(dst, dst, m11[0:1, 0:1], None, op0=ALU.mult)

        softmax16(g16[0:1, 0:16], g16[0:1, 16:32])

        @block.vector
        def _(v):
            v = cx.ser_v(v)
            cx.wait(v, cx.ev)
            v.tensor_copy(g16[0:1, 48:64], g16[0:1, 16:32])
            v.memset(g16[0:1, 32:48], 0.0)
            for k in range(4):
                v.reduce_max(m11[:], g16[0:1, 48:64], axis=AX.X)
                v.tensor_scalar(msks[0:1, k * 16:(k + 1) * 16],
                                g16[0:1, 48:64], m11[0:1, 0:1], None,
                                op0=ALU.is_equal)
                mk = msks[0:1, k * 16:(k + 1) * 16]
                v.tensor_tensor(sc4[0:1, k:k + 1],
                                mk, it16[:], op=ALU.mult) if False else None
                v.tensor_tensor(g16[0:1, 0:16], mk, it16[:], op=ALU.mult)
                v.reduce_sum(sc4[0:1, k:k + 1], g16[0:1, 0:16], axis=AX.X)
                v.scalar_tensor_tensor(g16[0:1, 32:48], mk, m11[0:1, 0:1],
                                       g16[0:1, 32:48], op0=ALU.mult,
                                       op1=ALU.add)
                v.scalar_tensor_tensor(g16[0:1, 48:64], mk, NEG,
                                       g16[0:1, 48:64], op0=ALU.mult,
                                       op1=ALU.add)

        softmax16(g16[0:1, 32:48], g16[0:1, 64:80])

        @block.vector
        def _(v):
            v = cx.ser_v(v)
            cx.wait(v, cx.ev)
            for k in range(4):
                v.tensor_tensor(g16[0:1, 0:16], msks[0:1, k * 16:(k + 1) * 16],
                                g16[0:1, 64:80], op=ALU.mult)
                v.reduce_sum(sc4[0:1, 4 + k:5 + k], g16[0:1, 0:16], axis=AX.X)
            for sl in range(2):
                v.tensor_tensor(g16[0:1, 0:4], ks2[0:1, sl * 4:(sl + 1) * 4],
                                sc4[0:1, 0:4], op=ALU.mult)
                v.reduce_sum(sc4[0:1, 8 + sl:9 + sl], g16[0:1, 0:4], axis=AX.X)
                v.tensor_tensor(g16[0:1, 0:4], ks2[0:1, sl * 4:(sl + 1) * 4],
                                sc4[0:1, 4:8], op=ALU.mult)
                v.reduce_sum(sc4[0:1, 10 + sl:11 + sl], g16[0:1, 0:4],
                             axis=AX.X)
                v.tensor_tensor(sc4[0:1, 12 + sl:13 + sl],
                                sc4[0:1, 10 + sl:11 + sl],
                                sc4[0:1, 10 + sl:11 + sl], op=ALU.mult)
                v.tensor_scalar(sc4[0:1, 12 + sl:13 + sl],
                                sc4[0:1, 12 + sl:13 + sl], 0.125, None,
                                op0=ALU.mult)

        oT2 = cx.sbuf("oT2", (128, S), BF16)
        wsel = cx.sbuf("wsel", (128, 3 * 8 * 64), BF16)
        gath = cx.sbuf("gath", (64, D), BF16)
        bsel = cx.sbuf("bsel", (64, 4), F32)
        brow = cx.sbuf("brow", (1, 64), F32)
        idxu = cx.sbuf("idxu", (64, 1), U32)
        idxf = cx.sbuf("idxf", (64, 1), F32)
        i64 = cx.sbuf("i64", (64, 1), F32)
        gcol = cx.sbuf("gcol", (128, 4), F32)
        qT = cx.sbuf("qT3", (64, S), BF16)
        kT = cx.sbuf("kT3", (64, S), BF16)
        vTM = cx.sbuf("vTM3", (128, 32 * 64), BF16)
        vTs = cx.sbuf("vTs3", (64, 2 * 512), BF16)
        strip = cx.sbuf("strip3", (128, S), F32)
        pstrip = cx.sbuf("pstrip3", (128, S), BF16)
        ptbig = cx.sbuf("ptbig3", (128, 16 * 128), BF16)
        rsum = cx.sbuf("rsum3", (128, 1), F32)
        nmax = cx.sbuf("nmax3", (128, 1), F32)

        @block.sync
        def _(sy):
            cx.inc(sy.dma_start(i64[:], inp["iota64"]), cx.ld, 16)
            sy.wait_ge(cx.ld, cx.val(cx.ld))

        emit_bcast_cols(cx, block, sc4[0:1, 12:14], gcol[:, 0:2], 2,
                        {"onesf": onesf})
        emit_bcast_cols(cx, block, sc4[0:1, 10:12], gcol[:, 2:4], 2,
                        {"onesf": onesf})

        for sl in range(2):
            @block.tensor
            def _(t):
                cx.wait(t, cx.vv, cx.ev)
                bi, ps = cx.bank(t)
                ins = t.matmul(ps[0:64, 0:1], onesf[0:1, 0:64],
                               sc4[0:1, 8 + sl:9 + sl], start=True, stop=True)
                cx.inc(ins, cx.mm)

                @block.vector
                def _(v):
                    v = cx.ser_v(v)
                    cx.wait(v, cx.mm, cx.ld)
                    v.tensor_scalar(idxf[:], ps[0:64, 0:1], 64.0, None,
                                    op0=ALU.mult)
                    v.tensor_tensor(idxf[:], idxf[:], i64[:], op=ALU.add)
                    v.tensor_copy(idxu[:], idxf[:])
                cx.bank_freed(bi, cx.vv)

            for j, wn in enumerate(("wqh", "wkh", "wvh")):
                @block.gpsimd
                def _(g):
                    g = cx.ser_g(g)
                    cx.wait(g, cx.vv)
                    cx.inc(g.indirect_dma_start(
                        out=gath[:, :], out_offset=None, in_=inp[wn],
                        in_offset=bass.IndirectOffsetOnAxis(
                            ap=idxu[:, 0:1], axis=0)), cx.lg, 16)
                    g.wait_ge(cx.lg, cx.val(cx.lg))

                @block.tensor
                def _(t):
                    cx.wait(t, cx.lg)
                    for ci in range(8):
                        bb, psb = cx.bankb(t)
                        ins = t.transpose(
                            psb[:, 0:64],
                            gath[:, ci * 128:(ci + 1) * 128],
                            idb[0:64, 0:64])
                        cx.inc(ins, cx.mm)

                        @block.scalar
                        def _(s2):
                            s2 = cx.ser_s(s2)
                            cx.wait(s2, cx.mm)
                            s2.activation(
                                wsel[:, j * 512 + ci * 64:
                                     j * 512 + (ci + 1) * 64],
                                psb[:, 0:64], AF.Copy)
                        cx.bankb_freed(bb, cx.ev)

                @block.gpsimd
                def _(g):
                    g = cx.ser_g(g)
                    cx.wait(g, cx.ev, cx.vv)
                    cx.inc(g.indirect_dma_start(
                        out=bsel[:, j:j + 1], out_offset=None,
                        in_=inp[("bqh", "bkh", "bvh")[j]],
                        in_offset=bass.IndirectOffsetOnAxis(
                            ap=idxu[:, 0:1], axis=0)), cx.lg, 16)
                    g.wait_ge(cx.lg, cx.val(cx.lg))

            @block.vector
            def _(v):
                v = cx.ser_v(v)
                cx.wait(v, cx.vv, cx.lg)
                v.tensor_scalar(bsel[:, 2:3], bsel[:, 2:3],
                                gcol[0:64, 2 + sl:3 + sl], None, op0=ALU.mult)

            for j, name in enumerate(("q", "k", "v")):
                for c in range(8):
                    c0 = c * 512
                    slot2 = c % 2

                    @block.tensor
                    def _(t):
                        cx.wait(t, cx.ev, cx.vv)
                        bi, ps = cx.bank(t)
                        for ci in range(8):
                            ins = t.matmul(
                                ps[0:64, :],
                                wsel[:, j * 512 + ci * 64:
                                     j * 512 + (ci + 1) * 64],
                                xn_t(ci, c0, 512),
                                start=(ci == 0), stop=(ci == 7))
                        cx.inc(ins, cx.mm)

                        @block.scalar
                        def _(s2):
                            s2 = cx.ser_s(s2)
                            cx.wait(s2, cx.mm, cx.vv)
                            if name == "v":
                                s2.activation(vTs[:, slot2 * 512:
                                                  (slot2 + 1) * 512],
                                              ps[0:64, :], AF.Identity,
                                              bias=bsel[:, 2:3],
                                              scale=gcol[0:64, 2 + sl:3 + sl])
                            else:
                                dst = qT if name == "q" else kT
                                s2.activation(dst[:, c0:c0 + 512], ps[0:64, :],
                                              AF.Identity,
                                              bias=bsel[:, j:j + 1])
                        cx.bank_freed(bi, cx.ev)

                        if name == "v":
                            @block.tensor
                            def _(t2):
                                cx.wait(t2, cx.ev)
                                for u in range(4):
                                    kt = c * 4 + u
                                    bb, psb = cx.bankb(t2)
                                    ins = t2.transpose(
                                        psb[:, 0:64],
                                        vTs[:, slot2 * 512 + u * 128:
                                            slot2 * 512 + (u + 1) * 128],
                                        idb[0:64, 0:64])
                                    cx.inc(ins, cx.mm)

                                    @block.vector
                                    def _(v):
                                        v = cx.ser_v(v)
                                        cx.wait(v, cx.mm)
                                        v.tensor_copy(
                                            vTM[:, kt * 64:(kt + 1) * 64],
                                            psb[:, 0:64])
                                    cx.bankb_freed(bb, cx.vv)

            for qi in range(32):
                @block.tensor
                def _(t):
                    cx.wait(t, cx.vv, cx.ev)
                    for c in range(8):
                        c0 = c * 512
                        bi, ps = cx.bank(t)
                        ins = t.matmul(ps[:], qT[:, qi * 128:(qi + 1) * 128],
                                       kT[:, c0:c0 + 512],
                                       start=True, stop=True)
                        cx.inc(ins, cx.mm)

                        @block.vector
                        def _(v):
                            v = cx.ser_v(v)
                            cx.wait(v, cx.mm, cx.ev)
                            v.tensor_copy(strip[:, c0:c0 + 512], ps[:])
                        cx.bank_freed(bi, cx.vv)

                @block.vector
                def _(v):
                    v = cx.ser_v(v)
                    cx.wait(v, cx.vv, cx.ev)
                    v.reduce_max(nmax[:], strip[:], axis=AX.X, negate=True)
                    v.tensor_scalar(nmax[:], nmax[:], gcol[:, sl:sl + 1],
                                    None, op0=ALU.mult)

                @block.scalar
                def _(s2):
                    s2 = cx.ser_s(s2)
                    cx.wait(s2, cx.vv, cx.mm)
                    s2.activation(pstrip[:], strip[:], AF.Exp,
                                  bias=nmax[:, 0:1], scale=gcol[:, sl:sl + 1],
                                  accum_out=rsum[:])

                @block.vector
                def _(v):
                    v = cx.ser_v(v)
                    cx.wait(v, cx.ev)
                    v.reciprocal(nmax[:], rsum[:])
                    v.tensor_scalar(pstrip[:], pstrip[:], nmax[:, 0:1], None,
                                    op0=ALU.mult)

                @block.tensor
                def _(t):
                    cx.wait(t, cx.vv)
                    bo, pso = cx.bank(t)
                    for G0 in range(0, 32, 16):
                        for u0 in range(0, 16, 8):
                            bb, psb = cx.bankb(t)
                            for u in range(8):
                                kt = G0 + u0 + u
                                ins = t.transpose(
                                    psb[:, u * 128:(u + 1) * 128],
                                    pstrip[:, kt * 128:(kt + 1) * 128], idb[:])
                            cx.inc(ins, cx.mm)

                            @block.vector
                            def _(v):
                                v = cx.ser_v(v)
                                cx.wait(v, cx.mm)
                                v.tensor_copy(
                                    ptbig[:, u0 * 128:(u0 + 8) * 128],
                                    psb[:, 0:8 * 128])
                            cx.bankb_freed(bb, cx.vv)
                        cx.wait(t, cx.vv)
                        for u in range(16):
                            kt = G0 + u
                            ins = t.matmul(pso[0:64, 0:128],
                                           vTM[:, kt * 64:(kt + 1) * 64],
                                           ptbig[:, u * 128:(u + 1) * 128],
                                           start=(kt == 0), stop=(kt == 31))
                        cx.inc(ins, cx.mm)

                    @block.scalar
                    def _(s2):
                        s2 = cx.ser_s(s2)
                        cx.wait(s2, cx.mm)
                        s2.activation(oT2[sl * 64:sl * 64 + 64,
                                          qi * 128:(qi + 1) * 128],
                                      pso[0:64, 0:128], AF.Copy)
                    cx.bank_freed(bo, cx.ev)

        @block.sync
        def _(sy):
            cx.wait(sy, cx.ev)
            cx.inc(sy.dma_start(o2_hb[:, :], oT2[:]), cx.st, 16)
            sy.wait_ge(cx.st, cx.val(cx.st))

        cx.free_sbuf("wsel", "gath", "bsel", "brow", "idxu", "idxf", "i64",
                     "gcol", "qT3", "kT3", "vTM3", "vTs3", "strip3",
                     "pstrip3", "ptbig3", "rsum3", "nmax3", "oT2")

        inw = cx.sbuf("inw", (128, 8 * 128), BF16)
        inwz = cx.sbuf("inwz", (128, 8 * 128), BF16)
        convd = cx.sbuf("convd", (128, 4 * 128), BF16)
        convb = cx.sbuf("convb", (128, 16), F32)
        xpw = cx.sbuf("xpw", (128, 96), BF16)
        xib = cx.sbuf("xib", (128, 4100), BF16)
        xcb2 = cx.sbuf("xcb2", (128, S), BF16)
        inw_used = {}

        @block.sync
        def _(sy):
            cx.inc(sy.dma_start(convb[:], inp["convb"]), cx.ld, 16)
            sy.wait_ge(cx.ld, cx.val(cx.ld))

        @block.vector
        def _(v):
            v = cx.ser_v(v)
            v.memset(xib[:, 0:4], 0.0)
            v.memset(dbc[:], 0.0)

        for i in range(16):
            @block.gpsimd
            def _(g):
                g = cx.ser_g(g)
                if i >= 1 and inw_used.get("all", 0) > 0:
                    g.wait_ge(cx.mm, inw_used["all"])
                wload(g, cx, inw[:], inp["in_w"][:, i * 128:(i + 1) * 128], 8)
                wload(g, cx, inwz[:],
                      inp["in_w"][:, 2048 + i * 128:2048 + (i + 1) * 128], 8)
                cx.inc(g.dma_start(
                    convd[:].rearrange("p (j o) -> p j o", j=4),
                    inp["convd"][:, i * 512:(i + 1) * 512]
                    .rearrange("p (j o) -> p j o", j=4)), cx.lg, 16)
                g.wait_ge(cx.lg, cx.val(cx.lg))
                cx.inc(g.dma_start(xpw[:],
                                   inp["xp_w"][i * 128:(i + 1) * 128, :]),
                       cx.lg, 16)
                g.wait_ge(cx.lg, cx.val(cx.lg))

            for c in range(8):
                c0 = c * 512

                @block.tensor
                def _(t):
                    cx.wait(t, cx.lg, cx.ld)
                    if c == 0:
                        cx.wait(t, cx.vv, cx.ev)
                    bi, ps = cx.bank(t)
                    for ci in range(8):
                        ins = t.matmul(ps[:],
                                       inw[:, ci * 128:(ci + 1) * 128],
                                       xn_t(ci, c0, 512),
                                       start=(ci == 0), stop=(ci == 7))
                    cx.inc(ins, cx.mm)
                    if c == 7:
                        inw_used[0] = cx.val(cx.mm)

                    @block.vector
                    def _(v):
                        v = cx.ser_v(v)
                        cx.wait(v, cx.mm)
                        v.tensor_copy(xib[:, 4 + c0:4 + c0 + 512], ps[:])
                    cx.bank_freed(bi, cx.vv)

            for c in range(8):
                c0 = c * 512

                @block.tensor
                def _(t):
                    cx.wait(t, cx.ev, cx.lg, cx.vv)
                    bi, ps = cx.bank(t)
                    for j in range(4):
                        ins = t.matmul(ps[:],
                                       convd[:, j * 128:(j + 1) * 128],
                                       xib[:, c0 + j + 1:c0 + j + 1 + 512],
                                       start=(j == 0), stop=(j == 3))
                    cx.inc(ins, cx.mm)
                    if c == 7:
                        inw_used["all"] = cx.val(cx.mm)

                    @block.scalar
                    def _(s2):
                        s2 = cx.ser_s(s2)
                        cx.wait(s2, cx.mm, cx.st)
                        s2.activation(xcb2[:, c0:c0 + 512], ps[:], AF.Silu,
                                      bias=convb[:, i:i + 1])
                    cx.bank_freed(bi, cx.ev)

                    bj, ps2 = cx.bank(t)
                    t.wait_ge(cx.ev, cx.val(cx.ev))
                    ins = t.matmul(ps2[0:96, :], xpw[:],
                                   xcb2[:, c0:c0 + 512], start=True, stop=True)
                    cx.inc(ins, cx.mm)

                    @block.vector
                    def _(v):
                        v = cx.ser_v(v)
                        cx.wait(v, cx.mm)
                        v.tensor_tensor(dbc[0:96, c0:c0 + 512],
                                        dbc[0:96, c0:c0 + 512],
                                        ps2[0:96, :], op=ALU.add)
                    cx.bank_freed(bj, cx.vv)

            @block.sync
            def _(sy):
                cx.wait(sy, cx.ev, cx.vv)
                cx.inc(sy.dma_start(xc_hb[i * 128:(i + 1) * 128, :], xcb2[:]),
                       cx.st, 16)
                sy.wait_ge(cx.st, cx.val(cx.st))

            for c in range(8):
                c0 = c * 512

                @block.tensor
                def _(t):
                    cx.wait(t, cx.lg, cx.ld, cx.st)
                    bi, ps = cx.bank(t)
                    for ci in range(8):
                        ins = t.matmul(
                            ps[:],
                            inwz[:, ci * 128:(ci + 1) * 128],
                            xn_t(ci, c0, 512),
                            start=(ci == 0), stop=(ci == 7))
                    cx.inc(ins, cx.mm)
                    if c == 7:
                        inw_used["all"] = cx.val(cx.mm)

                    @block.scalar
                    def _(s2):
                        s2 = cx.ser_s(s2)
                        cx.wait(s2, cx.mm, cx.st)
                        s2.activation(xcb2[:, c0:c0 + 512], ps[:], AF.Silu)
                    cx.bank_freed(bi, cx.ev)

            @block.sync
            def _(sy):
                cx.wait(sy, cx.ev)
                cx.inc(sy.dma_start(z_hb[i * 128:(i + 1) * 128, :], xcb2[:]),
                       cx.st, 16)
                sy.wait_ge(cx.st, cx.val(cx.st))

        cx.fence(block)
        cx.free_sbuf("inw", "inwz", "convd", "convb", "xpw", "xib", "xcb2",
                     "xn", "g16", "msks", "sc4", "m11", "it16", "ks2", "br16")

        @block.sync
        def _(sy):
            for i in range(16):
                cx.inc(sy.dma_start(asb[:, i * 16:(i + 1) * 16],
                                    inp["alog"][i * 128:(i + 1) * 128, :]),
                       cx.ld, 16)
                sy.wait_ge(cx.ld, cx.val(cx.ld))

        @block.scalar
        def _(s2):
            s2 = cx.ser_s(s2)
            cx.wait(s2, cx.ld, cx.vv)
            s2.activation(asb[:], asb[:], AF.Exp)

        @block.vector
        def _(v):
            v = cx.ser_v(v)
            cx.wait(v, cx.ev)
            v.tensor_scalar(asb[:], asb[:], -1.0, None, op0=ALU.mult)

        dtl = cx.sbuf("dtl", (64, S), BF16)
        dtw_sb = cx.sbuf("dtw_sb", (64, 2 * 128), BF16)
        dtb_sb = cx.sbuf("dtb_sb", (128, 16), F32)
        dtt = cx.sbuf("dtt", (128, S), F32)
        dta = cx.sbuf("dta", (128, S), F32)

        @block.sync
        def _(sy):
            cx.inc(sy.dma_start(dtb_sb[:], inp["dt_b"]), cx.ld, 16)
            sy.wait_ge(cx.ld, cx.val(cx.ld))

        @block.scalar
        def _(s2):
            s2 = cx.ser_s(s2)
            cx.wait(s2, cx.vv, cx.ev)
            s2.activation(dtl[:], dbc[0:64, :], AF.Copy)

        dtw_used = {}
        for i in range(16):
            slot = i % 2

            @block.gpsimd
            def _(g):
                g = cx.ser_g(g)
                if slot in dtw_used:
                    g.wait_ge(cx.mm, dtw_used[slot])
                cx.inc(g.dma_start(dtw_sb[:, slot * 128:(slot + 1) * 128],
                                   inp["dt_w"][:, i * 128:(i + 1) * 128]),
                       cx.lg, 16)
                g.wait_ge(cx.lg, cx.val(cx.lg))

            for c in range(8):
                c0 = c * 512

                @block.tensor
                def _(t):
                    cx.wait(t, cx.lg, cx.ld, cx.ev)
                    bi, ps = cx.bank(t)
                    ins = t.matmul(ps[:], dtw_sb[:, slot * 128:(slot + 1) * 128],
                                   dtl[:, c0:c0 + 512], start=True, stop=True)
                    cx.inc(ins, cx.mm)
                    if c == 7:
                        dtw_used[slot] = cx.val(cx.mm)

                    @block.scalar
                    def _(s2):
                        s2 = cx.ser_s(s2)
                        cx.wait(s2, cx.mm, cx.st, cx.vv)
                        s2.activation(dtt[:, c0:c0 + 512], ps[:], AF.Identity,
                                      bias=dtb_sb[:, i:i + 1])
                    cx.bank_freed(bi, cx.ev)

            @block.scalar
            def _(s2):
                s2 = cx.ser_s(s2)
                cx.wait(s2, cx.ev, cx.vv)
                s2.activation(dta[:], dtt[:], AF.Abs)
                s2.activation(dta[:], dta[:], AF.Exp, scale=-1.0)
                s2.activation(dta[:], dta[:], AF.Ln, bias=1.0)

            @block.vector
            def _(v):
                v = cx.ser_v(v)
                cx.wait(v, cx.ev)
                v.tensor_scalar(dtt[:], dtt[:], 0.0, None, op0=ALU.max)
                v.tensor_tensor(dtt[:], dtt[:], dta[:], op=ALU.add)

            @block.sync
            def _(sy):
                cx.wait(sy, cx.vv)
                cx.inc(sy.dma_start(dt_hb[i * 128:(i + 1) * 128, :], dtt[:]),
                       cx.st, 16)
                sy.wait_ge(cx.st, cx.val(cx.st))

        cx.fence(block)
        cx.free_sbuf("dtl", "dtw_sb", "dtb_sb", "dtt", "dta")

        CH = 1024
        NCH = S // CH
        yacc = cx.sbuf("yacc", (128, 16 * CH), F32)
        bcb = cx.sbuf("bcb", (128, 32 * CH), BF16)
        dt2 = cx.sbuf("dt2", (128, 2 * CH), F32)
        xc2 = cx.sbuf("xc2", (128, 2 * CH), BF16)
        zt2 = cx.sbuf("zt2", (128, 2 * CH), BF16)
        dtw2 = cx.sbuf("dtw2", (128, 2 * CH), F32)
        dA2 = cx.sbuf("dA2", (128, 2 * CH), F32)
        uu2 = cx.sbuf("uu2", (128, 2 * CH), BF16)
        hh2 = cx.sbuf("hh2", (128, 2 * CH), BF16)
        yt2 = cx.sbuf("yt2", (128, 2 * CH), BF16)
        yo2 = cx.sbuf("yo2", (128, 2 * CH), BF16)
        carry = cx.sbuf("carry", (128, 256), F32)
        grow2 = cx.sbuf("grow2", (1, 2 * CH), F32)
        dsk = cx.sbuf("dsk", (128, 16), F32)

        @block.sync
        def _(sy):
            cx.inc(sy.dma_start(dsk[:], inp["dskip"]), cx.ld, 16)
            sy.wait_ge(cx.ld, cx.val(cx.ld))

        cap = {"scan": [0, 0], "ymul": [0, 0], "yadd": [0, 0],
               "u15": [0, 0], "expl": [0, 0], "stt": [0, 0], "yo": [0, 0],
               "load": [0, 0], "grow": [0, 0], "st": [0, 0], "pool_last": 0}
        ntiles = 16 * NCH

        def emit_loads(g):
            ch_, i_ = divmod(g, 16)
            t0_ = ch_ * CH
            islot_ = g % 2

            @block.sync
            def _(sy):
                if cap["expl"][islot_]:
                    sy.wait_ge(cx.ev, cap["expl"][islot_])
                if cap["stt"][islot_]:
                    sy.wait_ge(cx.vv, cap["stt"][islot_])
                if cap["yo"][islot_]:
                    sy.wait_ge(cx.gg, cap["yo"][islot_])
                cx.inc(sy.dma_start(
                    dt2[:, islot_ * CH:(islot_ + 1) * CH],
                    dt_hb[i_ * 128:(i_ + 1) * 128, t0_:t0_ + CH]), cx.ld, 16)
                cx.inc(sy.dma_start(
                    xc2[:, islot_ * CH:(islot_ + 1) * CH],
                    xc_hb[i_ * 128:(i_ + 1) * 128, t0_:t0_ + CH]), cx.ld, 16)
                cx.inc(sy.dma_start(
                    zt2[:, islot_ * CH:(islot_ + 1) * CH],
                    z_hb[i_ * 128:(i_ + 1) * 128, t0_:t0_ + CH]), cx.ld, 16)
                sy.wait_ge(cx.ld, cx.val(cx.ld))
            cap["load"][islot_] = cx.val(cx.ld)

        for ch in range(NCH):
            t0 = ch * CH

            if cap["pool_last"]:
                @block.scalar
                def _(s2):
                    s2.wait_ge(cx.gg, cap["pool_last"])
            for r in range(32):
                row = (64 + r) if r < 16 else (80 + r - 16)
                gs = r % 2

                @block.sync
                def _(sy):
                    if cap["grow"][gs]:
                        sy.wait_ge(cx.mm, cap["grow"][gs])
                    cx.inc(sy.dma_start(grow2[:, gs * CH:(gs + 1) * CH],
                                        dbc[row:row + 1, t0:t0 + CH]),
                           cx.ld, 16)
                    sy.wait_ge(cx.ld, cx.val(cx.ld))
                ldc = cx.val(cx.ld)

                @block.tensor
                def _(t):
                    t.wait_ge(cx.ld, ldc)
                    for hf in range(2):
                        bi, ps = cx.bank(t)
                        ins = t.matmul(
                            ps[:], onesf[0:1, :],
                            grow2[:, gs * CH + hf * 512:gs * CH + hf * 512
                                  + 512], start=True, stop=True)
                        cx.inc(ins, cx.mm)
                        if hf == 1:
                            cap["grow"][gs] = cx.val(cx.mm)

                        @block.scalar
                        def _(s2):
                            s2 = cx.ser_s(s2)
                            cx.wait(s2, cx.mm)
                            s2.activation(
                                bcb[:, r * CH + hf * 512:r * CH + hf * 512
                                    + 512], ps[:], AF.Copy)
                        cx.bank_freed(bi, cx.ev)
            bcast_cap = cx.val(cx.ev)

            for i in range(16):
                g = ch * 16 + i
                islot = g % 2
                if g == 0:
                    emit_loads(0)
                if g + 1 < ntiles:
                    emit_loads(g + 1)

                @block.vector
                def _(v):
                    v = cx.ser_v(v)
                    v.wait_ge(cx.ld, cap["load"][islot])
                    if cap["u15"][islot]:
                        v.wait_ge(cx.gg, cap["u15"][islot])
                    v.tensor_tensor(dtw2[:, islot * CH:(islot + 1) * CH],
                                    dt2[:, islot * CH:(islot + 1) * CH],
                                    xc2[:, islot * CH:(islot + 1) * CH],
                                    op=ALU.mult)
                dtw_cap = cx.val(cx.vv)

                pend = None
                for n in range(16):
                    sl = n % 2
                    cc = i * 16 + n

                    @block.scalar
                    def _(s2):
                        s2 = cx.ser_s(s2)
                        if n == 0:
                            s2.wait_ge(cx.ld, cap["load"][islot])
                        if cap["scan"][sl]:
                            s2.wait_ge(cx.vv, cap["scan"][sl])
                        s2.activation(dA2[:, sl * CH:(sl + 1) * CH],
                                      dt2[:, islot * CH:(islot + 1) * CH],
                                      AF.Exp, scale=asb[:, cc:cc + 1])
                    cap["expl"][islot] = cx.val(cx.ev)
                    exp_cap = cx.val(cx.ev)

                    @block.gpsimd
                    def _(gp):
                        gp = cx.ser_g(gp)
                        if n == 0:
                            gp.wait_ge(cx.vv, dtw_cap)
                            if i == 0:
                                gp.wait_ge(cx.ev, bcast_cap)
                        if cap["scan"][sl]:
                            gp.wait_ge(cx.vv, cap["scan"][sl])
                        gp.tensor_tensor(uu2[:, sl * CH:(sl + 1) * CH],
                                         dtw2[:, islot * CH:(islot + 1) * CH],
                                         bcb[:, n * CH:(n + 1) * CH],
                                         op=ALU.mult)
                    u_cap = cx.val(cx.gg)
                    if n == 15:
                        cap["u15"][islot] = u_cap

                    if pend is not None:
                        pn, psl, pscan = pend

                        @block.gpsimd
                        def _(gp):
                            gp = cx.ser_g(gp)
                            gp.wait_ge(cx.vv, pscan)
                            if cap["yadd"][psl]:
                                gp.wait_ge(cx.vv, cap["yadd"][psl])
                            gp.tensor_tensor(
                                yt2[:, psl * CH:(psl + 1) * CH],
                                hh2[:, psl * CH:(psl + 1) * CH],
                                bcb[:, (16 + pn) * CH:(17 + pn) * CH],
                                op=ALU.mult)
                        cap["ymul"][psl] = cx.val(cx.gg)

                    @block.vector
                    def _(v):
                        v = cx.ser_v(v)
                        v.wait_ge(cx.ev, exp_cap)
                        v.wait_ge(cx.gg, u_cap)
                        if cap["ymul"][sl]:
                            v.wait_ge(cx.gg, cap["ymul"][sl])
                        if ch == 0:
                            v.tensor_tensor_scan(
                                hh2[:, sl * CH:(sl + 1) * CH],
                                dA2[:, sl * CH:(sl + 1) * CH],
                                uu2[:, sl * CH:(sl + 1) * CH],
                                0.0, op0=ALU.mult, op1=ALU.add)
                        else:
                            v.tensor_tensor_scan(
                                hh2[:, sl * CH:(sl + 1) * CH],
                                dA2[:, sl * CH:(sl + 1) * CH],
                                uu2[:, sl * CH:(sl + 1) * CH],
                                carry[:, cc:cc + 1], op0=ALU.mult,
                                op1=ALU.add)
                        v.tensor_copy(carry[:, cc:cc + 1],
                                      hh2[:, sl * CH + CH - 1:sl * CH + CH])
                    cap["scan"][sl] = cx.val(cx.vv)

                    if pend is not None:
                        pn, psl, _ = pend
                        pymul = cap["ymul"][psl]

                        @block.vector
                        def _(v):
                            v = cx.ser_v(v)
                            v.wait_ge(cx.gg, pymul)
                            dst = yacc[:, i * CH:(i + 1) * CH]
                            if pn == 0:
                                v.tensor_copy(dst,
                                              yt2[:, psl * CH:(psl + 1) * CH])
                            else:
                                v.tensor_tensor(
                                    dst, dst, yt2[:, psl * CH:(psl + 1) * CH],
                                    op=ALU.add)
                        cap["yadd"][psl] = cx.val(cx.vv)
                    pend = (n, sl, cap["scan"][sl])

                pn, psl, pscan = pend

                @block.gpsimd
                def _(gp):
                    gp = cx.ser_g(gp)
                    gp.wait_ge(cx.vv, pscan)
                    if cap["yadd"][psl]:
                        gp.wait_ge(cx.vv, cap["yadd"][psl])
                    gp.tensor_tensor(yt2[:, psl * CH:(psl + 1) * CH],
                                     hh2[:, psl * CH:(psl + 1) * CH],
                                     bcb[:, (16 + pn) * CH:(17 + pn) * CH],
                                     op=ALU.mult)
                cap["ymul"][psl] = cx.val(cx.gg)
                pymul = cap["ymul"][psl]

                @block.vector
                def _(v):
                    v = cx.ser_v(v)
                    v.wait_ge(cx.gg, pymul)
                    v.tensor_tensor(yacc[:, i * CH:(i + 1) * CH],
                                    yacc[:, i * CH:(i + 1) * CH],
                                    yt2[:, psl * CH:(psl + 1) * CH],
                                    op=ALU.add)
                cap["yadd"][psl] = cx.val(cx.vv)

                @block.vector
                def _(v):
                    v = cx.ser_v(v)
                    dst = yacc[:, i * CH:(i + 1) * CH]
                    v.scalar_tensor_tensor(dst,
                                           xc2[:, islot * CH:(islot + 1) * CH],
                                           dsk[:, i:i + 1], dst,
                                           op0=ALU.mult, op1=ALU.add)
                cap["stt"][islot] = cx.val(cx.vv)
                stt_cap = cap["stt"][islot]

                @block.gpsimd
                def _(gp):
                    gp = cx.ser_g(gp)
                    gp.wait_ge(cx.vv, stt_cap)
                    if cap["st"][islot]:
                        gp.wait_ge(cx.st, cap["st"][islot])
                    gp.tensor_tensor(yo2[:, islot * CH:(islot + 1) * CH],
                                     yacc[:, i * CH:(i + 1) * CH],
                                     zt2[:, islot * CH:(islot + 1) * CH],
                                     op=ALU.mult)
                cap["yo"][islot] = cx.val(cx.gg)
                yo_cap = cap["yo"][islot]

                @block.sync
                def _(sy):
                    sy.wait_ge(cx.gg, yo_cap)
                    cx.inc(sy.dma_start(y_hb[i * 128:(i + 1) * 128,
                                             t0:t0 + CH],
                                        yo2[:, islot * CH:(islot + 1) * CH]),
                           cx.st, 16)
                    sy.wait_ge(cx.st, cx.val(cx.st))
                cap["st"][islot] = cx.val(cx.st)

            cap["pool_last"] = cx.val(cx.gg)

        cx.fence(block)
        cx.free_sbuf("yacc", "bcb", "dt2", "xc2", "zt2", "dtw2", "dA2",
                     "uu2", "hh2", "yt2", "yo2", "carry", "grow2", "dsk",
                     "dbc", "asb")

        wcomb = cx.sbuf("wcomb", (128, 16 * D), BF16)
        projb = cx.sbuf("projb", (128, 8 * D), BF16)
        owt = cx.sbuf("owt", (128, 2 * 8 * 128), BF16)
        ow_used = {}

        @block.gpsimd
        def _(g):
            g = cx.ser_g(g)
            wload(g, cx, projb[:], inp["proj"], 8)

        for i in range(16):
            slot = i % 2

            @block.gpsimd
            def _(g):
                g = cx.ser_g(g)
                if slot in ow_used:
                    g.wait_ge(cx.mm, ow_used[slot])
                wload(g, cx, owt[:, slot * 1024:(slot + 1) * 1024],
                      inp["out_wT"][:, i * 128:(i + 1) * 128], 8)

            for c2 in range(2):
                @block.tensor
                def _(t):
                    cx.wait(t, cx.lg, cx.ev)
                    bi, ps = cx.bank(t)
                    for m in range(8):
                        ins = t.matmul(
                            ps[:],
                            owt[:, slot * 1024 + m * 128:
                                slot * 1024 + (m + 1) * 128],
                            projb[:, m * D + c2 * 512:m * D + c2 * 512 + 512],
                            start=(m == 0), stop=(m == 7))
                    cx.inc(ins, cx.mm)
                    if c2 == 1:
                        ow_used[slot] = cx.val(cx.mm)

                    @block.scalar
                    def _(s2):
                        s2 = cx.ser_s(s2)
                        cx.wait(s2, cx.mm)
                        s2.activation(wcomb[:, i * D + c2 * 512:
                                            i * D + c2 * 512 + 512],
                                      ps[:], AF.Copy)
                    cx.bank_freed(bi, cx.ev)

        ybuf = cx.sbuf("ybuf", (128, 16 * 512), BF16)
        obuf = cx.sbuf("obuf", (128, 512), BF16)
        wo_sb = cx.sbuf("wo3", (128, D), BF16)
        b2sb = cx.sbuf("b2sb", (2, D), BF16)
        ones2 = cx.sbuf("ones2", (2, 512), BF16)
        stg = cx.sbuf("stg3", (128, 2 * 512), F32)

        @block.gpsimd
        def _(g):
            g = cx.ser_g(g)
            cx.inc(g.dma_start(wo_sb[:], inp["wo"]), cx.lg, 16)
            g.wait_ge(cx.lg, cx.val(cx.lg))
            cx.inc(g.dma_start(b2sb[:], inp["bias2"]), cx.lg, 16)
            g.wait_ge(cx.lg, cx.val(cx.lg))

        @block.vector
        def _(v):
            v = cx.ser_v(v)
            v.memset(ones2[:], 1.0)

        for c in range(8):
            c0 = c * 512

            @block.sync
            def _(sy):
                cx.wait(sy, cx.mm)
                cx.inc(sy.dma_start(
                    ybuf[:].rearrange("p (i t) -> p i t", i=16),
                    y_hb[:, c0:c0 + 512].rearrange("(i p) t -> p i t", p=128)),
                    cx.ld, 16)
                sy.wait_ge(cx.ld, cx.val(cx.ld))
                cx.inc(sy.dma_start(obuf[:], o2_hb[:, c0:c0 + 512]),
                       cx.ld, 16)
                sy.wait_ge(cx.ld, cx.val(cx.ld))

            for po in range(8):
                @block.tensor
                def _(t):
                    cx.wait(t, cx.ld, cx.lg, cx.vv, cx.ev)
                    bi, ps = cx.bank(t)
                    t.matmul(ps[:], b2sb[:, po * 128:(po + 1) * 128], ones2[:],
                             start=True, stop=False)
                    t.matmul(ps[:], wo_sb[:, po * 128:(po + 1) * 128],
                             obuf[:], start=False, stop=False)
                    for i in range(16):
                        ins = t.matmul(
                            ps[:],
                            wcomb[:, i * D + po * 128:i * D + (po + 1) * 128],
                            ybuf[:, i * 512:(i + 1) * 512],
                            start=False, stop=(i == 15))
                    cx.inc(ins, cx.mm)
                    sslot = (c * 8 + po) % 2

                    @block.scalar
                    def _(s2):
                        s2 = cx.ser_s(s2)
                        cx.wait(s2, cx.mm, cx.st)
                        s2.activation(stg[:, sslot * 512:(sslot + 1) * 512],
                                      ps[:], AF.Copy)
                    cx.bank_freed(bi, cx.ev)

                    @block.sync
                    def _(sy):
                        cx.wait(sy, cx.ev)
                        cx.inc(sy.dma_start(
                            part_out[po * 128:(po + 1) * 128, c0:c0 + 512],
                            stg[:, sslot * 512:(sslot + 1) * 512]), cx.st, 16)
                        sy.wait_ge(cx.st, cx.val(cx.st))

        @block.sync
        def _(sy):
            cx.wait(sy, cx.st)

    return nc, cx, inp




def build_LE():
    nc = bass.Bass("TRN2", target_bir_lowering=False)
    inp = {}
    for name in ("p1", "p2", "p3", "p4", "p5"):
        inp[name] = nc.dram_tensor(name, [D, TOWN], F32,
                                   kind="ExternalInput").ap()
    xout = nc.dram_tensor("xout", [D, TOWN], F32, kind="ExternalOutput").ap()
    cx = Ctx(nc)
    with nc.Block() as block:
        cx.block = block
        a = cx.sbuf("a", (128, TOWN), F32)
        b = cx.sbuf("b", (128, TOWN), F32)
        for po in range(8):
            @block.sync
            def _(sy):
                cx.wait(sy, cx.vv, cx.st)
                cx.inc(sy.dma_start(a[:], inp["p1"][po * 128:(po + 1) * 128, :]),
                       cx.ld, 16)
                sy.wait_ge(cx.ld, cx.val(cx.ld))

            for name in ("p2", "p3", "p4", "p5"):
                @block.sync
                def _(sy):
                    cx.wait(sy, cx.vv)
                    cx.inc(sy.dma_start(b[:],
                                        inp[name][po * 128:(po + 1) * 128, :]),
                           cx.ld, 16)
                    sy.wait_ge(cx.ld, cx.val(cx.ld))

                @block.vector
                def _(v):
                    v = cx.ser_v(v)
                    cx.wait(v, cx.ld)
                    v.tensor_tensor(a[:], a[:], b[:], op=ALU.add)

            @block.sync
            def _(sy):
                cx.wait(sy, cx.vv)
                cx.inc(sy.dma_start(xout[po * 128:(po + 1) * 128, :], a[:]),
                       cx.st, 16)
                sy.wait_ge(cx.st, cx.val(cx.st))

        @block.sync
        def _(sy):
            cx.wait(sy, cx.st)
    return nc, cx, inp


def host_prep_L3(I, pA, pB, pC):
    DI = 2048
    conv_w = np.asarray(I["L3_attn_conv_w"], np.float32)
    maps = []
    iota16 = np.arange(16, dtype=np.float32)[None, :]
    iota64 = np.arange(64, dtype=np.float32)[:, None]
    idf = np.eye(128, dtype=np.float32)
    onesf = np.ones((128, 128), np.float32)
    idb = np.eye(128, dtype=np.float32).astype(bf16)
    onesln = np.full((128, 128), 1.0 / 1024.0, np.float32).astype(bf16)
    projw = np.asarray(I["L3_attn_proj_w"], np.float32)
    projb = np.asarray(I["L3_attn_proj_b"], np.float32)
    bo3 = np.asarray(I["L3_ffn_bo"], np.float32)
    for c in range(8):
        b, p = divmod(c, 2)

        def R(x):
            return f32c(x[:, ::-1]) if p == 1 else f32c(x)
        convd = np.zeros((128, 16 * 4 * 128), np.float32)
        for i in range(16):
            for j in range(4):
                blk = convd[:, (i * 4 + j) * 128:(i * 4 + j + 1) * 128]
                np.fill_diagonal(blk, conv_w[p, i * 128:(i + 1) * 128, j])
        bias2 = np.zeros((2, D), np.float32)
        if p == 0:
            bias2[0] = projb
            bias2[1] = bo3
        ksel2 = np.zeros((1, 8), np.float32)
        ksel2[0, 2 * p] = 1.0
        ksel2[0, 4 + 2 * p + 1] = 1.0
        m = {
            "pA": R(pA[b]), "pB": R(pB[b]), "pC": R(pC[b]),
            "in_w": f32c(I["L3_attn_in_w"][p]),
            "convd": convd,
            "convb": f32c(np.asarray(I["L3_attn_conv_b"][p])
                          .reshape(16, 128).T),
            "xp_w": f32c(I["L3_attn_xp_w"][p]),
            "dt_w": f32c(I["L3_attn_dt_w"][p]),
            "dt_b": f32c(np.asarray(I["L3_attn_dt_b"][p]).reshape(16, 128).T),
            "alog": f32c(I["L3_attn_A_log"][p]),
            "dskip": f32c(np.asarray(I["L3_attn_D"][p]).reshape(16, 128).T),
            "out_wT": f32c(np.asarray(I["L3_attn_out_w"][p]).T),
            "proj": f32c(projw[p * D:(p + 1) * D, :]),
            "wr": f32c(I["L3_ffn_Wr"]), "br": f32c(I["L3_ffn_br"][:, None]),
            "wqh": f32c(np.asarray(I["L3_ffn_Wq"]).T),
            "wkh": f32c(np.asarray(I["L3_ffn_Wk"]).T),
            "wvh": f32c(np.asarray(I["L3_ffn_Wv"]).T),
            "bqh": f32c(I["L3_ffn_bq"][:, None]),
            "bkh": f32c(I["L3_ffn_bk"][:, None]),
            "bvh": f32c(I["L3_ffn_bv"][:, None]),
            "wo": f32c(I["L3_ffn_Wo"][p * 128:(p + 1) * 128, :]),
            "bias2": bias2,
            "iota16": iota16, "iota64": iota64, "ksel2": ksel2,
            "idf": idf, "onesf": onesf, "idb": idb, "onesln": onesln,
        }
        maps.append(m)
    return maps


def run_L3(I, resC1, resC3):
    pA = [resC1[2 * b]["attn_out"] for b in range(4)]
    pB = [resC1[2 * b + 1]["attn_out"] for b in range(4)]
    pC = [np.concatenate([resC3[2 * b]["xout"],
                          resC3[2 * b + 1]["xout"]], axis=1)
          for b in range(4)]
    maps = host_prep_L3(I, pA, pB, pC)
    nc, cx, _ = build_L3()
    res = run_bass_kernel_spmd(nc, maps, core_ids=list(range(8))).results
    cx.close()
    parts = []
    for c in range(8):
        po = res[c]["part_out"]
        parts.append(f32c(po[:, ::-1]) if c % 2 == 1 else po)
    return parts, pA, pB, pC


def kernel(**inputs):
    I = {k: np.asarray(v) for k, v in inputs.items()}
    resA = run_L0(I)
    resB = run_L1(I, resA)
    resC1, resC3 = run_L2(I, resB)
    partsD, pA2, pB2, pC2 = run_L3(I, resC1, resC3)

    nc, cx, _ = build_LE()
    maps = []
    for c in range(8):
        b, p = divmod(c, 2)
        own = slice(p * TOWN, (p + 1) * TOWN)
        maps.append({
            "p1": f32c(pA2[b][:, own]), "p2": f32c(pB2[b][:, own]),
            "p3": f32c(pC2[b][:, own]), "p4": f32c(partsD[2 * b][:, own]),
            "p5": f32c(partsD[2 * b + 1][:, own]),
        })
    resE = run_bass_kernel_spmd(nc, maps, core_ids=list(range(8))).results
    cx.close()

    out = np.empty((B, S, D), np.float32)
    for c in range(8):
        b, p = divmod(c, 2)
        out[b, p * TOWN:(p + 1) * TOWN, :] = resE[c]["xout"].T
    return out



def rope_tables():
    half = 32
    inv = (1.0 / 10000.0 ** (np.arange(half, dtype=np.float32) / half))
    pos = np.arange(S, dtype=np.float32)
    ang = np.outer(pos, inv).astype(np.float32)
    return f32c(np.cos(ang).T), f32c(np.sin(ang).T)


def causal_masks():
    m = np.zeros((128, 4 * 512), np.float32)
    for r in range(4):
        blk = m[:, r * 512:(r + 1) * 512]
        for row in range(128):
            lo = r * 128 + row + 1
            if lo < 512:
                blk[row, lo:] = NEG
    return m


def host_prep_L0(I):
    x = np.asarray(I["L0_attn_x"], np.float32)
    qkv_w = np.asarray(I["L0_attn_qkv_w"], np.float32)
    qkv_b = np.asarray(I["L0_attn_qkv_b"], np.float32)
    out_w = np.asarray(I["L0_attn_out_w"], np.float32)
    out_b = np.asarray(I["L0_attn_out_b"], np.float32)
    rw = np.asarray(I["L0_ffn_router_w"], np.float32)
    rb = np.asarray(I["L0_ffn_router_b"], np.float32)
    w1 = np.asarray(I["L0_ffn_w1"], np.float32)
    b1 = np.asarray(I["L0_ffn_b1"], np.float32)
    w2 = np.asarray(I["L0_ffn_w2"], np.float32)
    b2 = np.asarray(I["L0_ffn_b2"], np.float32)
    sw1 = np.asarray(I["L0_ffn_sw1"], np.float32)
    sb1 = np.asarray(I["L0_ffn_sb1"], np.float32)
    sw2 = np.asarray(I["L0_ffn_sw2"], np.float32)
    sb2 = np.asarray(I["L0_ffn_sb2"], np.float32)

    cos, sin = rope_tables()
    cmask = causal_masks()
    idf = np.eye(128, dtype=np.float32)
    idb = np.eye(128, dtype=np.float32).astype(bf16)
    onesln = np.full((128, 128), 1.0 / 1024.0, np.float32).astype(bf16)

    w1big = np.concatenate(list(w1) + [sw1[0], sw1[1]], axis=1)
    b1big = np.concatenate([b1.reshape(-1), sb1[0], sb1[1]])[:, None]
    w2big = np.concatenate([w2.reshape(64 * 256, D), sw2[0], sw2[1]], axis=0)
    biasmat = np.zeros((128, D), np.float32)
    biasmat[0:64] = b2
    biasmat[64] = sb2[0]
    biasmat[65] = sb2[1]
    biasmat[66] = out_b

    qT = f32c(qkv_w[0:D, :].T)
    kTw = f32c(qkv_w[D:2 * D, :].T)
    vTw = f32c(qkv_w[2 * D:3 * D, :].T)
    woT = f32c(out_w.T)

    maps = []
    for c in range(8):
        b, p = divmod(c, 2)
        hs = p * 512
        maps.append({
            "xful": f32c(x[b].T),
            "xown": f32c(x[b, p * TOWN:(p + 1) * TOWN, :].T),
            "wq": f32c(qT[:, hs:hs + 512]),
            "wk": f32c(kTw[:, hs:hs + 512]),
            "wv": f32c(vTw[:, hs:hs + 512]),
            "bq": f32c(qkv_b[0:D][hs:hs + 512][:, None]),
            "bk": f32c(qkv_b[D:2 * D][hs:hs + 512][:, None]),
            "bv": f32c(qkv_b[2 * D:3 * D][hs:hs + 512][:, None]),
            "wo": f32c(woT[hs:hs + 512, :]),
            "cos": cos, "sin": sin, "cmask": cmask,
            "rw": f32c(rw), "rb": f32c(rb[:, None]),
            "w1big": w1big, "b1big": b1big, "w2big": w2big,
            "biasmat": biasmat,
            "idf": idf, "idb": idb, "onesln": onesln,
        })
    return maps


def run_L0(I):
    nc, cx, _ = build_L0()
    maps = host_prep_L0(I)
    res = run_bass_kernel_spmd(nc, maps, core_ids=list(range(8)))
    cx.close()
    return res.results



